# version 05
# speedup vs baseline: 1.3969x; 1.0570x over previous
import numpy as np
import ml_dtypes
import concourse.bass as bass
import concourse.mybir as mybir
from concourse.bass_utils import run_bass_kernel_spmd
from contextlib import ExitStack

F32 = mybir.dt.float32
BF16 = mybir.dt.bfloat16
I32 = mybir.dt.int32
AF = mybir.ActivationFunctionType
OP = mybir.AluOpType
AX = mybir.AxisListType

NCORE = 8
B = 2
S = 4096
DMODEL = 2048
NTOK = B * S
TOK_PER_CORE = NTOK // NCORE

INV_SQRT_D0 = 1.0 / float(np.sqrt(128.0))


ENGS = ("pe", "dve", "act", "pool", "sp")


class Rec:
    __slots__ = ("eng", "fn", "reads", "writes", "queue", "deps", "inc", "idx",
                 "dma_ord", "tag", "waits", "extra", "lane")

    def __init__(self, eng, fn, reads, writes, queue, tag):
        self.eng = eng
        self.fn = fn
        self.reads = reads
        self.writes = writes
        self.queue = queue
        self.deps = None
        self.inc = False
        self.idx = -1
        self.dma_ord = -1
        self.tag = tag


class Prog:

    def __init__(self, name="prog"):
        self.nc = bass.Bass("TRN2", target_bir_lowering=False,
                            detect_race_conditions=False)
        self.name = name
        self.recs = []
        self.stack = ExitStack()
        self._scopes = []
        self._fence = ()
        self._sbuf_bytes = 0
        self._uid = 0

    def push_scope(self):
        self._scopes.append(ExitStack())

    def pop_scope(self):
        self._scopes.pop().close()
        self.fence()

    def dram(self, name, shape, dtype, kind="Internal"):
        return self.nc.dram_tensor(name, list(shape), dtype, kind=kind)

    def _uname(self, name):
        if not hasattr(self, "_names"):
            self._names = set()
        if name in self._names:
            self._uid += 1
            name = f"{name}_u{self._uid}"
        self._names.add(name)
        return name

    def sbuf(self, name, shape, dtype, scoped=True):
        st = self._scopes[-1] if (scoped and self._scopes) else self.stack
        h = st.enter_context(self.nc.sbuf_tensor(self._uname(name), list(shape), dtype))
        return h

    def sbuf_p(self, name, shape, dtype):
        return self.stack.enter_context(
            self.nc.sbuf_tensor(self._uname(name), list(shape), dtype))

    def psum(self, name, shape, dtype=F32):
        st = self._scopes[-1] if self._scopes else self.stack
        return st.enter_context(self.nc.psum_tensor(self._uname(name), list(shape), dtype))

    def sem(self, name):
        return self.stack.enter_context(self.nc.semaphore(name))

    def const_col(self, value, dtype=F32):
        key = ("cc", float(value), dtype)
        if not hasattr(self, "_ccols"):
            self._ccols = {}
        if key not in self._ccols:
            self._uid += 1
            name = f"ccol{self._uid}"
            h = self.sbuf_p(name, [128, 1], dtype)
            self.op("pool", lambda g, h=h, value=value: g.memset(h[:, :], value),
                    reads=(), writes=(name,), tag="ccol")
            self._ccols[key] = (h, name)
        return self._ccols[key]

    def op(self, eng, fn, reads=(), writes=(), tag=""):
        r = Rec(eng, fn, tuple(reads), tuple(writes), None, tag)
        r.extra = self._fence
        self.recs.append(r)
        return r

    def dma(self, eng, fn, reads=(), writes=(), tag=""):
        r = Rec(eng, fn, tuple(reads), tuple(writes), eng, tag)
        r.extra = self._fence
        if not hasattr(self, "_qrr"):
            self._qrr = {}
        self._qrr[eng] = self._qrr.get(eng, 0) + 1
        r.lane = (self._qrr[eng] - 1) % 8
        self.recs.append(r)
        return r

    def fence(self):
        lastper = {}
        for r in self.recs:
            key = (r.eng, r.queue, getattr(r, "lane", None))
            lastper[key] = r
        self._fence = tuple(lastper.values())

    def finalize(self):
        last_w = {}
        readers = {}
        def needs_edge(prod, cons):
            if prod.queue is not None:
                return True
            if prod.eng != cons.eng:
                return True
            return prod.eng != "pe" or cons.queue is not None

        for i, r in enumerate(self.recs):
            r.idx = i
            deps = set()
            for k in r.reads:
                lw = last_w.get(k)
                if lw is not None and needs_edge(lw, r):
                    deps.add(lw)
            for k in r.writes:
                lw = last_w.get(k)
                if lw is not None and needs_edge(lw, r):
                    deps.add(lw)
                for rd in readers.get(k, ()):
                    if rd is not r and needs_edge(rd, r):
                        deps.add(rd)
            for fr in r.extra:
                if fr is not r and fr.idx < i and needs_edge(fr, r):
                    deps.add(fr)
            for k in r.reads:
                readers.setdefault(k, []).append(r)
            for k in r.writes:
                last_w[k] = r
                readers[k] = []
            r.deps = deps
            for d in deps:
                if d.queue is None:
                    d.inc = True
        NLANES = 8
        eng_inc_count = {e: 0 for e in ENGS}
        lane_count = {}
        lane_last = {}
        inc_val = {}
        dma_key = {}
        for r in self.recs:
            if r.queue is not None:
                lk = (r.queue, r.lane)
                prev = lane_last.get(lk)
                if prev is not None:
                    r.deps.add(prev)
                lane_last[lk] = r
                lane_count[lk] = lane_count.get(lk, 0) + 1
                dma_key[r.idx] = (lk, lane_count[lk] * 16)
            elif r.inc:
                eng_inc_count[r.eng] += 1
                inc_val[r.idx] = eng_inc_count[r.eng]

        nc = self.nc
        esems = {e: self.stack.enter_context(nc.semaphore(f"se_{e}")) for e in ENGS}
        dsems = {lk: self.stack.enter_context(nc.semaphore(f"sd_{lk[0]}_{lk[1]}"))
                 for lk in lane_count}

        watermark = {e: {} for e in ENGS}
        for r in self.recs:
            needs = {}
            for d in r.deps:
                if d.queue is not None:
                    semk, v = dma_key[d.idx]
                else:
                    semk = d.eng
                    v = inc_val[d.idx]
                if needs.get(semk, 0) < v:
                    needs[semk] = v
            wm = watermark[r.eng]
            waits = []
            for semk, v in needs.items():
                if wm.get(semk, 0) < v:
                    wm[semk] = v
                    waits.append((semk, v))
            r.waits = waits

        by_eng = {e: [r for r in self.recs if r.eng == e] for e in ENGS}

        def replay(eng, proxy):
            for r in by_eng[eng]:
                for semk, v in r.waits:
                    proxy.wait_ge(dsems[semk] if isinstance(semk, tuple)
                                  else esems[semk], v)
                inst = r.fn(proxy)
                if r.queue is not None:
                    inst.then_inc(dsems[dma_key[r.idx][0]], 16)
                elif r.inc:
                    inst.then_inc(esems[r.eng], 1)

        with nc.Block() as block:
            @block.tensor
            def _(t):
                replay("pe", t)

            @block.vector
            def _(v):
                replay("dve", v)

            @block.scalar
            def _(s):
                replay("act", s)

            @block.gpsimd
            def _(g):
                replay("pool", g)

            @block.sync
            def _(sy):
                replay("sp", sy)
                for lk, cnt in lane_count.items():
                    sy.wait_ge(dsems[lk], 16 * cnt)

        self.stack.close()
        return self.nc



def emit_consts(p: Prog):
    c = {}
    c["ident"] = p.sbuf_p("c_ident", [128, 128], BF16)
    c["ones_col"] = p.sbuf_p("c_ones_col", [128, 1], BF16)
    c["ones_row"] = p.sbuf_p("c_ones_row", [1, 128], BF16)
    c["ones_row_f"] = p.sbuf_p("c_ones_row_f", [1, 128], F32)
    c["masks"] = p.sbuf_p("c_masks", [128, 4 * 512], BF16)

    p.op("pool", lambda g: g.memset(c["ident"][:, :], 1.0),
         writes=("c_identraw",), tag="consts")
    p.op("pool", lambda g: g.memset(c["ones_col"][:, :], 1.0),
         writes=("c_ones",), tag="consts")
    p.op("pool", lambda g: g.memset(c["ones_row"][:, :], 1.0),
         writes=("c_ones",), tag="consts")
    p.op("pool", lambda g: g.memset(c["ones_row_f"][:, :], 1.0),
         writes=("c_ones",), tag="consts")
    p.op("pool", lambda g: g.memset(c["masks"][:, :], 1.0),
         writes=("c_masksraw",), tag="consts")
    p.op("pool", lambda g: g.affine_select(c["ident"][:, :], c["ident"][:, :],
                                           [[1, 128]], OP.is_equal, 0.0,
                                           channel_multiplier=-1),
         reads=("c_identraw",), writes=("c_ident",), tag="consts")
    for k in range(4):
        def f_mask(g, k=k):
            return g.affine_select(c["masks"][:, 512 * k:512 * (k + 1)],
                                   c["masks"][:, 512 * k:512 * (k + 1)],
                                   [[1, 512]], OP.is_ge, 0.0,
                                   base=-128 * k, channel_multiplier=-1)
        p.op("pool", f_mask, reads=("c_masksraw",), writes=("c_masks",), tag="mask")
    return c


def emit_ln_to_featmajor(p: Prog, c, xsrc_ap_fn, n_tiles, dst_dram, dst_col0,
                         pfx, eps=1e-5, extra_reads=()):
    xt = [p.sbuf(f"{pfx}_x{i}", [128, DMODEL], F32) for i in range(2)]
    sq = p.sbuf(f"{pfx}_sq", [128, DMODEL], F32)
    s1 = p.sbuf(f"{pfx}_s1", [128, 1], F32)
    s2 = p.sbuf(f"{pfx}_s2", [128, 1], F32)
    mu = p.sbuf(f"{pfx}_mu", [128, 1], F32)
    msq = p.sbuf(f"{pfx}_msq", [128, 1], F32)
    var = p.sbuf(f"{pfx}_var", [128, 1], F32)
    stdv = p.sbuf(f"{pfx}_std", [128, 1], F32)
    rstd = p.sbuf(f"{pfx}_rstd", [128, 1], F32)
    ln = p.sbuf(f"{pfx}_ln", [128, DMODEL], BF16)
    stg = [p.sbuf(f"{pfx}_stg{i}", [128, DMODEL], BF16) for i in range(2)]
    pst = [p.psum(f"{pfx}_pst{i}", [128, 128], BF16) for i in range(2)]

    for tt in range(n_tiles):
        xb = xt[tt % 2]
        kx = f"{pfx}_x{tt % 2}"
        p.dma("sp", lambda sy, tt=tt, xb=xb: sy.dma_start(xb[:, :], xsrc_ap_fn(tt)),
              reads=extra_reads, writes=(kx,), tag=f"{pfx}_ldx{tt}")
        p.op("dve", lambda v, xb=xb: v.reduce_sum(s1[:, :], xb[:, :], axis=AX.X),
             reads=(kx,), writes=(f"{pfx}_s1",))
        p.op("act", lambda s, xb=xb: s.activation(sq[:, :], xb[:, :], AF.Square),
             reads=(kx,), writes=(f"{pfx}_sq",))
        p.op("dve", lambda v: v.reduce_sum(s2[:, :], sq[:, :], axis=AX.X),
             reads=(f"{pfx}_sq",), writes=(f"{pfx}_s2",))
        p.op("dve", lambda v: v.tensor_scalar_mul(mu[:, :], s1[:, :], 1.0 / DMODEL),
             reads=(f"{pfx}_s1",), writes=(f"{pfx}_mu",))
        p.op("dve", lambda v: v.tensor_mul(msq[:, :], mu[:, :], mu[:, :]),
             reads=(f"{pfx}_mu",), writes=(f"{pfx}_msq",))
        p.op("dve", lambda v: v.scalar_tensor_tensor(
            var[:, :], s2[:, :], 1.0 / DMODEL, msq[:, :], op0=OP.mult, op1=OP.subtract),
            reads=(f"{pfx}_s2", f"{pfx}_msq"), writes=(f"{pfx}_var",))
        epsc, epsk = p.const_col(eps)
        p.op("act", lambda s, epsc=epsc: s.activation(stdv[:, :], var[:, :], AF.Sqrt,
                                                      bias=epsc[:, :]),
             reads=(f"{pfx}_var", epsk), writes=(f"{pfx}_std",))
        p.op("dve", lambda v: v.reciprocal(rstd[:, :], stdv[:, :]),
             reads=(f"{pfx}_std",), writes=(f"{pfx}_rstd",))
        p.op("dve", lambda v, xb=xb: v.tensor_scalar(
            ln[:, :], xb[:, :], mu[:, :], rstd[:, :], op0=OP.subtract, op1=OP.mult),
            reads=(kx, f"{pfx}_mu", f"{pfx}_rstd"), writes=(f"{pfx}_ln",))
        st = stg[tt % 2]
        kst = f"{pfx}_stg{tt % 2}"
        for hc in range(16):
            pb_ = pst[hc % 2]
            kpb_ = f"{pfx}_pst{hc % 2}"
            p.op("pe", lambda t, hc=hc, pb_=pb_: t.transpose(
                pb_[:, :], ln[:, 128 * hc:128 * (hc + 1)], c["ident"][:, :]),
                reads=(f"{pfx}_ln", "c_ident"), writes=(kpb_,))
            p.op("dve", lambda v, hc=hc, st=st, pb_=pb_: v.tensor_copy(
                st[:, 128 * hc:128 * (hc + 1)], pb_[:, :]),
                reads=(kpb_,), writes=(kst,))
        def f_st(sy, tt=tt, st=st):
            W = dst_dram.shape[1]
            dst = bass.AP(dst_dram, (dst_col0 + 128 * tt) + 0,
                          [[W, 128], [128 * W, 16], [1, 128]])
            src = bass.AP(st, 0, [[DMODEL, 128], [128, 16], [1, 128]])
            return sy.dma_start(dst, src)
        p.dma("act", f_st, reads=(kst,), writes=(f"{pfx}_all",),
              tag=f"{pfx}_st{tt}")
    return ln



def build_launch_A(debug=False, p=None, io=None):
    standalone = p is None
    if standalone:
        p = Prog("A")
        xb_d = p.dram("xb", [S, DMODEL], F32, kind="ExternalInput")
        wsl_d = p.dram("wsl", [DMODEL, 1536], F32, kind="ExternalInput")
        oTn_d = p.dram("oTn", [512, S], BF16, kind="ExternalOutput")
        lnT_d = p.dram("lnT", [DMODEL, S], BF16)
        a2a_blocks = False
    else:
        xb_d, wsl_d, oTn_d, lnT_d = io["xb"], io["wsl"], io["oTn"], io["lnT"]
        a2a_blocks = False
    if debug:
        dbg_lnT = p.dram("dbg_lnT", [DMODEL, S], BF16, kind="ExternalOutput")
        dbg_q = p.dram("dbg_q", [128, S], BF16, kind="ExternalOutput")
        dbg_v = p.dram("dbg_v", [128, S], BF16, kind="ExternalOutput")
        dbg_n2 = p.dram("dbg_n2", [1, 2 * S], F32, kind="ExternalOutput")
        dbg_negMb = p.dram("dbg_negMb", [128, 1], F32, kind="ExternalOutput")
        dbg_den = p.dram("dbg_den", [1, 512], F32, kind="ExternalOutput")
        dbg_eT = p.dram("dbg_eT", [128, 512], BF16, kind="ExternalOutput")

    c = io["consts"] if io else emit_consts(p)

    emit_ln_to_featmajor(p, c, lambda tt: xb_d[128 * tt:128 * (tt + 1), :],
                         S // 128, lnT_d, 0, "ln0")

    wq = p.sbuf("wq", [128, DMODEL], BF16)
    wk = p.sbuf("wk", [128, DMODEL], BF16)
    wv = p.sbuf("wv", [128, DMODEL], BF16)
    lnG = p.sbuf("lnG", [128, 16 * 512], BF16)
    qsb = p.sbuf("qsb", [128, S], BF16)
    ksb = p.sbuf("ksb", [128, S], BF16)
    vsb = p.sbuf("vsb", [128, S], BF16)
    sqs = p.sbuf("sqs", [128, S], BF16)
    n2 = p.sbuf("n2", [1, 2 * S], F32)
    mx = p.sbuf("mx", [1, 2], F32)
    mprod = p.sbuf("mprod", [1, 1], F32)
    mroot = p.sbuf("mroot", [1, 1], F32)
    negM = p.sbuf("negM", [1, 1], F32)
    negMb = p.sbuf("negMb", [128, 1], F32)
    eT = [p.sbuf(f"eT{i}", [128, 512], BF16) for i in range(2)]
    den = p.sbuf("den", [1, 512], F32)
    denr = p.sbuf("denr", [1, 512], F32)
    o_un = p.sbuf("o_un", [128, 512], BF16)
    oTn = p.sbuf("oTn_sb", [128, S], BF16)

    psQ = p.psum("psQ", [128, 512])
    psS = [p.psum(f"psS{i}", [128, 512]) for i in range(2)]
    psO = p.psum("psO", [128, 512])
    psD = p.psum("psD", [1, 512])
    psB = p.psum("psB", [128, 512])
    psV = psQ
    psN = psD

    n_span = S // 512

    for head in range(4):
        for wbuf, key, col0 in ((wq, "wq", 128 * head), (wk, "wk", 512 + 128 * head),
                                (wv, "wv", 1024 + 128 * head)):
            def f_ldw(g, wbuf=wbuf, col0=col0):
                src = bass.AP(wsl_d, col0, [[1536, 128], [128 * 1536, 16], [1, 128]])
                dst = bass.AP(wbuf, 0, [[DMODEL, 128], [128, 16], [1, 128]])
                return g.dma_start(dst, src)
            p.dma("pool", f_ldw, reads=(), writes=(key,), tag=f"ldw_h{head}")

        for sp in range(n_span):
            def f_ldln(sy, sp=sp):
                src = bass.AP(lnT_d, 512 * sp, [[S, 128], [128 * S, 16], [1, 512]])
                dst = bass.AP(lnG, 0, [[16 * 512, 128], [512, 16], [1, 512]])
                return sy.dma_start(dst, src)
            p.dma("sp", f_ldln, reads=("ln0_all",),
                  writes=("lnG",), tag=f"ldln{sp}")
            for (wbuf, wkey, dst, dkey) in ((wq, "wq", qsb, "qsb"), (wk, "wk", ksb, "ksb")):
                for hc in range(16):
                    p.op("pe", lambda t, hc=hc, sp=sp, wbuf=wbuf: t.matmul(
                        psQ[:, :], wbuf[:, 128 * hc:128 * (hc + 1)],
                        lnG[:, 512 * hc:512 * (hc + 1)],
                        start=(hc == 0), stop=(hc == 15)),
                        reads=(wkey, "lnG"), writes=("psQ",))
                p.op("dve", lambda v, sp=sp, dst=dst: v.tensor_copy(
                    dst[:, 512 * sp:512 * (sp + 1)], psQ[:, :]),
                    reads=("psQ",), writes=(dkey,))
            for tc in range(4):
                for hc in range(16):
                    p.op("pe", lambda t, hc=hc, sp=sp, tc=tc: t.matmul(
                        psV[:, 0:128],
                        lnG[:, 512 * hc + 128 * tc:512 * hc + 128 * (tc + 1)],
                        wv[:, 128 * hc:128 * (hc + 1)],
                        start=(hc == 0), stop=(hc == 15)),
                        reads=("wv", "lnG"), writes=("psQ",))
                p.op("dve", lambda v, sp=sp, tc=tc: v.tensor_copy(
                    vsb[:, 512 * sp + 128 * tc:512 * sp + 128 * (tc + 1)], psV[:, 0:128]),
                    reads=("psQ",), writes=("vsb",))

        for (src, skey, off) in ((qsb, "qsb", 0), (ksb, "ksb", S)):
            p.op("act", lambda s, src=src: s.activation(sqs[:, :], src[:, :], AF.Square),
                 reads=(skey,), writes=("sqs",))
            for sp in range(n_span):
                p.op("pe", lambda t, sp=sp: t.matmul(
                    psN[:, :], c["ones_col"][:, :], sqs[:, 512 * sp:512 * (sp + 1)],
                    start=True, stop=True),
                    reads=("sqs", "c_ones"), writes=("psD",))
                p.op("dve", lambda v, sp=sp, off=off: v.tensor_copy(
                    n2[:, off + 512 * sp:off + 512 * (sp + 1)], psN[:, :]),
                    reads=("psD",), writes=("n2",))
        p.op("dve", lambda v: v.reduce_max(mx[:, 0:1], n2[:, 0:S], axis=AX.X),
             reads=("n2",), writes=("mx",))
        p.op("dve", lambda v: v.reduce_max(mx[:, 1:2], n2[:, S:2 * S], axis=AX.X),
             reads=("n2",), writes=("mx",))
        p.op("dve", lambda v: v.tensor_mul(mprod[:, :], mx[:, 0:1], mx[:, 1:2]),
             reads=("mx",), writes=("mprod",))
        p.op("act", lambda s: s.activation(mroot[:, :], mprod[:, :], AF.Sqrt,
                                           scale=1.0 / 128.0),
             reads=("mprod",), writes=("mroot",))
        p.op("dve", lambda v: v.tensor_scalar_mul(negM[:, :], mroot[:, :], -1.0),
             reads=("mroot",), writes=("negM",))
        p.op("pe", lambda t: t.matmul(psB[:, 0:1], c["ones_row_f"][:, :], negM[:, :],
                                      start=True, stop=True),
             reads=("negM", "c_ones"), writes=("psB",))
        p.op("dve", lambda v: v.tensor_copy(negMb[:, :], psB[:, 0:1]),
             reads=("psB",), writes=("negMb",))

        for sp in range(n_span):
            n_ch = 4 * (sp + 1)
            for ch in range(n_ch):
                pss = psS[ch % 2]
                kps = f"psS{ch % 2}"
                ket = f"eT{ch % 2}"
                ebuf = eT[ch % 2]
                p.op("pe", lambda t, ch=ch, sp=sp, pss=pss: t.matmul(
                    pss[:, :], ksb[:, 128 * ch:128 * (ch + 1)],
                    qsb[:, 512 * sp:512 * (sp + 1)], start=True, stop=True),
                    reads=("ksb", "qsb"), writes=(kps,))
                p.op("act", lambda s, pss=pss, ebuf=ebuf: s.activation(
                    ebuf[:, :], pss[:, :], AF.Exp, bias=negMb[:, :],
                    scale=INV_SQRT_D0),
                    reads=(kps, "negMb"), writes=(ket,))
                off = 128 * ch - 512 * sp
                if off >= 0:
                    k = off // 128
                    p.op("dve", lambda v, ebuf=ebuf, k=k: v.tensor_mul(
                        ebuf[:, :], ebuf[:, :], c["masks"][:, 512 * k:512 * (k + 1)]),
                        reads=(ket, "c_masks"), writes=(ket,))
                p.op("pe", lambda t, ch=ch, ebuf=ebuf, n_ch=n_ch: t.matmul(
                    psD[:, :], c["ones_col"][:, :], ebuf[:, :],
                    start=(ch == 0), stop=(ch == n_ch - 1)),
                    reads=(ket, "c_ones"), writes=("psD",))
                p.op("pe", lambda t, ch=ch, ebuf=ebuf, n_ch=n_ch: t.matmul(
                    psO[:, :], vsb[:, 128 * ch:128 * (ch + 1)], ebuf[:, :],
                    start=(ch == 0), stop=(ch == n_ch - 1)),
                    reads=(ket, "vsb"), writes=("psO",))
            p.op("dve", lambda v: v.tensor_copy(den[:, :], psD[:, :]),
                 reads=("psD",), writes=("den",))
            p.op("dve", lambda v: v.reciprocal(denr[:, :], den[:, :]),
                 reads=("den",), writes=("denr",))
            p.op("pe", lambda t: t.matmul(psB[:, :], c["ones_row_f"][:, :], denr[:, :],
                                          start=True, stop=True),
                 reads=("denr", "c_ones"), writes=("psB",))
            p.op("dve", lambda v: v.tensor_copy(o_un[:, :], psO[:, :]),
                 reads=("psO",), writes=("o_un",))
            p.op("dve", lambda v, sp=sp: v.tensor_mul(
                oTn[:, 512 * sp:512 * (sp + 1)], o_un[:, :], psB[:, :]),
                reads=("o_un", "psB"), writes=("oTn_sb",))
        if a2a_blocks:
            for jb in range(4):
                p.dma("sp", lambda sy, head=head, jb=jb: sy.dma_start(
                    oTn_d[512 * jb + 128 * head:512 * jb + 128 * (head + 1), :],
                    oTn[:, 1024 * jb:1024 * (jb + 1)]),
                    reads=("oTn_sb",), writes=(f"oTn_out{head}",), tag="st_o")
        else:
            p.dma("sp", lambda sy, head=head: sy.dma_start(
                oTn_d[128 * head:128 * (head + 1), :], oTn[:, :]),
                reads=("oTn_sb",), writes=(f"oTn_out{head}",), tag=f"st_o{head}")
        if debug and head == 0:
            p.dma("sp", lambda sy: sy.dma_start(dbg_lnT[:, :], lnT_d[:, :]),
                  reads=("ln0_all",), writes=("dbg1",))
            p.dma("sp", lambda sy: sy.dma_start(dbg_q[:, :], qsb[:, :]),
                  reads=("qsb",), writes=("dbg2",))
            p.dma("sp", lambda sy: sy.dma_start(dbg_v[:, :], vsb[:, :]),
                  reads=("vsb",), writes=("dbg3",))
            p.dma("sp", lambda sy: sy.dma_start(dbg_n2[:, :], n2[:, :]),
                  reads=("n2",), writes=("dbg4",))
            p.dma("sp", lambda sy: sy.dma_start(dbg_negMb[:, :], negMb[:, :]),
                  reads=("negMb",), writes=("dbg5",))
            p.dma("sp", lambda sy: sy.dma_start(dbg_den[:, :], den[:, :]),
                  reads=("den",), writes=("dbg6",))
            p.dma("sp", lambda sy: sy.dma_start(dbg_eT[:, :], eT[1][:, :]),
                  reads=("eT1",), writes=("dbg7",))
    if standalone:
        return p.finalize()



def emit_ln_from_sbuf(p, c, get_tile, n_tiles, dst_dram, dst_col0, pfx,
                      eps=1e-5, post_tile=None):
    sq = p.sbuf(f"{pfx}_sq", [128, DMODEL], F32)
    s1 = p.sbuf(f"{pfx}_s1", [128, 1], F32)
    s2 = p.sbuf(f"{pfx}_s2", [128, 1], F32)
    mu = p.sbuf(f"{pfx}_mu", [128, 1], F32)
    msq = p.sbuf(f"{pfx}_msq", [128, 1], F32)
    var = p.sbuf(f"{pfx}_var", [128, 1], F32)
    stdv = p.sbuf(f"{pfx}_std", [128, 1], F32)
    rstd = p.sbuf(f"{pfx}_rstd", [128, 1], F32)
    ln = p.sbuf(f"{pfx}_ln", [128, DMODEL], BF16)
    stg = [p.sbuf(f"{pfx}_stg{i}", [128, DMODEL], BF16) for i in range(2)]
    pst = [p.psum(f"{pfx}_pst{i}", [128, 128], BF16) for i in range(2)]
    epsc, epsk = p.const_col(eps)

    for tt in range(n_tiles):
        xb, kx = get_tile(tt)
        p.op("dve", lambda v, xb=xb: v.reduce_sum(s1[:, :], xb[:, :], axis=AX.X),
             reads=(kx,), writes=(f"{pfx}_s1",))
        p.op("act", lambda s, xb=xb: s.activation(sq[:, :], xb[:, :], AF.Square),
             reads=(kx,), writes=(f"{pfx}_sq",))
        p.op("dve", lambda v: v.reduce_sum(s2[:, :], sq[:, :], axis=AX.X),
             reads=(f"{pfx}_sq",), writes=(f"{pfx}_s2",))
        p.op("dve", lambda v: v.tensor_scalar_mul(mu[:, :], s1[:, :], 1.0 / DMODEL),
             reads=(f"{pfx}_s1",), writes=(f"{pfx}_mu",))
        p.op("dve", lambda v: v.tensor_mul(msq[:, :], mu[:, :], mu[:, :]),
             reads=(f"{pfx}_mu",), writes=(f"{pfx}_msq",))
        p.op("dve", lambda v: v.scalar_tensor_tensor(
            var[:, :], s2[:, :], 1.0 / DMODEL, msq[:, :], op0=OP.mult, op1=OP.subtract),
            reads=(f"{pfx}_s2", f"{pfx}_msq"), writes=(f"{pfx}_var",))
        p.op("act", lambda s: s.activation(stdv[:, :], var[:, :], AF.Sqrt,
                                           bias=epsc[:, :]),
             reads=(f"{pfx}_var", epsk), writes=(f"{pfx}_std",))
        p.op("dve", lambda v: v.reciprocal(rstd[:, :], stdv[:, :]),
             reads=(f"{pfx}_std",), writes=(f"{pfx}_rstd",))
        p.op("dve", lambda v, xb=xb: v.tensor_scalar(
            ln[:, :], xb[:, :], mu[:, :], rstd[:, :], op0=OP.subtract, op1=OP.mult),
            reads=(kx, f"{pfx}_mu", f"{pfx}_rstd"), writes=(f"{pfx}_ln",))
        st = stg[tt % 2]
        kst = f"{pfx}_stg{tt % 2}"
        for hc in range(16):
            pb_ = pst[hc % 2]
            kpb_ = f"{pfx}_pst{hc % 2}"
            p.op("pe", lambda t, hc=hc, pb_=pb_: t.transpose(
                pb_[:, :], ln[:, 128 * hc:128 * (hc + 1)], c["ident"][:, :]),
                reads=(f"{pfx}_ln", "c_ident"), writes=(kpb_,))
            p.op("dve", lambda v, hc=hc, st=st, pb_=pb_: v.tensor_copy(
                st[:, 128 * hc:128 * (hc + 1)], pb_[:, :]),
                reads=(kpb_,), writes=(kst,))
        def f_st(sy, tt=tt, st=st):
            W = dst_dram.shape[1]
            dst = bass.AP(dst_dram, (dst_col0 + 128 * tt) + 0,
                          [[W, 128], [128 * W, 16], [1, 128]])
            src2 = bass.AP(st, 0, [[DMODEL, 128], [128, 16], [1, 128]])
            return sy.dma_start(dst, src2)
        p.dma("act", f_st, reads=(kst,), writes=(f"{pfx}_all",), tag=f"{pfx}_st{tt}")
        if post_tile is not None:
            post_tile(tt, st, kst)


def build_launch_B(p=None, io=None):
    standalone = p is None
    with_router = standalone
    if standalone:
        p = Prog("B")
        xsl_d = p.dram("xsl", [TOK_PER_CORE, DMODEL], F32, kind="ExternalInput")
        gosl_d = p.dram("gosl", [DMODEL, TOK_PER_CORE], BF16, kind="ExternalInput")
        woT_d = p.dram("woT", [DMODEL, DMODEL], F32, kind="ExternalInput")
        gwT_d = p.dram("gwT", [DMODEL, 16], F32, kind="ExternalInput")
        x1_d = p.dram("x1", [TOK_PER_CORE, DMODEL], F32, kind="ExternalOutput")
        ln1T_d = p.dram("ln1T", [DMODEL, TOK_PER_CORE], BF16, kind="ExternalOutput")
        comb_d = p.dram("comb", [TOK_PER_CORE, 16], F32, kind="ExternalOutput")
        c = emit_consts(p)
    else:
        xsl_d, gosl_d, woT_d = io["xsl"], io["gosl"], io["woT"]
        x1_d, ln1T_d = io["x1"], io["ln1T"]
        c = io["consts"]
    NT = TOK_PER_CORE // 128

    go = p.sbuf("go", [128, 16 * TOK_PER_CORE // 1], BF16)
    xs = [p.sbuf(f"xs{i}", [128, DMODEL], F32) for i in range(NT)]
    wo = [p.sbuf(f"wo{i}", [128, 16 * 512], BF16) for i in range(2)]
    gw = p.sbuf("gw", [128, 16 * 16], BF16)
    psO = p.psum("psO", [128, 512])
    psL = p.psum("psL", [128, 16])

    if standalone:
        def f_go(sy):
            src = bass.AP(gosl_d, 0, [[TOK_PER_CORE, 128], [128 * TOK_PER_CORE, 16],
                                      [1, TOK_PER_CORE]])
            dst = bass.AP(go, 0, [[16 * TOK_PER_CORE, 128], [TOK_PER_CORE, 16],
                                  [1, TOK_PER_CORE]])
            return sy.dma_start(dst, src)
    else:
        _pidc = {}
        def f_go(sy):
            if "c" not in _pidc:
                _pidc["c"] = (sy.partition_id() % 4) * 1024
            src = bass.AP(gosl_d, _pidc["c"], [[S, 128], [128 * S, 16],
                                               [1, TOK_PER_CORE]])
            dst = bass.AP(go, 0, [[16 * TOK_PER_CORE, 128], [TOK_PER_CORE, 16],
                                  [1, TOK_PER_CORE]])
            return sy.dma_start(dst, src)
    p.dma("sp", f_go, writes=("go",), tag="ldgo")
    for tt in range(NT):
        p.dma("sp", lambda sy, tt=tt: sy.dma_start(
            xs[tt][:, :], xsl_d[128 * tt:128 * (tt + 1), :]),
            writes=(f"xs{tt}",), tag="ldx")
    if with_router:
        def f_gw(g):
            src = bass.AP(gwT_d, 0, [[16, 128], [128 * 16, 16], [1, 16]])
            dst = bass.AP(gw, 0, [[16 * 16, 128], [16, 16], [1, 16]])
            return g.dma_start(dst, src)
        p.dma("pool", f_gw, writes=("gw",), tag="ldgw")

    for fb in range(4):
        wob = wo[fb % 2]
        kwo = f"wo{fb % 2}"
        def f_wo(g, fb=fb, wob=wob):
            src = bass.AP(woT_d, 512 * fb, [[DMODEL, 128], [128 * DMODEL, 16], [1, 512]])
            dst = bass.AP(wob, 0, [[16 * 512, 128], [512, 16], [1, 512]])
            return g.dma_start(dst, src)
        p.dma("pool", f_wo, writes=(kwo,), tag=f"ldwo{fb}")
        for tt in range(NT):
            for hc in range(16):
                p.op("pe", lambda t, hc=hc, tt=tt, wob=wob: t.matmul(
                    psO[:, :], go[:, TOK_PER_CORE * hc + 128 * tt:
                                  TOK_PER_CORE * hc + 128 * (tt + 1)],
                    wob[:, 512 * hc:512 * (hc + 1)],
                    start=(hc == 0), stop=(hc == 15)),
                    reads=("go", kwo), writes=("psO",))
            p.op("dve", lambda v, tt=tt, fb=fb: v.tensor_add(
                xs[tt][:, 512 * fb:512 * (fb + 1)],
                xs[tt][:, 512 * fb:512 * (fb + 1)], psO[:, :]),
                reads=("psO", f"xs{tt}"), writes=(f"xs{tt}",))
    for tt in range(NT):
        p.dma("sp", lambda sy, tt=tt: sy.dma_start(
            x1_d[128 * tt:128 * (tt + 1), :], xs[tt][:, :]),
            reads=(f"xs{tt}",), writes=("x1o",), tag="stx1")

    lg = p.sbuf("lg", [128, 16], F32)
    ee = p.sbuf("ee", [128, 16], F32)
    ww = p.sbuf("ww", [128, 16], F32)
    macc = p.sbuf("macc", [128, 16], F32)
    mask1 = p.sbuf("mask1", [128, 16], F32)
    mrow = p.sbuf("mrow", [128, 1], F32)
    negm = p.sbuf("negm", [128, 1], F32)
    s4 = p.sbuf("s4", [128, 1], F32)
    r4 = p.sbuf("r4", [128, 1], F32)
    combt = p.sbuf("combt", [128, 16], F32)

    def router(tt, st, kst):
        for hc in range(16):
            p.op("pe", lambda t, hc=hc, st=st: t.matmul(
                psL[:, :], st[:, 128 * hc:128 * (hc + 1)],
                gw[:, 16 * hc:16 * (hc + 1)],
                start=(hc == 0), stop=(hc == 15)),
                reads=(kst, "gw"), writes=("psL",))
        p.op("dve", lambda v: v.tensor_copy(lg[:, :], psL[:, :]),
             reads=("psL",), writes=("lg",))
        p.op("dve", lambda v: v.reduce_max(negm[:, :], lg[:, :], axis=AX.X,
                                           negate=True),
             reads=("lg",), writes=("negm",))
        p.op("act", lambda s: s.activation(ee[:, :], lg[:, :], AF.Exp,
                                           bias=negm[:, :]),
             reads=("lg", "negm"), writes=("ee",))
        p.op("dve", lambda v: v.tensor_copy(ww[:, :], ee[:, :]),
             reads=("ee",), writes=("ww",))
        p.op("pool", lambda g: g.memset(macc[:, :], 0.0),
             writes=("macc",))
        for it in range(4):
            p.op("dve", lambda v: v.reduce_max(mrow[:, :], ww[:, :], axis=AX.X),
                 reads=("ww",), writes=("mrow",))
            p.op("dve", lambda v: v.tensor_single_scalar(
                mask1[:, :], ww[:, :], mrow[:, :], OP.is_ge),
                reads=("ww", "mrow"), writes=("mask1",))
            p.op("dve", lambda v: v.tensor_max(macc[:, :], macc[:, :], mask1[:, :]),
                 reads=("macc", "mask1"), writes=("macc",))
            if it < 3:
                p.op("dve", lambda v: v.scalar_tensor_tensor(
                    ww[:, :], mask1[:, :], -1e30, ww[:, :], op0=OP.mult, op1=OP.add),
                    reads=("mask1", "ww"), writes=("ww",))
        p.op("dve", lambda v: v.tensor_mul(combt[:, :], macc[:, :], ee[:, :]),
             reads=("macc", "ee"), writes=("combt",))
        p.op("dve", lambda v: v.reduce_sum(s4[:, :], combt[:, :], axis=AX.X),
             reads=("combt",), writes=("s4",))
        p.op("dve", lambda v: v.reciprocal(r4[:, :], s4[:, :]),
             reads=("s4",), writes=("r4",))
        p.op("dve", lambda v: v.tensor_scalar_mul(combt[:, :], combt[:, :],
                                                  r4[:, :]),
             reads=("combt", "r4"), writes=("combt",))
        p.dma("sp", lambda sy, tt=tt: sy.dma_start(
            comb_d[128 * tt:128 * (tt + 1), :], combt[:, :]),
            reads=("combt",), writes=("combo",), tag="stcomb")

    emit_ln_from_sbuf(p, c, lambda tt: (xs[tt], f"xs{tt}"), NT, ln1T_d, 0,
                      "ln1", post_tile=router if with_router else None)
    if standalone:
        return p.finalize()



def emit_consts_min(p):
    c = {}
    c["ones_col"] = p.sbuf("c_ones_col", [128, 1], BF16)
    c["ones_row_f"] = p.sbuf("c_ones_row_f", [1, 128], F32)
    p.op("pool", lambda g: g.memset(c["ones_col"][:, :], 1.0), writes=("c_ones",))
    p.op("pool", lambda g: g.memset(c["ones_row_f"][:, :], 1.0), writes=("c_ones",))
    return c


def emit_dq_strip(p, wt, kwt, wt_off, stile, kst, st_off, shift, width,
                  final_writes, psub=None):
    t = p._dq_tmp
    lo, hi = (0, 128) if psub is None else psub
    P = slice(lo, hi)
    n2c, n2k = p.const_col(-2.0)
    n6c, n6k = p.const_col(-6.0)
    W = width
    f, e8, d, r1, r2, m = t["f"], t["e8"], t["d"], t["r1"], t["r2"], t["m"]
    fi = t["fi"]
    q, G = t["f"], t["e8"]
    p.op("dve", lambda v: v.tensor_scalar(
        fi[P, 0:W], bass.AP(wt, lo * wt.shape[1] + wt_off,
                            [[wt.shape[1], hi - lo], [1, W]]),
        int(shift), 15, op0=OP.logical_shift_right, op1=OP.bitwise_and),
        reads=(kwt,), writes=("dq_fi",))
    p.op("dve", lambda v: v.tensor_copy(f[P, 0:W], fi[P, 0:W]),
         reads=("dq_fi",), writes=("dq_f",))
    p.op("dve", lambda v: v.tensor_scalar(
        e8[P, 0:W], f[P, 0:W], 8.0, 8.0, op0=OP.is_ge, op1=OP.mult),
        reads=("dq_f",), writes=("dq_e8",))
    p.op("dve", lambda v: v.scalar_tensor_tensor(
        d[P, 0:W], e8[P, 0:W], -1.0, f[P, 0:W], op0=OP.mult, op1=OP.add),
        reads=("dq_e8", "dq_f"), writes=("dq_d",))
    p.op("act", lambda s: s.activation(r1[P, 0:W], d[P, 0:W], AF.Relu,
                                       bias=n2c[P, :], scale=0.5),
         reads=("dq_d", n2k), writes=("dq_r1",))
    p.op("act", lambda s: s.activation(r2[P, 0:W], d[P, 0:W], AF.Relu,
                                       bias=n6c[P, :]),
         reads=("dq_d", n6k), writes=("dq_r2",))
    p.op("dve", lambda v: v.scalar_tensor_tensor(
        m[P, 0:W], d[P, 0:W], 0.5, r1[P, 0:W], op0=OP.mult, op1=OP.add),
        reads=("dq_d", "dq_r1"), writes=("dq_m",))
    p.op("dve", lambda v: v.tensor_add(m[P, 0:W], m[P, 0:W], r2[P, 0:W]),
         reads=("dq_m", "dq_r2"), writes=("dq_m",))
    p.op("dve", lambda v: v.tensor_scalar(
        q[P, 0:W], e8[P, 0:W], -0.25, 1.0, op0=OP.mult, op1=OP.add),
        reads=("dq_e8",), writes=("dq_q",))
    p.op("dve", lambda v: v.tensor_mul(
        G[P, 0:W], q[P, 0:W],
        bass.AP(stile, lo * stile.shape[1] + st_off,
                [[stile.shape[1], hi - lo], [1, W]])),
        reads=("dq_q", kst), writes=("dq_G",))
    for (src_ap_fn, out_ap_fn, out_key) in final_writes:
        p.op("dve", lambda v, src_ap_fn=src_ap_fn, out_ap_fn=out_ap_fn:
             v.tensor_mul(out_ap_fn(), src_ap_fn(m), src_ap_fn(G)),
             reads=("dq_m", "dq_G"), writes=(out_key,))


def emit_router_C(p, c, g1_d, gwTp_d, cbuf):
    gw = p.sbuf("gwC", [128, 16 * 16], BF16)
    lg = p.sbuf("lgC", [128, 16], F32)
    ee = p.sbuf("eeC", [128, 16], F32)
    ww = p.sbuf("wwC", [128, 16], F32)
    macc = p.sbuf("maccC", [128, 16], F32)
    mask1 = p.sbuf("mask1C", [128, 16], F32)
    mrow = p.sbuf("mrowC", [128, 1], F32)
    negm = p.sbuf("negmC", [128, 1], F32)
    s4 = p.sbuf("s4C", [128, 1], F32)
    r4 = p.sbuf("r4C", [128, 1], F32)
    combt = p.sbuf("combtC", [128, 16], F32)
    g1t = p.sbuf("g1tC", [128, 16 * 128], BF16)
    psR = p.psum("psRC", [128, 16])

    def f_gw(g):
        srcap = bass.AP(gwTp_d, 0, [[16, 128], [128 * 16, 16], [1, 16]])
        dst = bass.AP(gw, 0, [[16 * 16, 128], [16, 16], [1, 16]])
        return g.dma_start(dst, srcap)
    p.dma("pool", f_gw, writes=("gwC",), tag="ldgwC")
    for tt in range(NTOK // 128):
        r, tc = tt // 8, 128 * (tt % 8)
        def f_g1t(sy, r=r, tc=tc):
            srcap = bass.AP(g1_d, 2048 * r * 1024 + tc,
                            [[1024, 128], [128 * 1024, 16], [1, 128]])
            dst = bass.AP(g1t, 0, [[16 * 128, 128], [128, 16], [1, 128]])
            return sy.dma_start(dst, srcap)
        p.dma("sp", f_g1t, reads=("g1tC",), writes=("g1tC",), tag="ldg1t")
        for hc in range(16):
            p.op("pe", lambda t, hc=hc: t.matmul(
                psR[:, :], g1t[:, 128 * hc:128 * (hc + 1)],
                gw[:, 16 * hc:16 * (hc + 1)],
                start=(hc == 0), stop=(hc == 15)),
                reads=("g1tC", "gwC"), writes=("psRC",))
        p.op("dve", lambda v: v.tensor_copy(lg[:, :], psR[:, :]),
             reads=("psRC",), writes=("lgC",))
        p.op("dve", lambda v: v.reduce_max(negm[:, :], lg[:, :], axis=AX.X,
                                           negate=True),
             reads=("lgC",), writes=("negmC",))
        p.op("act", lambda s: s.activation(ee[:, :], lg[:, :], AF.Exp,
                                           bias=negm[:, :]),
             reads=("lgC", "negmC"), writes=("eeC",))
        p.op("dve", lambda v: v.tensor_copy(ww[:, :], ee[:, :]),
             reads=("eeC",), writes=("wwC",))
        p.op("pool", lambda g: g.memset(macc[:, :], 0.0), reads=("maccC",),
             writes=("maccC",))
        for it in range(4):
            p.op("dve", lambda v: v.reduce_max(mrow[:, :], ww[:, :], axis=AX.X),
                 reads=("wwC",), writes=("mrowC",))
            p.op("dve", lambda v: v.tensor_single_scalar(
                mask1[:, :], ww[:, :], mrow[:, :], OP.is_ge),
                reads=("wwC", "mrowC"), writes=("mask1C",))
            p.op("dve", lambda v: v.tensor_max(macc[:, :], macc[:, :], mask1[:, :]),
                 reads=("maccC", "mask1C"), writes=("maccC",))
            if it < 3:
                p.op("dve", lambda v: v.scalar_tensor_tensor(
                    ww[:, :], mask1[:, :], -1e30, ww[:, :], op0=OP.mult, op1=OP.add),
                    reads=("mask1C", "wwC"), writes=("wwC",))
        p.op("dve", lambda v: v.tensor_mul(combt[:, :], macc[:, :], ee[:, :]),
             reads=("maccC", "eeC"), writes=("combtC",))
        p.op("dve", lambda v: v.reduce_sum(s4[:, :], combt[:, :], axis=AX.X),
             reads=("combtC",), writes=("s4C",))
        p.op("dve", lambda v: v.reciprocal(r4[:, :], s4[:, :]),
             reads=("s4C",), writes=("r4C",))
        p.op("dve", lambda v, tt=tt: v.tensor_scalar_mul(
            cbuf[:, 2 * tt:2 * tt + 2], combt[:, 0:2], r4[:, :]),
            reads=("combtC", "r4C"), writes=("cbuf",))


def build_launch_C(p=None, io=None):
    standalone = p is None
    if standalone:
        p = Prog("C")
        g1_d = p.dram("g1", [DMODEL, NTOK], BF16, kind="ExternalInput")
        pkgu_d = p.dram("pkgu", [2, 256, 3072], I32, kind="ExternalInput")
        scgu_d = p.dram("scgu", [2, 16, 3072], F32, kind="ExternalInput")
        pkdn_d = p.dram("pkdn", [2, 192, 2048], I32, kind="ExternalInput")
        scdn_d = p.dram("scdn", [2, 12, 2048], F32, kind="ExternalInput")
        combsl_d = p.dram("combsl", [NTOK, 2], F32, kind="ExternalInput")
        pksguc_d = p.dram("pksguc", [256, 384], I32, kind="ExternalInput")
        scsguc_d = p.dram("scsguc", [16, 384], F32, kind="ExternalInput")
        pksdnc_d = p.dram("pksdnc", [192, 256], I32, kind="ExternalInput")
        scsdnc_d = p.dram("scsdnc", [12, 256], F32, kind="ExternalInput")
        partial_d = p.dram("partial", [NTOK, DMODEL], BF16, kind="ExternalOutput")
        sgslo_d = p.dram("sgslo", [2048, 384], BF16, kind="ExternalOutput")
        sdno_d = p.dram("sdno", [1536, 256], BF16, kind="ExternalOutput")
        wgu2_d = p.dram("wgu2", [2048, 1536], BF16)
        c = emit_consts_min(p)
        g1ag = False
    else:
        g1_d = io["g1"]
        pkgu_d, scgu_d = io["pkgu"], io["scgu"]
        pkdn_d, scdn_d = io["pkdn"], io["scdn"]
        pksgu_d, scsgu_d = io["pksgu"], io["scsgu"]
        pksdn_d, scsdn_d = io["pksdn"], io["scsdn"]
        gwTp_d = io["gwTp"]
        partial_d = io["partial"]
        sgu_loc_d, sdn_loc_d = io["sgu_loc"], io["sdn_loc"]
        wgu2_d = io["wgu2"]
        c = io["consts"]
        g1ag = True

    wgu = p.sbuf("wgu", [128, 16 * 1536], BF16)
    wdn = p.sbuf("wdn", [128, 12 * 2048], BF16)
    g1s = [p.sbuf(f"g1s{i}", [128, 16 * 512], BF16) for i in range(2)]
    acts = p.sbuf("acts", [128, 6 * 512], BF16)
    sgs = [p.sbuf(f"sg{i}", [128, 512], BF16) for i in range(2)]
    po = p.sbuf("po", [128, DMODEL], BF16)
    cbuf = p.sbuf("cbuf", [128, 64 * 2], F32)
    wt = p.sbuf("wt", [128, 1536], I32)
    wtb = p.sbuf("wtb", [128, 1536], I32)
    stg = p.sbuf("stg", [128, 3072], BF16)
    h2st = p.sbuf("h2st", [128, 1536], BF16)
    shst = p.sbuf("shst", [128, 1536], BF16)
    dqt = {}
    for nm in ("f", "e8", "d", "r1", "r2", "m"):
        dqt[nm] = p.sbuf(f"dq_{nm}", [128, 1536], BF16)
    dqt["fi"] = p.sbuf("dq_fi", [128, 1536], I32)
    p._dq_tmp = dqt

    psGs = [p.psum(f"psG{i}", [128, 512]) for i in range(2)]
    psUs = [p.psum(f"psU{i}", [128, 512]) for i in range(2)]
    psD1 = p.psum("psD1", [128, 512])
    psD2 = p.psum("psD2", [128, 512])

    if not g1ag:
        def f_cb(sy):
            srcap = bass.AP(combsl_d, 0, [[2, 128], [256, 64], [1, 2]])
            dst = bass.AP(cbuf, 0, [[128, 128], [2, 64], [1, 2]])
            return sy.dma_start(dst, srcap)
        p.dma("sp", f_cb, writes=("cbuf",), tag="ldcomb")
    else:
        emit_router_C(p, c, g1_d, gwTp_d, cbuf)

    n_span = NTOK // 512

    def gu_writes(strip, ch):
        du = 768 * strip
        TW = 1536
        def r1_src(lo, n_ic):
            return lambda buf: bass.AP(buf, lo, [[TW, 128], [8, 128], [1, n_ic]])
        def r1_out(dbuf, off, n_ic):
            def f():
                w = dbuf.shape[1]
                return bass.AP(dbuf, off, [[w, 128], [1, 128], [128, n_ic]])
            return f
        def r2_src():
            return lambda buf: bass.AP(buf, 1024, [[TW, 128], [8, 64], [2, 4], [1, 2]])
        def r2_out(dbuf, off):
            def f():
                w = dbuf.shape[1]
                return bass.AP(dbuf, off, [[w, 128], [1, 64], [128, 4], [64, 2]])
            return f
        return [
            (r1_src(0, 6), r1_out(wgu, 1536 * ch + du, 6), "wgu"),
            (r1_src(6, 2), r1_out(h2st, du, 2), "h2st"),
            (r2_src(), r2_out(h2st, du + 128 * 2), "h2st"),
        ]

    def dequant_gu(e):
        for k8b in range(2):
            for r_ in range(8):
                def f_sc(g, e=e, k8b=k8b, r_=r_):
                    return g.dma_start(
                        stg[16 * r_:16 * (r_ + 1), :],
                        scgu_d[e, 8 * k8b + r_:8 * k8b + r_ + 1, :]
                        .to_broadcast((16, 3072)))
                p.dma("pool", f_sc, reads=("stg",), writes=("stg",), tag="ldsc")
            for strip, wbuf, kw in ((0, wt, "wt"), (1, wtb, "wtb")):
                p.dma("pool", lambda g, e=e, k8b=k8b, strip=strip, wbuf=wbuf:
                      g.dma_start(
                    wbuf[:, :], pkgu_d[e, 128 * k8b:128 * (k8b + 1),
                                       1536 * strip:1536 * (strip + 1)]),
                    reads=(kw,), writes=(kw,), tag="ldwt")
            for j in range(8):
                ch = 2 * j + k8b
                for strip, wbuf, kw in ((0, wt, "wt"), (1, wtb, "wtb")):
                    emit_dq_strip(p, wbuf, kw, 0, stg, "stg",
                                  1536 * strip, 4 * j, 1536,
                                  gu_writes(strip, ch))
                p.dma("act", lambda sy, ch=ch: sy.dma_start(
                    wgu2_d[128 * ch:128 * (ch + 1), :], h2st[:, :]),
                    reads=("h2st",), writes=("wgu2",), tag="sth2")

    def dn_w(ic, lo=0, hi=128, cw=2048, dst=None, coff=0):
        dbuf = wdn if dst is None else dst
        def f():
            return bass.AP(dbuf, cw * 0 + dbuf.shape[1] * lo + 2048 * ic + coff
                           if dst is None else dbuf.shape[1] * lo + coff,
                           [[dbuf.shape[1], hi - lo], [1, cw]])
        return f

    def dequant_dn(e):
        p.dma("pool", lambda g, e=e: g.dma_start(
            wt[:, 0:1024], pkdn_d[e, 0:128, 0:1024]),
            reads=("wt",), writes=("wt",), tag="lddnw")
        p.dma("pool", lambda g, e=e: g.dma_start(
            wtb[:, 0:1024], pkdn_d[e, 0:128, 1024:2048]),
            reads=("wtb",), writes=("wtb",), tag="lddnw")
        for r_ in range(8):
            def f_scf(g, e=e, r_=r_):
                return g.dma_start(
                    stg[16 * r_:16 * (r_ + 1), 0:2048],
                    scdn_d[e, r_:r_ + 1, :].to_broadcast((16, 2048)))
            p.dma("pool", f_scf, reads=("stg",), writes=("stg",), tag="lddns")
        for ic in range(8):
            for st2, wbuf, kw in ((0, wt, "wt"), (1, wtb, "wtb")):
                fw = [(lambda buf: bass.AP(buf, 0, [[1536, 128], [1, 1024]]),
                       (lambda ic=ic, st2=st2: bass.AP(
                           wdn, 2048 * ic + 1024 * st2, [[12 * 2048, 128], [1, 1024]])),
                       "wdn")]
                emit_dq_strip(p, wbuf, kw, 0, stg, "stg", 1024 * st2,
                              4 * ic, 1024, fw)
        for st2, wbuf, kw in ((0, wt, "wt"), (1, wtb, "wtb")):
            for half in range(2):
                def f_wp(g, e=e, st2=st2, wbuf=wbuf, half=half):
                    return g.dma_start(
                        wbuf[64 * half:64 * (half + 1), 0:1024],
                        pkdn_d[e, 128:192, 1024 * st2:1024 * (st2 + 1)])
                p.dma("pool", f_wp, reads=(kw,), writes=(kw,), tag="lddnwp")
        for half in range(2):
            for r_ in range(4):
                def f_scp(g, e=e, half=half, r_=r_):
                    return g.dma_start(
                        stg[64 * half + 16 * r_:64 * half + 16 * (r_ + 1), 0:2048],
                        scdn_d[e, 8 + r_:9 + r_, :].to_broadcast((16, 2048)))
                p.dma("pool", f_scp, reads=("stg",), writes=("stg",), tag="lddnsp")
        for q2 in range(4):
            ic = 8 + q2
            for half, jj in ((0, 2 * q2), (1, 2 * q2 + 1)):
                lo, hi = 64 * half, 64 * (half + 1)
                for st2, wbuf, kw in ((0, wt, "wt"), (1, wtb, "wtb")):
                    fw = [(lambda buf, lo=lo: bass.AP(
                              buf, lo * 1536, [[1536, 64], [1, 1024]]),
                           (lambda ic=ic, lo=lo, st2=st2: bass.AP(
                               wdn, lo * 12 * 2048 + 2048 * ic + 1024 * st2,
                               [[12 * 2048, 64], [1, 1024]])),
                           "wdn")]
                    emit_dq_strip(p, wbuf, kw, 0, stg, "stg",
                                  1024 * st2, 4 * jj, 1024, fw, psub=(lo, hi))

    def expert_mms(e, half):
        first_write = (e == 0 and half == 0)
        for sp in range(n_span):
            gs = g1s[sp % 2]
            kg = f"g1s{sp % 2}"
            if g1ag:
                def f_g1(sy, sp=sp, gs=gs):
                    r = sp // 2
                    srcap = bass.AP(g1_d, 2048 * r * 1024 + 512 * (sp % 2),
                                    [[8 * 1024, 128], [1024, 8],
                                     [1024 * 1024, 2], [1, 512]])
                    dst = bass.AP(gs, 0, [[16 * 512, 128], [2 * 512, 8],
                                          [512, 2], [1, 512]])
                    return sy.dma_start(dst, srcap)
            else:
                def f_g1(sy, sp=sp, gs=gs):
                    srcap = bass.AP(g1_d, 512 * sp,
                                    [[8 * NTOK, 128], [NTOK, 8], [1024 * NTOK, 2], [1, 512]])
                    dst = bass.AP(gs, 0, [[16 * 512, 128], [2 * 512, 8], [512, 2], [1, 512]])
                    return sy.dma_start(dst, srcap)
            p.dma("sp", f_g1, reads=(kg,), writes=(kg,), tag=f"ldg1_{sp}")
            for il in range(6):
                psG, psU = psGs[il % 2], psUs[il % 2]
                kG, kU = f"psG{il % 2}", f"psU{il % 2}"
                for hc in range(16):
                    p.op("pe", lambda t, il=il, hc=hc, gs=gs, psG=psG: t.matmul(
                        psG[:, :], wgu[:, 1536 * hc + 128 * il:1536 * hc + 128 * (il + 1)],
                        gs[:, 512 * hc:512 * (hc + 1)],
                        start=(hc == 0), stop=(hc == 15)),
                        reads=("wgu", kg), writes=(kG,))
                sg = sgs[il % 2]
                ksg = f"sg{il % 2}"
                p.op("act", lambda s, psG=psG, sg=sg: s.activation(
                    sg[:, :], psG[:, :], AF.Silu),
                     reads=(kG,), writes=(ksg,))
                for hc in range(16):
                    p.op("pe", lambda t, il=il, hc=hc, gs=gs, psU=psU: t.matmul(
                        psU[:, :], wgu[:, 1536 * hc + 768 + 128 * il:
                                       1536 * hc + 768 + 128 * (il + 1)],
                        gs[:, 512 * hc:512 * (hc + 1)],
                        start=(hc == 0), stop=(hc == 15)),
                        reads=("wgu", kg), writes=(kU,))
                p.op("dve", lambda v, il=il, psU=psU, sg=sg: v.tensor_mul(
                    acts[:, 512 * il:512 * (il + 1)], sg[:, :], psU[:, :]),
                    reads=(ksg, kU), writes=("acts",))
            for t128 in range(4):
                for fb in range(4):
                    ps = psD1 if fb % 2 == 0 else psD2
                    kps = "psD1" if fb % 2 == 0 else "psD2"
                    for il in range(6):
                        icg = 6 * half + il
                        p.op("pe", lambda t, il=il, t128=t128, fb=fb, icg=icg,
                             ps=ps: t.matmul(
                            ps[:, :], acts[:, 512 * il + 128 * t128:
                                           512 * il + 128 * (t128 + 1)],
                            wdn[:, 2048 * icg + 512 * fb:2048 * icg + 512 * (fb + 1)],
                            start=(il == 0), stop=(il == 5)),
                            reads=("acts", "wdn"), writes=(kps,))
                    tg = sp * 4 + t128
                    p.op("dve", lambda v, fb=fb, tg=tg, e=e, ps=ps, kps=kps:
                         v.tensor_scalar_mul(
                        po[:, 512 * fb:512 * (fb + 1)], ps[:, :],
                        cbuf[:, 2 * tg + e:2 * tg + e + 1]),
                        reads=(kps, "cbuf"), writes=("po",))
                row0 = 512 * sp + 128 * t128
                if first_write:
                    p.dma("sp", lambda sy, row0=row0: sy.dma_start(
                        partial_d[row0:row0 + 128, :], po[:, :]),
                        reads=("po",), writes=("partial",), tag="stpo")
                else:
                    p.dma("pool", lambda g, row0=row0: g.dma_start(
                        partial_d[row0:row0 + 128, :], po[:, :], accum_op=OP.add),
                        reads=("po", "partial"), writes=("partial",), tag="stpoA")

    def load_half2():
        def f_h2(sy):
            srcap = bass.AP(wgu2_d, 0, [[1536, 128], [128 * 1536, 16], [1, 1536]])
            dst = bass.AP(wgu, 0, [[16 * 1536, 128], [1536, 16], [1, 1536]])
            return sy.dma_start(dst, srcap)
        p.dma("sp", f_h2, reads=("wgu2", "wgu"), writes=("wgu",), tag="ldh2")

    for e in range(2):
        dequant_gu(e)
        dequant_dn(e)
        expert_mms(e, 0)
        load_half2()
        expert_mms(e, 1)

    if not g1ag:
        for k8b in range(2):
            p.dma("pool", lambda g, k8b=k8b: g.dma_start(
                wt[:, 0:384], pksguc_d[128 * k8b:128 * (k8b + 1), :]),
                reads=("wt",), writes=("wt",), tag="ldswt")
            for r_ in range(8):
                def f_ssc(g, k8b=k8b, r_=r_):
                    return g.dma_start(
                        stg[16 * r_:16 * (r_ + 1), 0:384],
                        scsguc_d[8 * k8b + r_:8 * k8b + r_ + 1, :].to_broadcast((16, 384)))
                p.dma("pool", f_ssc, reads=("stg",), writes=("stg",), tag="ldssc")
            for j in range(8):
                ch = 2 * j + k8b
                fw = [(lambda buf: bass.AP(buf, 0, [[1536, 128], [1, 384]]),
                       lambda: bass.AP(shst, 0, [[1536, 128], [1, 384]]),
                       "shst")]
                emit_dq_strip(p, wt, "wt", 0, stg, "stg", 0, 4 * j, 384, fw)
                p.dma("act", lambda sy, ch=ch: sy.dma_start(
                    sgslo_d[128 * ch:128 * (ch + 1), :], shst[:, 0:384]),
                    reads=("shst",), writes=("sgslo",), tag="stsgu")
        p.dma("pool", lambda g: g.dma_start(wt[:, 0:256], pksdnc_d[0:128, :]),
              reads=("wt",), writes=("wt",), tag="ldsdw")
        for r_ in range(8):
            def f_sds(g, r_=r_):
                return g.dma_start(
                    stg[16 * r_:16 * (r_ + 1), 0:256],
                    scsdnc_d[r_:r_ + 1, :].to_broadcast((16, 256)))
            p.dma("pool", f_sds, reads=("stg",), writes=("stg",), tag="ldsds")
        for ic in range(8):
            fw = [(lambda buf: bass.AP(buf, 0, [[1536, 128], [1, 256]]),
                   lambda: bass.AP(shst, 0, [[1536, 128], [1, 256]]),
                   "shst")]
            emit_dq_strip(p, wt, "wt", 0, stg, "stg", 0, 4 * ic, 256, fw)
            p.dma("act", lambda sy, ic=ic: sy.dma_start(
                sdno_d[128 * ic:128 * (ic + 1), :], shst[:, 0:256]),
                reads=("shst",), writes=("sdno",), tag="stsdn")
        for half in range(2):
            def f_wp2(g, half=half):
                return g.dma_start(wt[64 * half:64 * (half + 1), 0:256],
                                   pksdnc_d[128:192, :])
            p.dma("pool", f_wp2, reads=("wt",), writes=("wt",), tag="ldsdwp")
        for half in range(2):
            for r_ in range(4):
                def f_sdsp(g, half=half, r_=r_):
                    return g.dma_start(
                        stg[64 * half + 16 * r_:64 * half + 16 * (r_ + 1), 0:256],
                        scsdnc_d[8 + r_:9 + r_, :].to_broadcast((16, 256)))
                p.dma("pool", f_sdsp, reads=("stg",), writes=("stg",), tag="ldsdsp")
        for q2 in range(4):
            ic = 8 + q2
            for half, jj in ((0, 2 * q2), (1, 2 * q2 + 1)):
                lo, hi = 64 * half, 64 * (half + 1)
                fw = [(lambda buf, lo=lo: bass.AP(buf, lo * 1536, [[1536, 64], [1, 256]]),
                       lambda lo=lo: bass.AP(shst, lo * 1536, [[1536, 64], [1, 256]]),
                       "shst")]
                emit_dq_strip(p, wt, "wt", 0, stg, "stg", 0, 4 * jj, 256, fw,
                              psub=(lo, hi))
            p.dma("act", lambda sy, ic=ic: sy.dma_start(
                sdno_d[128 * ic:128 * (ic + 1), :], shst[:, 0:256]),
                reads=("shst",), writes=("sdno",), tag="stsdnp")

    else:
        shstF = p.sbuf("shstF", [128, 3072], BF16)
        def sgu_w(strip):
            du = 1536 * strip
            TW = 1536
            return [
                (lambda buf: bass.AP(buf, 0, [[TW, 128], [8, 128], [1, 8]]),
                 (lambda du=du: bass.AP(shstF, du, [[3072, 128], [1, 128], [128, 8]])),
                 "shstF"),
                (lambda buf: bass.AP(buf, 1024, [[TW, 128], [8, 64], [2, 4], [1, 2]]),
                 (lambda du=du: bass.AP(shstF, du + 128 * 8,
                                        [[3072, 128], [1, 64], [128, 4], [64, 2]])),
                 "shstF"),
            ]
        for k8b in range(2):
            for r_ in range(8):
                def f_ssc(g, k8b=k8b, r_=r_):
                    return g.dma_start(
                        stg[16 * r_:16 * (r_ + 1), :],
                        scsgu_d[8 * k8b + r_:8 * k8b + r_ + 1, :]
                        .to_broadcast((16, 3072)))
                p.dma("pool", f_ssc, reads=("stg",), writes=("stg",), tag="ldssc")
            for strip, wbuf, kw in ((0, wt, "wt"), (1, wtb, "wtb")):
                p.dma("pool", lambda g, k8b=k8b, strip=strip, wbuf=wbuf:
                      g.dma_start(
                    wbuf[:, :], pksgu_d[128 * k8b:128 * (k8b + 1),
                                        1536 * strip:1536 * (strip + 1)]),
                    reads=(kw,), writes=(kw,), tag="ldswt")
            for j in range(8):
                ch = 2 * j + k8b
                for strip, wbuf, kw in ((0, wt, "wt"), (1, wtb, "wtb")):
                    emit_dq_strip(p, wbuf, kw, 0, stg, "stg", 1536 * strip,
                                  4 * j, 1536, sgu_w(strip))
                p.dma("act", lambda sy, ch=ch: sy.dma_start(
                    sgu_loc_d[128 * ch:128 * (ch + 1), :], shstF[:, :]),
                    reads=("shstF",), writes=("sgu_loc",), tag="stsgu")
        for r_ in range(8):
            def f_sdsF(g, r_=r_):
                return g.dma_start(
                    stg[16 * r_:16 * (r_ + 1), 0:2048],
                    scsdn_d[r_:r_ + 1, :].to_broadcast((16, 2048)))
            p.dma("pool", f_sdsF, reads=("stg",), writes=("stg",), tag="ldsds")
        p.dma("pool", lambda g: g.dma_start(wt[:, 0:1024], pksdn_d[0:128, 0:1024]),
              reads=("wt",), writes=("wt",), tag="ldsdw")
        p.dma("pool", lambda g: g.dma_start(wtb[:, 0:1024], pksdn_d[0:128, 1024:2048]),
              reads=("wtb",), writes=("wtb",), tag="ldsdw")
        for ic in range(8):
            for st2, wbuf, kw in ((0, wt, "wt"), (1, wtb, "wtb")):
                fw = [(lambda buf: bass.AP(buf, 0, [[1536, 128], [1, 1024]]),
                       (lambda st2=st2: bass.AP(shstF, 1024 * st2,
                                                [[3072, 128], [1, 1024]])),
                       "shstF")]
                emit_dq_strip(p, wbuf, kw, 0, stg, "stg", 1024 * st2,
                              4 * ic, 1024, fw)
            p.dma("act", lambda sy, ic=ic: sy.dma_start(
                sdn_loc_d[128 * ic:128 * (ic + 1), :], shstF[:, 0:2048]),
                reads=("shstF",), writes=("sdn_loc",), tag="stsdnF")
        for st2, wbuf, kw in ((0, wt, "wt"), (1, wtb, "wtb")):
            for half in range(2):
                def f_wpF(g, st2=st2, wbuf=wbuf, half=half):
                    return g.dma_start(
                        wbuf[64 * half:64 * (half + 1), 0:1024],
                        pksdn_d[128:192, 1024 * st2:1024 * (st2 + 1)])
                p.dma("pool", f_wpF, reads=(kw,), writes=(kw,), tag="ldsdwp")
        for half in range(2):
            for r_ in range(4):
                def f_sdspF(g, half=half, r_=r_):
                    return g.dma_start(
                        stg[64 * half + 16 * r_:64 * half + 16 * (r_ + 1), 0:2048],
                        scsdn_d[8 + r_:9 + r_, :].to_broadcast((16, 2048)))
                p.dma("pool", f_sdspF, reads=("stg",), writes=("stg",), tag="ldsdsp")
        for q2 in range(4):
            ic = 8 + q2
            for half, jj in ((0, 2 * q2), (1, 2 * q2 + 1)):
                lo, hi = 64 * half, 64 * (half + 1)
                for st2, wbuf, kw in ((0, wt, "wt"), (1, wtb, "wtb")):
                    fw = [(lambda buf, lo=lo: bass.AP(
                              buf, lo * 1536, [[1536, 64], [1, 1024]]),
                           (lambda lo=lo, st2=st2: bass.AP(
                               shstF, lo * 3072 + 1024 * st2,
                               [[3072, 64], [1, 1024]])),
                           "shstF")]
                    emit_dq_strip(p, wbuf, kw, 0, stg, "stg", 1024 * st2,
                                  4 * jj, 1024, fw, psub=(lo, hi))
            p.dma("act", lambda sy, ic=ic: sy.dma_start(
                sdn_loc_d[128 * ic:128 * (ic + 1), :], shstF[:, 0:2048]),
                reads=("shstF",), writes=("sdn_loc",), tag="stsdnF")

    if standalone:
        return p.finalize()




def build_launch_D(p=None, io=None):
    standalone = p is None
    if standalone:
        p = Prog("D")
        x1sl_d = p.dram("x1sl", [TOK_PER_CORE, DMODEL], F32, kind="ExternalInput")
        ln1sl_d = p.dram("ln1sl", [DMODEL, TOK_PER_CORE], BF16, kind="ExternalInput")
        parts_d = [p.dram(f"pp{k}", [TOK_PER_CORE, DMODEL], BF16, kind="ExternalInput")
                   for k in range(NCORE)]
        sgu_d = p.dram("sgu", [DMODEL, 3072], BF16, kind="ExternalInput")
        sdn_d = p.dram("sdn", [1536, DMODEL], BF16, kind="ExternalInput")
        x2_d = p.dram("x2", [TOK_PER_CORE, DMODEL], F32, kind="ExternalOutput")
        ln2T_d = p.dram("ln2T", [DMODEL, TOK_PER_CORE], BF16, kind="ExternalOutput")
        c = emit_consts(p)
        rs_d = None
    else:
        x1sl_d, ln1sl_d = io["x1"], io["ln1T"]
        rs_d = io["rs_out"]
        sgu_d, sdn_d = io["sgu_loc"], io["sdn_loc"]
        x2_d, ln2T_d = io["x2"], io["ln2T"]
        c = io["consts"]
    NT = TOK_PER_CORE // 128

    wsg = [p.sbuf(f"wsg{i}", [128, 16 * 256], BF16) for i in range(2)]
    sdns = [p.sbuf(f"sdns{i}", [128, 12 * 512], BF16) for i in range(2)]
    lg = p.sbuf("lgD", [128, 16 * 512], BF16)
    actD = p.sbuf("actD", [128, 12 * 512], BF16)
    sgD = p.sbuf("sgD", [128, 512], BF16)
    pt = [p.sbuf(f"pt{i}", [128, DMODEL], BF16) for i in range(2)]
    rsum = p.sbuf("rsum", [128, DMODEL], F32)
    xt = [p.sbuf(f"xtD{i}", [128, DMODEL], F32) for i in range(NT)]
    psG = p.psum("psGD", [128, 512])
    psU = p.psum("psUD", [128, 512])
    psD1 = p.psum("psDD1", [128, 512])
    psD2 = p.psum("psDD2", [128, 512])

    for sp in range(2):
        def f_lg(sy, sp=sp):
            srcap = bass.AP(ln1sl_d, 512 * sp,
                            [[8 * TOK_PER_CORE, 128], [TOK_PER_CORE, 8],
                             [1024 * TOK_PER_CORE, 2], [1, 512]])
            dst = bass.AP(lg, 0, [[16 * 512, 128], [2 * 512, 8], [512, 2], [1, 512]])
            return sy.dma_start(dst, srcap)
        p.dma("sp", f_lg, reads=("lgD",), writes=("lgD",), tag="ldlg")
        for ic in range(12):
            wb = wsg[ic % 2]
            kwb = f"wsg{ic % 2}"
            for gu in range(2):
                def f_w(sy, ic=ic, wb=wb, gu=gu):
                    srcap = bass.AP(sgu_d, 1536 * gu + 128 * ic,
                                    [[3072, 128], [128 * 3072, 16], [1, 128]])
                    dst = bass.AP(wb, 128 * gu,
                                  [[16 * 256, 128], [256, 16], [1, 128]])
                    return sy.dma_start(dst, srcap)
                p.dma("sp", f_w, reads=(kwb,), writes=(kwb,), tag="ldsgu")
            for hc in range(16):
                p.op("pe", lambda t, hc=hc, wb=wb: t.matmul(
                    psG[:, :], wb[:, 256 * hc:256 * hc + 128],
                    lg[:, 512 * hc:512 * (hc + 1)],
                    start=(hc == 0), stop=(hc == 15)),
                    reads=(kwb, "lgD"), writes=("psGD",))
            p.op("act", lambda s: s.activation(sgD[:, :], psG[:, :], AF.Silu),
                 reads=("psGD",), writes=("sgD",))
            for hc in range(16):
                p.op("pe", lambda t, hc=hc, wb=wb: t.matmul(
                    psU[:, :], wb[:, 256 * hc + 128:256 * (hc + 1)],
                    lg[:, 512 * hc:512 * (hc + 1)],
                    start=(hc == 0), stop=(hc == 15)),
                    reads=(kwb, "lgD"), writes=("psUD",))
            p.op("dve", lambda v, ic=ic: v.tensor_mul(
                actD[:, 512 * ic:512 * (ic + 1)], sgD[:, :], psU[:, :]),
                reads=("sgD", "psUD"), writes=("actD",))
        for fb in range(4):
            sb = sdns[fb % 2]
            ksb = f"sdns{fb % 2}"
            def f_sdn(sy, fb=fb, sb=sb):
                srcap = bass.AP(sdn_d, 512 * fb,
                                [[DMODEL, 128], [128 * DMODEL, 12], [1, 512]])
                dst = bass.AP(sb, 0, [[12 * 512, 128], [512, 12], [1, 512]])
                return sy.dma_start(dst, srcap)
            p.dma("sp", f_sdn, reads=(ksb,), writes=(ksb,), tag="ldsdn")
            for t128 in range(4):
                tt = 4 * sp + t128
                ps = psD1 if t128 % 2 == 0 else psD2
                kps = "psDD1" if t128 % 2 == 0 else "psDD2"
                for ic in range(12):
                    p.op("pe", lambda t, ic=ic, t128=t128, fb=fb, ps=ps, sb=sb:
                         t.matmul(
                        ps[:, :], actD[:, 512 * ic + 128 * t128:
                                       512 * ic + 128 * (t128 + 1)],
                        sb[:, 512 * ic:512 * (ic + 1)],
                        start=(ic == 0), stop=(ic == 11)),
                        reads=("actD", ksb), writes=(kps,))
                if fb == 0:
                    p.dma("sp", lambda sy, tt=tt: sy.dma_start(
                        xt[tt][:, :], x1sl_d[128 * tt:128 * (tt + 1), :]),
                        reads=(f"xtD{tt}",), writes=(f"xtD{tt}",), tag="ldx1")
                    if rs_d is not None:
                        pb = pt[0]
                        p.dma("act", lambda sy, tt=tt, pb=pb: sy.dma_start(
                            pb[:, :], rs_d[128 * tt:128 * (tt + 1), :]),
                            reads=("pt0",), writes=("pt0",), tag="ldrs")
                        p.op("dve", lambda v, pb=pb: v.tensor_copy(
                            rsum[:, :], pb[:, :]),
                            reads=("pt0",), writes=("rsum",))
                    else:
                        for k in range(NCORE):
                            pb = pt[k % 2]
                            kpb = f"pt{k % 2}"
                            p.dma("act", lambda sy, k=k, tt=tt, pb=pb: sy.dma_start(
                                pb[:, :], parts_d[k][128 * tt:128 * (tt + 1), :]),
                                reads=(kpb,), writes=(kpb,), tag="ldpt")
                            if k == 0:
                                p.op("dve", lambda v, pb=pb: v.tensor_copy(
                                    rsum[:, :], pb[:, :]),
                                    reads=(kpb,), writes=("rsum",))
                            else:
                                p.op("dve", lambda v, pb=pb: v.tensor_add(
                                    rsum[:, :], rsum[:, :], pb[:, :]),
                                    reads=("rsum", kpb), writes=("rsum",))
                    p.op("dve", lambda v, tt=tt: v.tensor_add(
                        xt[tt][:, :], xt[tt][:, :], rsum[:, :]),
                        reads=(f"xtD{tt}", "rsum"), writes=(f"xtD{tt}",))
                p.op("dve", lambda v, tt=tt, fb=fb, ps=ps: v.tensor_add(
                    xt[tt][:, 512 * fb:512 * (fb + 1)],
                    xt[tt][:, 512 * fb:512 * (fb + 1)], ps[:, :]),
                    reads=(f"xtD{tt}", kps), writes=(f"xtD{tt}",))
        for t128 in range(4):
            tt = 4 * sp + t128
            p.dma("sp", lambda sy, tt=tt: sy.dma_start(
                x2_d[128 * tt:128 * (tt + 1), :], xt[tt][:, :]),
                reads=(f"xtD{tt}",), writes=("x2o",), tag="stx2")

    emit_ln_from_sbuf(p, c, lambda tt: (xt[tt], f"xtD{tt}"), NT, ln2T_d, 0, "ln2")
    if standalone:
        return p.finalize()


def gu_colperm():
    perm = np.zeros(1536, np.int64)
    for ic in range(8):
        perm[128 * ic:128 * (ic + 1)] = 8 * np.arange(128) + ic
    for q in range(4):
        for r in range(2):
            j = 2 * q + r
            perm[128 * (8 + q) + 64 * r:128 * (8 + q) + 64 * (r + 1)] = \
                1024 + 8 * np.arange(64) + j
    return perm


def dn_rowperm():
    perm = np.zeros(1536, np.int64)
    for ch in range(8):
        perm[128 * ch:128 * (ch + 1)] = 8 * np.arange(128) + ch
    for q in range(4):
        for r in range(2):
            j = 2 * q + r
            perm[128 * (8 + q) + 64 * r:128 * (8 + q) + 64 * (r + 1)] = \
                8 * (128 + np.arange(64)) + j
    return perm


def run_launch_D(x1, G1, parts, SGU, SDN):
    cp = gu_colperm()
    colperm = np.concatenate([cp, 1536 + cp])
    SGUp = _np(SGU[:, colperm])
    nc = _get_nc('D', build_launch_D)
    in_maps = []
    for cidx in range(NCORE):
        t0 = 1024 * cidx
        m = {
            "x1sl": _np(x1[t0:t0 + 1024]),
            "ln1sl": _np(G1[:, t0:t0 + 1024]),
            "sgu": SGUp, "sdn": SDN,
        }
        for k in range(NCORE):
            m[f"pp{k}"] = _np(parts[k][t0:t0 + 1024])
        in_maps.append(m)
    res = run_bass_kernel_spmd(nc, in_maps, core_ids=list(range(NCORE)))
    x2 = np.concatenate([res.results[i]["x2"] for i in range(NCORE)], axis=0)
    G2 = np.concatenate([res.results[i]["ln2T"] for i in range(NCORE)], axis=1)
    return x2, G2



D1 = 64
NQH = 8
ROPE_THETA = 10000.0


def build_launch_E(debug=False, p=None, io=None):
    standalone = p is None
    if standalone:
        p = Prog("E")
        g2_d = p.dram("g2", [DMODEL, S], BF16, kind="ExternalInput")
        wq_d = p.dram("wq", [DMODEL, 512], F32, kind="ExternalInput")
        wk_d = p.dram("wk", [DMODEL, 128], F32, kind="ExternalInput")
        wv_d = p.dram("wv", [DMODEL, 128], F32, kind="ExternalInput")
        wb_d = p.dram("wb", [DMODEL, 8], F32, kind="ExternalInput")
        bb_d = p.dram("bb", [8, 1], F32, kind="ExternalInput")
        qnw_d = p.dram("qnw", [64, 1], F32, kind="ExternalInput")
        knw_d = p.dram("knw", [64, 1], F32, kind="ExternalInput")
        oT2_d = p.dram("oT2", [512, S], BF16, kind="ExternalOutput")
        beta_d = p.dram("betad", [8, S], F32)
        g2ag = False
    else:
        g2_d = io["g2"]
        wq_d, wk_d, wv_d, wb_d = io["wq"], io["wk"], io["wv"], io["wb"]
        bb_d, qnw_d, knw_d = io["bb"], io["qnw"], io["knw"]
        oT2_d = io["oT2"]
        beta_d = io["betad"]
        g2ag = True
    if debug:
        dbg = {}
        for nm, shape, dt in (("q0", [128, S], BF16), ("k0", [128, S], BF16),
                              ("v0", [128, S], BF16), ("beta", [8, S], F32),
                              ("eT0", [128, 512], BF16), ("rows", [1, 3072], F32),
                              ("u01", [128, 128], BF16), ("cos", [128, S], BF16),
                              ("sin", [128, S], BF16), ("perm", [128, 128], BF16),
                              ("negM", [128, 1], F32)):
            dbg[nm] = p.dram("dbg_" + nm, shape, dt, kind="ExternalOutput")

    NSP = S // 512

    wqs = p.sbuf("wqs", [128, 16 * 512], BF16)
    wks = p.sbuf("wks", [128, 16 * 128], BF16)
    wvs = p.sbuf("wvs", [128, 16 * 128], BF16)
    wbs = p.sbuf("wbs", [128, 16 * 8], BF16)
    g2s_single = p.sbuf("g2sE", [128, 16 * 512], BF16)
    g2s = [g2s_single, g2s_single]
    qsb = [p.sbuf(f"qsbE{i}", [128, S], BF16) for i in range(4)]
    ksb = p.sbuf("ksbE", [128, S], BF16)
    kswap = p.sbuf("kswapE", [128, S], BF16)
    perm64 = p.sbuf("perm64E", [128, 128], BF16)
    vsb = p.sbuf("vsbE", [128, S], BF16)
    eT = p.sbuf("eTE", [128, 32 * 512], BF16)
    u_sb = p.sbuf("u_sbE", [128, 32 * 64], BF16)
    oth = p.sbuf("othE", [128, S], BF16)
    cosb = p.sbuf("cosb", [128, S], BF16)
    sinb = p.sbuf("sinb", [128, S], BF16)
    permt = p.sbuf("permt", [128, 128], BF16)
    bo2 = p.sbuf("bo2", [128, 2], BF16)
    bo2t = p.sbuf("bo2t", [2, 128], F32)
    masks = p.sbuf("masksE", [128, 8 * 512], BF16)
    rows = p.sbuf("rowsE", [1, 6 * 512], F32)
    rstdr = p.sbuf("rstdr", [2, 512], F32)
    sqs = p.sbuf("sqsE", [128, 512], BF16)
    qe = p.sbuf("qeE", [128, 512], BF16)
    rt1 = p.sbuf("rt1", [128, 512], BF16)
    rt2 = p.sbuf("rt2", [128, 512], BF16)
    qnw2 = p.sbuf("qnw2", [128, 1], F32)
    knw2 = p.sbuf("knw2", [128, 1], F32)
    bbc = p.sbuf("bbc", [8, 1], F32)
    scal = p.sbuf("scalE", [1, 8], F32)
    negM2b = p.sbuf("negM2b", [128, 1], F32)
    brdcol = p.sbuf("brdcol", [128, 1], F32)
    rccol = p.sbuf("rccol", [128, 1], F32)
    dcol = p.sbuf("dcol", [128, 1], F32)
    NTs = p.sbuf("NTs", [128, 128], BF16)
    Ns = p.sbuf("Ns", [128, 128], BF16)
    Pj = [p.sbuf(f"PjE{i}", [128, 128], BF16) for i in range(2)]
    PTj = [p.sbuf(f"PTjE{i}", [128, 128], BF16) for i in range(2)]
    ucur = p.sbuf("ucur", [128, 64], BF16)
    edg = p.sbuf("edgE", [128, 512], BF16)
    edg2 = p.sbuf("edgE2", [128, 512], BF16)
    t1s = p.sbuf("t1sE", [128, 64], F32)
    brow = p.sbuf("browE", [1, 512], F32)
    iotaf = p.sbuf("iotaf", [128, 1024], F32)
    rr1 = p.sbuf("rr1E", [128, 1024], F32)
    angm = rr1
    invf = p.sbuf("invfE", [128, 1], F32)
    srow = p.sbuf("srowE", [128, 1], F32)
    swpc = p.sbuf("swpc", [128, 1], F32)
    colix = p.sbuf("colix", [128, 128], F32)
    pcol = p.sbuf("pcolE", [128, 1], F32)
    ident_e = p.sbuf("identE", [128, 128], BF16)
    betast = p.sbuf("betast", [8, 512], F32)
    oIb = p.sbuf("oIbE", [128, 64], BF16)

    pb = [p.psum(f"pbE{i}", [128, 512]) for i in range(7)]
    pbT = p.psum("pbTE", [128, 512], BF16)

    c = {"ones_col": p.sbuf("c_ones_col", [128, 1], BF16),
         "ones_row_f": p.sbuf("c_ones_row_f", [1, 128], F32),
         "ones_row": p.sbuf("c_ones_row", [1, 128], BF16)}
    p.op("pool", lambda g: g.memset(c["ones_col"][:, :], 1.0), writes=("c_ones",))
    p.op("pool", lambda g: g.memset(c["ones_row_f"][:, :], 1.0), writes=("c_ones",))
    p.op("pool", lambda g: g.memset(c["ones_row"][:, :], 1.0), writes=("c_ones",))

    p.op("pool", lambda g: g.memset(masks[:, :], 1.0), writes=("mraw",))
    for k in range(4):
        p.op("pool", lambda g, k=k: g.affine_select(
            masks[:, 512 * k:512 * (k + 1)], masks[:, 512 * k:512 * (k + 1)],
            [[1, 512]], OP.is_gt, 0.0, base=-128 * k, channel_multiplier=-1),
            reads=("mraw",), writes=("mstrict",), tag="mkstrict")
    for k in range(4):
        def f_md(g, k=k):
            return g.affine_select(
                masks[:, 512 * (4 + k):512 * (5 + k)],
                masks[:, 512 * (4 + k):512 * (5 + k)],
                [[1, 512]], OP.is_equal, 0.0, base=-128 * k,
                channel_multiplier=-1)
        p.op("pool", f_md, reads=("mraw",), writes=("mdiag",), tag="mkdiag")
    p.op("pool", lambda g: g.memset(bo2[:, :], 0.0), writes=("bo2r",))
    p.op("pool", lambda g: g.memset(bo2[0:64, 0:1], 1.0), reads=("bo2r",), writes=("bo2",))
    p.op("pool", lambda g: g.memset(bo2[64:128, 1:2], 1.0), reads=("bo2r",), writes=("bo2",))
    bo2i = p.sbuf("bo2iE", [2, 128], F32)
    p.op("pool", lambda g: g.iota(bo2i[:, :], [[1, 2], [0, 64]],
                                  channel_multiplier=0,
                                  allow_small_or_imprecise_dtypes=True),
         writes=("bo2i",))
    p.op("pool", lambda g: g.iota(bo2t[:, :], [[0, 128]], channel_multiplier=1,
                                  allow_small_or_imprecise_dtypes=True),
         writes=("bo2tr",))
    p.op("dve", lambda v: v.tensor_tensor(bo2t[:, :], bo2i[:, :], bo2t[:, :],
                                          OP.is_equal),
         reads=("bo2i", "bo2tr"), writes=("bo2t",))
    p.op("pool", lambda g: g.iota(swpc[:, :], [[0, 1]], channel_multiplier=1, allow_small_or_imprecise_dtypes=True),
         writes=("swp0",))
    ge64t = p.sbuf("ge64E", [128, 1], F32)
    p.op("dve", lambda v: v.tensor_scalar(ge64t[:, :], swpc[:, :], 64.0, -64.0,
                                          op0=OP.is_ge, op1=OP.mult),
         reads=("swp0",), writes=("ge64",))
    p.op("dve", lambda v: v.tensor_add(srow[:, :], swpc[:, :], ge64t[:, :]),
         reads=("swp0", "ge64"), writes=("pm64",))
    p.op("dve", lambda v: v.tensor_scalar(srow[:, :], srow[:, :], 32.0, None,
                                          op0=OP.is_ge),
         reads=("pm64",), writes=("srm",))
    p.op("dve", lambda v: v.scalar_tensor_tensor(
        swpc[:, :], srow[:, :], -64.0, swpc[:, :], op0=OP.mult, op1=OP.add),
        reads=("srm", "swp0"), writes=("swp1",))
    p.op("dve", lambda v: v.tensor_scalar_add(swpc[:, :], swpc[:, :], 32.0),
         reads=("swp1",), writes=("swpc",))
    p.op("pool", lambda g: g.iota(colix[:, :], [[1, 128]], channel_multiplier=0, allow_small_or_imprecise_dtypes=True),
         writes=("colix",))
    p.op("dve", lambda v: v.tensor_single_scalar(permt[:, :], colix[:, :],
                                                 swpc[:, :], OP.is_equal),
         reads=("colix", "swpc"), writes=("permt",))
    p.op("pool", lambda g: g.iota(pcol[:, :], [[0, 1]], channel_multiplier=1, allow_small_or_imprecise_dtypes=True),
         writes=("pcol0",))
    p.op("dve", lambda v: v.tensor_single_scalar(ident_e[:, :], colix[:, :],
                                                 pcol[:, :], OP.is_equal),
         reads=("colix", "pcol0"), writes=("identE",))
    sw64 = p.sbuf("sw64E", [128, 1], F32)
    p.op("dve", lambda v: v.tensor_scalar(sw64[:, :], pcol[:, :], 64.0, -128.0,
                                          op0=OP.is_ge, op1=OP.mult),
         reads=("pcol0",), writes=("sw64a",))
    p.op("dve", lambda v: v.tensor_add(sw64[:, :], sw64[:, :], pcol[:, :]),
         reads=("sw64a", "pcol0"), writes=("sw64b",))
    p.op("dve", lambda v: v.tensor_scalar_add(sw64[:, :], sw64[:, :], 64.0),
         reads=("sw64b",), writes=("sw64",))
    p.op("dve", lambda v: v.tensor_single_scalar(perm64[:, :], colix[:, :],
                                                 sw64[:, :], OP.is_equal),
         reads=("colix", "sw64"), writes=("perm64",))
    p.op("dve", lambda v: v.tensor_scalar(srow[:, :], srow[:, :], 2.0, -1.0,
                                          op0=OP.mult, op1=OP.add),
         reads=("srm",), writes=("srow",))
    p.op("pool", lambda g: g.iota(invf[:, :], [[0, 1]], channel_multiplier=1, allow_small_or_imprecise_dtypes=True),
         writes=("invf0",))
    p.op("dve", lambda v: v.tensor_scalar(rr1[:, 0:1], invf[:, :], 32.0, None,
                                          op0=OP.is_ge),
         reads=("invf0", "rr1b"), writes=("rr1b",))
    p.op("dve", lambda v: v.tensor_scalar(rr1[:, 1:2], invf[:, :], 64.0, None,
                                          op0=OP.is_ge),
         reads=("invf0", "rr1b"), writes=("rr1b",))
    p.op("dve", lambda v: v.tensor_add(rr1[:, 0:1], rr1[:, 0:1], rr1[:, 1:2]),
         reads=("rr1b",), writes=("rr1b",))
    p.op("dve", lambda v: v.tensor_scalar(rr1[:, 1:2], invf[:, :], 96.0, None,
                                          op0=OP.is_ge),
         reads=("invf0", "rr1b"), writes=("rr1b",))
    p.op("dve", lambda v: v.tensor_add(rr1[:, 0:1], rr1[:, 0:1], rr1[:, 1:2]),
         reads=("rr1b",), writes=("rr1b",))
    p.op("dve", lambda v: v.scalar_tensor_tensor(
        invf[:, :], rr1[:, 0:1], -32.0, invf[:, :], op0=OP.mult, op1=OP.add),
        reads=("rr1b", "invf0"), writes=("invf1",))
    p.op("act", lambda s: s.activation(invf[:, :], invf[:, :], AF.Exp,
                                       scale=-(2.0 / 64.0) * float(np.log(ROPE_THETA))),
         reads=("invf1",), writes=("invf",))
    TWO_PI = float(2.0 * np.pi)
    MAGIC = 12582912.0
    halfpi_c, halfpi_k = p.const_col(float(np.pi / 2.0))
    for blk in range(4):
        bs = slice(1024 * blk, 1024 * (blk + 1))
        p.op("pool", lambda g, blk=blk: g.iota(
            iotaf[:, :], [[1, 1024]], base=1024 * blk, channel_multiplier=0,
            allow_small_or_imprecise_dtypes=True),
            reads=("iotafb",), writes=("iotafb",))
        p.op("dve", lambda v: v.tensor_scalar_mul(iotaf[:, :], iotaf[:, :],
                                                  invf[:, :]),
             reads=("iotafb", "invf"), writes=("iotafb",))
        for trig, outb, outk in (("sin", sinb, "sinraw"), ("cos", cosb, "cosb")):
            extra = 0.25 if trig == "cos" else 0.0
            p.op("dve", lambda v, extra=extra: v.tensor_scalar(
                rr1[:, :], iotaf[:, :], 1.0 / TWO_PI, MAGIC + extra,
                op0=OP.mult, op1=OP.add),
                reads=("iotafb", "rr1b"), writes=("rr1b",))
            p.op("dve", lambda v: v.tensor_scalar_add(rr1[:, :], rr1[:, :], -MAGIC),
                 reads=("rr1b",), writes=("rr1b",))
            p.op("dve", lambda v: v.scalar_tensor_tensor(
                rr1[:, :], rr1[:, :], -TWO_PI, iotaf[:, :], op0=OP.mult, op1=OP.add),
                reads=("rr1b", "iotafb"), writes=("rr1b",))
            if trig == "sin":
                p.op("act", lambda s, outb=outb, bs=bs: s.activation(
                    outb[:, bs], angm[:, :], AF.Sin),
                    reads=("rr1b",), writes=(outk,))
            else:
                p.op("act", lambda s, outb=outb, bs=bs: s.activation(
                    outb[:, bs], angm[:, :], AF.Sin, bias=halfpi_c[:, :]),
                    reads=("rr1b", halfpi_k), writes=(outk,))
    p.op("dve", lambda v: v.tensor_scalar_mul(sinb[:, :], sinb[:, :], srow[:, :]),
         reads=("sinraw", "srow"), writes=("sinb",))

    p.dma("sp", lambda sy: sy.dma_start(bbc[:, :], bb_d[:, :]), writes=("bbc",))
    p.dma("sp", lambda sy: sy.dma_start(qnw2[0:64, :], qnw_d[:, :]), writes=("qnw2",))
    p.dma("sp", lambda sy: sy.dma_start(qnw2[64:128, :], qnw_d[:, :]),
          reads=("qnw2",), writes=("qnw2",))
    p.dma("sp", lambda sy: sy.dma_start(knw2[0:64, :], knw_d[:, :]), writes=("knw2",))
    p.dma("sp", lambda sy: sy.dma_start(knw2[64:128, :], knw_d[:, :]),
          reads=("knw2",), writes=("knw2",))
    qnwrow = p.sbuf("qnwrow", [1, 64], F32)
    knwrow = p.sbuf("knwrow", [1, 64], F32)
    p.dma("sp", lambda sy: sy.dma_start(
        qnwrow[:, :], bass.AP(qnw_d, 0, [[64, 1], [1, 64]])), writes=("qnwrow",))
    p.dma("sp", lambda sy: sy.dma_start(
        knwrow[:, :], bass.AP(knw_d, 0, [[64, 1], [1, 64]])), writes=("knwrow",))
    p.op("dve", lambda v: v.reduce_max(scal[:, 0:1], qnwrow[:, :], axis=AX.X,
                                       apply_absolute_value=True),
         reads=("qnwrow",), writes=("scal",))
    p.op("dve", lambda v: v.reduce_max(scal[:, 1:2], knwrow[:, :], axis=AX.X,
                                       apply_absolute_value=True),
         reads=("scal", "knwrow"), writes=("scal",))
    p.op("dve", lambda v: v.tensor_mul(scal[:, 2:3], scal[:, 0:1], scal[:, 1:2]),
         reads=("scal",), writes=("scal",))
    p.op("dve", lambda v: v.tensor_scalar_mul(scal[:, 3:4], scal[:, 2:3], -8.0),
         reads=("scal",), writes=("scal",))
    p.op("pe", lambda t: t.matmul(pb[5][:, 0:1], c["ones_row_f"][:, :],
                                  scal[:, 3:4], start=True, stop=True),
         reads=("scal", "c_ones"), writes=("pb5",))
    p.op("dve", lambda v: v.tensor_copy(negM2b[:, :], pb[5][:, 0:1]),
         reads=("pb5",), writes=("negM2b",))

    for (wd, ws, key, wth) in ((wq_d, wqs, "wqs", 512), (wk_d, wks, "wks", 128),
                               (wv_d, wvs, "wvs", 128), (wb_d, wbs, "wbs", 8)):
        def f_ldw(g, wd=wd, ws=ws, wth=wth):
            srcap = bass.AP(wd, 0, [[wth, 128], [128 * wth, 16], [1, wth]])
            dst = bass.AP(ws, 0, [[16 * wth, 128], [wth, 16], [1, wth]])
            return g.dma_start(dst, srcap)
        p.dma("pool", f_ldw, writes=(key,), tag="ldw")

    for sp in range(NSP):
        gs = g2s[sp % 2]
        kg = "g2sE"
        if g2ag:
            def f_g2(sy, sp=sp, gs=gs):
                srcap = bass.AP(g2_d, 2048 * (sp // 2) * 1024 + 512 * (sp % 2),
                                [[1024, 128], [128 * 1024, 16], [1, 512]])
                dst = bass.AP(gs, 0, [[16 * 512, 128], [512, 16], [1, 512]])
                return sy.dma_start(dst, srcap)
        else:
            def f_g2(sy, sp=sp, gs=gs):
                srcap = bass.AP(g2_d, 512 * sp, [[S, 128], [128 * S, 16], [1, 512]])
                dst = bass.AP(gs, 0, [[16 * 512, 128], [512, 16], [1, 512]])
                return sy.dma_start(dst, srcap)
        p.dma("sp", f_g2, reads=(kg,), writes=(kg,), tag=f"ldg2_{sp}")
        for hc in range(16):
            p.op("pe", lambda t, hc=hc, gs=gs: t.matmul(
                pb[5][0:8, :], wbs[:, 8 * hc:8 * (hc + 1)],
                gs[:, 512 * hc:512 * (hc + 1)], start=(hc == 0), stop=(hc == 15)),
                reads=("wbs", kg), writes=("pb5",))
        p.op("dve", lambda v: v.tensor_scalar(
            betast[0:8, :], pb[5][0:8, :], bbc[0:8, :], None, op0=OP.add),
            reads=("pb5", "bbc"), writes=("betarow",))
        p.dma("act", lambda sy, sp=sp: sy.dma_start(
            beta_d[:, 512 * sp:512 * (sp + 1)], betast[0:8, :]),
            reads=("betarow",), writes=("betad",), tag="stbeta")
        for tc in range(4):
            for hc in range(16):
                p.op("pe", lambda t, tc=tc, hc=hc, gs=gs: t.matmul(
                    pb[6][:, 0:128],
                    gs[:, 512 * hc + 128 * tc:512 * hc + 128 * (tc + 1)],
                    wvs[:, 128 * hc:128 * (hc + 1)],
                    start=(hc == 0), stop=(hc == 15)),
                    reads=("wvs", kg), writes=("pb6",))
            p.op("dve", lambda v, sp=sp, tc=tc: v.tensor_copy(
                vsb[:, 512 * sp + 128 * tc:512 * sp + 128 * (tc + 1)],
                pb[6][:, 0:128]),
                reads=("pb6",), writes=("vsbE",))
        for pr in range(5):
            wsrc, wkey, col0 = (wqs, "wqs", 128 * pr) if pr < 4 else (wks, "wks", 0)
            nw2, nwk = (qnw2, "qnw2") if pr < 4 else (knw2, "knw2")
            dstb, dkey = (qsb[pr], f"qsbE{pr}") if pr < 4 else (ksb, "ksbE")
            wpitch = 512 if pr < 4 else 128
            psq = pb[pr % 4]
            kpsq = f"pb{pr % 4}"
            for hc in range(16):
                p.op("pe", lambda t, hc=hc, gs=gs, wsrc=wsrc, col0=col0,
                     wpitch=wpitch, psq=psq: t.matmul(
                    psq[:, :], wsrc[:, wpitch * hc + col0:wpitch * hc + col0 + 128],
                    gs[:, 512 * hc:512 * (hc + 1)],
                    start=(hc == 0), stop=(hc == 15)),
                    reads=(wkey, kg), writes=(kpsq,))
            p.op("act", lambda s, psq=psq: s.activation(sqs[:, :], psq[:, :],
                                                        AF.Square),
                 reads=(kpsq,), writes=("sqsE",))
            p.op("pe", lambda t: t.matmul(pb[4][0:2, :], bo2[:, :], sqs[:, :],
                                          start=True, stop=True),
                 reads=("sqsE", "bo2"), writes=("pb4",))
            eps6c, eps6k = p.const_col(1e-6)
            p.op("act", lambda s, eps6c=eps6c: s.activation(
                rstdr[:, :], pb[4][0:2, :], AF.Sqrt, bias=eps6c[0:2, :],
                scale=1.0 / 64.0),
                reads=("pb4", eps6k), writes=("rstdA",))
            p.op("dve", lambda v: v.reciprocal(rstdr[:, :], rstdr[:, :]),
                 reads=("rstdA",), writes=("rstdr",))
            p.op("pe", lambda t: t.matmul(pb[4][:, :], bo2t[:, :], rstdr[:, :],
                                          start=True, stop=True),
                 reads=("rstdr", "bo2t"), writes=("pb4",))
            p.op("dve", lambda v, psq=psq: v.tensor_copy(qe[:, :], psq[:, :]),
                 reads=(kpsq,), writes=("qeE",))
            p.op("dve", lambda v, nw2=nw2: v.scalar_tensor_tensor(
                qe[:, :], qe[:, :], nw2[:, :], pb[4][:, :],
                op0=OP.mult, op1=OP.mult),
                reads=("qeE", nwk, "pb4"), writes=("qeE",))
            p.op("pe", lambda t: t.matmul(pb[6][:, :], permt[:, :], qe[:, :],
                                          start=True, stop=True),
                 reads=("permt", "qeE"), writes=("pb6",))
            p.op("dve", lambda v, sp=sp: v.tensor_mul(
                rt1[:, :], qe[:, :], cosb[:, 512 * sp:512 * (sp + 1)]),
                reads=("qeE", "cosb"), writes=("rt1",))
            p.op("dve", lambda v, sp=sp: v.tensor_mul(
                rt2[:, :], pb[6][:, :], sinb[:, 512 * sp:512 * (sp + 1)]),
                reads=("pb6", "sinb"), writes=("rt2",))
            p.op("dve", lambda v, sp=sp, dstb=dstb: v.tensor_add(
                dstb[:, 512 * sp:512 * (sp + 1)], rt1[:, :], rt2[:, :]),
                reads=("rt1", "rt2"), writes=(dkey,))

    for sp2 in range(NSP):
        p.op("pe", lambda t, sp2=sp2: t.matmul(
            pb[0][:, :], perm64[:, :], ksb[:, 512 * sp2:512 * (sp2 + 1)],
            start=True, stop=True),
            reads=("perm64", "ksbE"), writes=("pb0",))
        p.op("dve", lambda v, sp2=sp2: v.tensor_copy(
            kswap[:, 512 * sp2:512 * (sp2 + 1)], pb[0][:, :]),
            reads=("pb0",), writes=("kswapE",))

    R_DS, R_DG, R_DC, R_RC, R_BRD, R_DRW = 0, 512, 1024, 1536, 2048, 2560
    for h in range(NQH):
        pr = h // 2
        qb = 64 * (h % 2)
        kvb = 64 * ((h // 4) % 2)
        kvo = kvb
        ksrc, kskey = (ksb, "ksbE") if qb == kvb else (kswap, "kswapE")
        qkey = f"qsbE{pr}"
        for sp in range(NSP):
            nch = 4 * (sp + 1)
            p.dma("sp", lambda sy, h=h, sp=sp: sy.dma_start(
                brow[:, :], beta_d[h:h + 1, 512 * sp:512 * (sp + 1)]),
                reads=("betad", "browE"), writes=("browE",), tag="ldbrow")
            pend = []
            for ch in range(nch):
                pss = pb[ch % 2]
                kpss = f"pb{ch % 2}"
                ket = f"eTE{ch % 2}"
                p.op("pe", lambda t, ch=ch, sp=sp, pss=pss, qb=qb,
                     pr=pr, ksrc=ksrc: t.matmul(
                    pss[:, :], ksrc[qb:qb + 64, 128 * ch:128 * (ch + 1)],
                    qsb[pr][qb:qb + 64, 512 * sp:512 * (sp + 1)],
                    start=True, stop=True),
                    reads=(kskey, qkey), writes=(kpss,))
                p.op("act", lambda s, ch=ch, pss=pss: s.activation(
                    eT[:, 512 * ch:512 * (ch + 1)], pss[:, :], AF.Exp,
                    bias=negM2b[:, :], scale=0.125),
                    reads=(kpss, "negM2b"), writes=(ket,))
                off = 128 * ch - 512 * sp
                newpend = []
                if off >= 0:
                    k = off // 128
                    ebuf = edg if k % 2 == 0 else edg2
                    kedg = f"edgE{k % 2}"
                    p.op("dve", lambda v, ch=ch, k=k, ebuf=ebuf: v.tensor_mul(
                        ebuf[:, :], eT[:, 512 * ch:512 * (ch + 1)],
                        masks[:, 512 * (4 + k):512 * (5 + k)]),
                        reads=(ket, "mdiag"), writes=(kedg,))
                    def f_diag(k=k, ebuf=ebuf, kedg=kedg):
                        p.op("pe", lambda t, k=k, ebuf=ebuf: t.matmul(
                            pb[3][0:1, :], c["ones_col"][:, :], ebuf[:, :],
                            start=(k == 0), stop=(k == 3)),
                            reads=(kedg, "c_ones"), writes=("pb3",))
                    newpend.append(f_diag)
                    p.op("dve", lambda v, ch=ch, k=k: v.tensor_mul(
                        eT[:, 512 * ch:512 * (ch + 1)],
                        eT[:, 512 * ch:512 * (ch + 1)],
                        masks[:, 512 * k:512 * (k + 1)]),
                        reads=(ket, "mstrict"), writes=(ket,))
                def f_rsum(ch=ch, nch=nch, ket=ket):
                    p.op("pe", lambda t, ch=ch, nch=nch: t.matmul(
                        pb[2][0:1, :], c["ones_col"][:, :],
                        eT[:, 512 * ch:512 * (ch + 1)],
                        start=(ch == 0), stop=(ch == nch - 1)),
                        reads=(ket, "c_ones"), writes=("pb2",))
                newpend.append(f_rsum)
                for f in pend:
                    f()
                pend = newpend
            for f in pend:
                f()
            p.op("dve", lambda v: v.tensor_copy(rows[:, R_DS:R_DS + 512],
                                                pb[2][0:1, :]),
                 reads=("pb2",), writes=("rDS",))
            p.op("dve", lambda v: v.tensor_copy(rows[:, R_DG:R_DG + 512],
                                                pb[3][0:1, :]),
                 reads=("pb3",), writes=("rDG",))
            p.op("dve", lambda v: v.tensor_add(
                rows[:, R_DC:R_DC + 512], rows[:, R_DS:R_DS + 512],
                rows[:, R_DG:R_DG + 512]),
                reads=("rDS", "rDG"), writes=("rDC",))
            p.op("dve", lambda v: v.reciprocal(rows[:, R_RC:R_RC + 512],
                                               rows[:, R_DC:R_DC + 512]),
                 reads=("rDC",), writes=("rRC",))
            p.op("dve", lambda v: v.tensor_scalar(
                rows[:, R_DS:R_DS + 512], rows[:, R_DS:R_DS + 512], 1e-20, None,
                op0=OP.max),
                reads=("rDS",), writes=("rDS",))
            p.op("dve", lambda v: v.reciprocal(rows[:, R_BRD:R_BRD + 512],
                                               rows[:, R_DS:R_DS + 512]),
                 reads=("rDS",), writes=("rBRD",))
            p.op("dve", lambda v: v.tensor_mul(
                rows[:, R_BRD:R_BRD + 512], rows[:, R_BRD:R_BRD + 512],
                brow[:, :]),
                reads=("rBRD", "browE"), writes=("rBRD",))
            p.op("dve", lambda v: v.tensor_scalar_mul(
                rows[:, R_BRD:R_BRD + 512], rows[:, R_BRD:R_BRD + 512], -1.0),
                reads=("rBRD",), writes=("rBRD",))
            p.op("dve", lambda v: v.tensor_mul(
                rows[:, R_DRW:R_DRW + 512], rows[:, R_DG:R_DG + 512],
                rows[:, R_RC:R_RC + 512]),
                reads=("rDG", "rRC"), writes=("rDRW",))
            for iq in range(4):
                I = 4 * sp + iq
                for (roff, dst, dk, bank) in ((R_BRD, brdcol, "brdcol", 5),
                                              (R_RC, rccol, "rccol", 6),
                                              (R_DRW, dcol, "dcol", 0)):
                    p.op("pe", lambda t, roff=roff, iq=iq, bank=bank: t.matmul(
                        pb[bank][:, 0:1],
                        rows[0:1, roff + 128 * iq:roff + 128 * (iq + 1)],
                        c["ones_row_f"][0:1, 0:1], start=True, stop=True),
                        reads=("rBRD", "rRC", "rDRW", "c_ones"),
                        writes=(f"pb{bank}",))
                    p.op("dve", lambda v, dst=dst, bank=bank: v.tensor_copy(
                        dst[:, :], pb[bank][:, 0:1]),
                        reads=(f"pb{bank}",), writes=(dk,))
                for J in range(I):
                    p.op("pe", lambda t, J=J, iq=iq, I=I: t.matmul(
                        pb[4][:, 0:64],
                        eT[:, 512 * J + 128 * iq:512 * J + 128 * (iq + 1)],
                        u_sb[:, 64 * J:64 * (J + 1)],
                        start=(J == 0), stop=False),
                        reads=("eTE0", "eTE1", "u_sbE"), writes=("pb4",))
                vcol0 = 128 * I + kvo
                if I > 0:
                    p.op("dve", lambda v, vcol0=vcol0: v.scalar_tensor_tensor(
                        ucur[:, :], pb[4][:, 0:64], brdcol[:, :],
                        vsb[:, vcol0:vcol0 + 64], op0=OP.mult, op1=OP.add),
                        reads=("pb4", "brdcol", "vsbE"), writes=("ucur",))
                else:
                    p.op("dve", lambda v, vcol0=vcol0: v.tensor_copy(
                        ucur[:, :], vsb[:, vcol0:vcol0 + 64]),
                        reads=("vsbE",), writes=("ucur",))
                p.op("pe", lambda t, iq=iq: t.matmul(
                    pb[5][:, 0:128], c["ones_row_f"][:, :],
                    rows[0:1, R_BRD + 128 * iq:R_BRD + 128 * (iq + 1)],
                    start=True, stop=True),
                    reads=("rBRD", "c_ones"), writes=("pb5",))
                p.op("dve", lambda v, I=I, iq=iq: v.tensor_mul(
                    NTs[:, :], eT[:, 512 * I + 128 * iq:512 * I + 128 * (iq + 1)],
                    pb[5][:, 0:128]),
                    reads=("eTE0", "eTE1", "pb5"), writes=("NTs",))
                p.op("pe", lambda t: t.transpose(pbT[:, 0:128], NTs[:, :],
                                                 ident_e[:, :]),
                     reads=("NTs", "identE"), writes=("pbT",))
                p.op("dve", lambda v: v.tensor_copy(Ns[:, :], pbT[:, 0:128]),
                     reads=("pbT",), writes=("Ns",))
                for lev in range(7):
                    PTl, kPT = (NTs, "NTs") if lev == 0 else (PTj[(lev - 1) % 2],
                                                              f"PTjE{(lev - 1) % 2}")
                    Pl, kP = (Ns, "Ns") if lev == 0 else (Pj[(lev - 1) % 2],
                                                          f"PjE{(lev - 1) % 2}")
                    p.op("pe", lambda t, PTl=PTl: t.matmul(
                        pb[5][:, 0:64], PTl[:, :], ucur[:, :],
                        start=True, stop=True),
                        reads=(kPT, "ucur"), writes=("pb5",))
                    p.op("dve", lambda v: v.tensor_add(ucur[:, :], ucur[:, :],
                                                       pb[5][:, 0:64]),
                         reads=("ucur", "pb5"), writes=("ucur",))
                    if lev < 6:
                        p.op("pe", lambda t, PTl=PTl, Pl=Pl: t.matmul(
                            pb[6][:, 0:128], PTl[:, :], Pl[:, :],
                            start=True, stop=True),
                            reads=(kPT, kP), writes=("pb6",))
                        p.op("dve", lambda v, lev=lev: v.tensor_copy(
                            Pj[lev % 2][:, :], pb[6][:, 0:128]),
                            reads=("pb6",), writes=(f"PjE{lev % 2}",))
                        p.op("pe", lambda t, PTl=PTl, Pl=Pl: t.matmul(
                            pb[0][:, 0:128], Pl[:, :], PTl[:, :],
                            start=True, stop=True),
                            reads=(kPT, kP), writes=("pb0",))
                        p.op("dve", lambda v, lev=lev: v.tensor_copy(
                            PTj[lev % 2][:, :], pb[0][:, 0:128]),
                            reads=("pb0",), writes=(f"PTjE{lev % 2}",))
                p.op("dve", lambda v, I=I: v.tensor_copy(
                    u_sb[:, 64 * I:64 * (I + 1)], ucur[:, :]),
                    reads=("ucur",), writes=("u_sbE",))
                p.op("pe", lambda t, I=I, iq=iq: t.matmul(
                    pb[4][:, 0:64],
                    eT[:, 512 * I + 128 * iq:512 * I + 128 * (iq + 1)],
                    ucur[:, :], start=(I == 0), stop=True),
                    reads=("eTE0", "eTE1", "ucur"), writes=("pb4",))
                p.op("dve", lambda v: v.tensor_scalar_mul(t1s[:, :],
                                                          pb[4][:, 0:64],
                                                          rccol[:, :]),
                     reads=("pb4", "rccol"), writes=("t1sE",))
                p.op("dve", lambda v: v.scalar_tensor_tensor(
                    oIb[:, :], ucur[:, :], dcol[:, :], t1s[:, :],
                    op0=OP.mult, op1=OP.add),
                    reads=("ucur", "dcol", "t1sE"), writes=("oIE",))
                p.op("pe", lambda t: t.transpose(pbT[0:64, 0:128], oIb[:, :],
                                                 ident_e[:, :]),
                     reads=("oIE", "identE"), writes=("pbT",))
                p.op("dve", lambda v, I=I, h=h: v.tensor_copy(
                    oth[64 * (h % 2):64 * (h % 2) + 64, 128 * I:128 * (I + 1)],
                    pbT[0:64, 0:128]),
                    reads=("pbT",), writes=("othE",))
        if h % 2 == 1:
            p.dma("sp", lambda sy, h=h: sy.dma_start(
                oT2_d[128 * (h // 2):128 * (h // 2) + 128, :], oth[:, :]),
                reads=("othE",), writes=("oT2o",), tag="stoT2")
        if debug and h == 0:
            p.dma("sp", lambda sy: sy.dma_start(dbg["q0"][:, :], qsb[0][:, :]),
                  reads=("qsbE0",), writes=("dbg",))
            p.dma("sp", lambda sy: sy.dma_start(dbg["k0"][:, :], ksb[:, :]),
                  reads=("ksbE",), writes=("dbg",))
            p.dma("sp", lambda sy: sy.dma_start(dbg["v0"][:, :], vsb[:, :]),
                  reads=("vsbE",), writes=("dbg",))
            p.dma("sp", lambda sy: sy.dma_start(dbg["beta"][:, :], beta_d[:, :]),
                  reads=("betad",), writes=("dbg",))
            p.dma("sp", lambda sy: sy.dma_start(dbg["eT0"][:, :], eT[:, 0:512]),
                  reads=("eTE0", "eTE1"), writes=("dbg",))
            p.dma("sp", lambda sy: sy.dma_start(dbg["rows"][:, :], rows[:, :]),
                  reads=("rDS", "rDG", "rDC", "rRC", "rBRD", "rDRW"), writes=("dbg",))
            p.dma("sp", lambda sy: sy.dma_start(dbg["u01"][:, :], u_sb[:, 0:128]),
                  reads=("u_sbE",), writes=("dbg",))
            p.dma("sp", lambda sy: sy.dma_start(dbg["cos"][:, :], cosb[:, :]),
                  reads=("cosb",), writes=("dbg",))
            p.dma("sp", lambda sy: sy.dma_start(dbg["sin"][:, :], sinb[:, :]),
                  reads=("sinb",), writes=("dbg",))
            p.dma("sp", lambda sy: sy.dma_start(dbg["perm"][:, :], permt[:, :]),
                  reads=("permt",), writes=("dbg",))
            p.dma("sp", lambda sy: sy.dma_start(dbg["negM"][:, :], negM2b[:, :]),
                  reads=("negM2b",), writes=("dbg",))
    if standalone:
        return p.finalize()


def run_launch_E(G2, ins):
    nc = _get_nc('E', build_launch_E)
    Wq, Wk, Wv, Wb = (ins["L1_attn_Wq"], ins["L1_attn_Wk"], ins["L1_attn_Wv"],
                      ins["L1_attn_Wb"])
    in_maps = []
    for cidx in range(NCORE):
        b, m = cidx // 4, cidx % 4
        in_maps.append({
            "g2": _np(G2[:, 4096 * b:4096 * (b + 1)]),
            "wq": _np(Wq[:, 512 * m:512 * (m + 1)]),
            "wk": _np(Wk[:, 128 * m:128 * (m + 1)]),
            "wv": _np(Wv[:, 128 * m:128 * (m + 1)]),
            "wb": _np(Wb[:, 8 * m:8 * (m + 1)]),
            "bb": _np(ins["L1_attn_bb"][8 * m:8 * (m + 1)].reshape(8, 1)),
            "qnw": _np(ins["L1_attn_q_norm_w"].reshape(64, 1)),
            "knw": _np(ins["L1_attn_k_norm_w"].reshape(64, 1)),
        })
    res = run_bass_kernel_spmd(nc, in_maps, core_ids=list(range(NCORE)))
    GO2 = np.zeros((B, DMODEL, S), dtype=ml_dtypes.bfloat16)
    for cidx in range(NCORE):
        b, m = cidx // 4, cidx % 4
        GO2[b, 512 * m:512 * (m + 1), :] = res.results[cidx]["oT2"]
    return GO2



def build_launch_F(p=None, io=None):
    standalone = p is None
    if standalone:
        p = Prog("F")
        x2sl_d = p.dram("x2sl", [TOK_PER_CORE, DMODEL], F32, kind="ExternalInput")
        go2sl_d = p.dram("go2sl", [DMODEL, TOK_PER_CORE], BF16, kind="ExternalInput")
        wo_d = p.dram("wo", [DMODEL, DMODEL], F32, kind="ExternalInput")
        rw_d = p.dram("rw", [DMODEL, 8], F32, kind="ExternalInput")
        shg_d = p.dram("shg", [DMODEL, 8192], F32, kind="ExternalInput")
        shu_d = p.dram("shu", [DMODEL, 8192], F32, kind="ExternalInput")
        shd_d = p.dram("shd", [8192, DMODEL], F32, kind="ExternalInput")
        rg_d = p.dram("rg", [DMODEL, 1024], F32, kind="ExternalInput")
        ru_d = p.dram("ru", [DMODEL, 1024], F32, kind="ExternalInput")
        rd_d = p.dram("rd", [1024, DMODEL], F32, kind="ExternalInput")
        out_d = p.dram("out", [TOK_PER_CORE, DMODEL], F32, kind="ExternalOutput")
        x3_d = p.dram("x3d", [TOK_PER_CORE, DMODEL], F32)
        ln3_d = p.dram("ln3d", [DMODEL, TOK_PER_CORE], BF16)
        c = emit_consts(p)
    else:
        x2sl_d, go2sl_d = io["x2"], io["go2"]
        wo_d, rw_d = io["wo"], io["rw"]
        shg_d, shu_d, shd_d = io["shg"], io["shu"], io["shd"]
        rg_d, ru_d, rd_d = io["rg"], io["ru"], io["rd"]
        out_d, x3_d, ln3_d = io["out"], io["x3d"], io["ln3d"]
        c = io["consts"]
    p.const_col(1e-5)
    NT = TOK_PER_CORE // 128
    _pidcF = {}

    xt = [p.sbuf(f"xtF{i}", [128, DMODEL], F32, scoped=False) for i in range(2)]
    rwF = p.sbuf("rwF", [128, 16 * 8], BF16, scoped=False)
    cbF = p.sbuf("cbF", [128, 8], F32, scoped=False)
    p.push_scope()
    go2 = [p.sbuf(f"go2F{i}", [128, 16 * 128], BF16) for i in range(2)]
    woF = [p.sbuf(f"woF{i}", [128, 16 * 512], BF16) for i in range(2)]
    psA = p.psum("psAF", [128, 512])
    psB2 = p.psum("psBF", [128, 512])

    def f_rw(g):
        srcap = bass.AP(rw_d, 0, [[8, 128], [128 * 8, 16], [1, 8]])
        dst = bass.AP(rwF, 0, [[16 * 8, 128], [8, 16], [1, 8]])
        return g.dma_start(dst, srcap)
    p.dma("pool", f_rw, writes=("rwF",), tag="ldrw")

    for tt in range(NT):
        xb = xt[tt % 2]
        kx = f"xtF{tt % 2}"
        p.dma("sp", lambda sy, tt=tt, xb=xb: sy.dma_start(
            xb[:, :], x2sl_d[128 * tt:128 * (tt + 1), :]),
            reads=(kx,), writes=(kx,), tag="ldx2")
        gb = go2[tt % 2]
        kgb = f"go2F{tt % 2}"
        if standalone:
            def f_go(sy, tt=tt, gb=gb):
                srcap = bass.AP(go2sl_d, 128 * tt,
                                [[TOK_PER_CORE, 128], [128 * TOK_PER_CORE, 16], [1, 128]])
                dst = bass.AP(gb, 0, [[16 * 128, 128], [128, 16], [1, 128]])
                return sy.dma_start(dst, srcap)
        else:
            def f_go(sy, tt=tt, gb=gb):
                if "c" not in _pidcF:
                    _pidcF["c"] = (sy.partition_id() % 4) * 1024
                srcap = bass.AP(go2sl_d, _pidcF["c"] + 128 * tt,
                                [[S, 128], [128 * S, 16], [1, 128]])
                dst = bass.AP(gb, 0, [[16 * 128, 128], [128, 16], [1, 128]])
                return sy.dma_start(dst, srcap)
        p.dma("sp", f_go, reads=(kgb,), writes=(kgb,), tag="ldgo2")
        for fb in range(4):
            wb = woF[fb % 2]
            kwb = f"woF{fb % 2}"
            if tt == 0 or True:
                def f_wo(g, fb=fb, wb=wb):
                    srcap = bass.AP(wo_d, 512 * fb,
                                    [[DMODEL, 128], [128 * DMODEL, 16], [1, 512]])
                    dst = bass.AP(wb, 0, [[16 * 512, 128], [512, 16], [1, 512]])
                    return g.dma_start(dst, srcap)
                p.dma("pool", f_wo, reads=(kwb,), writes=(kwb,), tag="ldwoF")
            for hc in range(16):
                p.op("pe", lambda t, hc=hc, gb=gb, wb=wb: t.matmul(
                    psA[:, :], gb[:, 128 * hc:128 * (hc + 1)],
                    wb[:, 512 * hc:512 * (hc + 1)],
                    start=(hc == 0), stop=(hc == 15)),
                    reads=(kgb, kwb), writes=("psAF",))
            p.op("dve", lambda v, fb=fb, xb=xb: v.tensor_add(
                xb[:, 512 * fb:512 * (fb + 1)],
                xb[:, 512 * fb:512 * (fb + 1)], psA[:, :]),
                reads=(kx, "psAF"), writes=(kx,))
        p.dma("sp", lambda sy, tt=tt, xb=xb: sy.dma_start(
            x3_d[128 * tt:128 * (tt + 1), :], xb[:, :]),
            reads=(kx,), writes=("x3d",), tag="stx3")

    lgF = p.sbuf("lgF", [128, 8], F32)
    m1 = p.sbuf("m1F", [128, 1], F32)
    nm1 = p.sbuf("nm1F", [128, 1], F32)
    mk1 = p.sbuf("mk1F", [128, 8], F32)
    w2 = p.sbuf("w2F", [128, 8], F32)
    m2 = p.sbuf("m2F", [128, 1], F32)
    e1 = p.sbuf("e1F", [128, 1], F32)
    dn1 = p.sbuf("dn1F", [128, 1], F32)
    rc1 = p.sbuf("rc1F", [128, 1], F32)

    def routerF(tt, st, kst):
        for hc in range(16):
            p.op("pe", lambda t, hc=hc, st=st: t.matmul(
                psB2[:, 0:8], st[:, 128 * hc:128 * (hc + 1)],
                rwF[:, 8 * hc:8 * (hc + 1)],
                start=(hc == 0), stop=(hc == 15)),
                reads=(kst, "rwF"), writes=("psBF",))
        p.op("dve", lambda v: v.tensor_copy(lgF[:, :], psB2[:, 0:8]),
             reads=("psBF",), writes=("lgF",))
        p.op("dve", lambda v: v.reduce_max(m1[:, :], lgF[:, :], axis=AX.X),
             reads=("lgF",), writes=("m1F",))
        p.op("dve", lambda v: v.tensor_scalar_mul(nm1[:, :], m1[:, :], -1.0),
             reads=("m1F",), writes=("nm1F",))
        p.op("dve", lambda v: v.tensor_single_scalar(mk1[:, :], lgF[:, :],
                                                     m1[:, :], OP.is_ge),
             reads=("lgF", "m1F"), writes=("mk1F",))
        p.op("dve", lambda v: v.scalar_tensor_tensor(
            w2[:, :], mk1[:, :], -1e30, lgF[:, :], op0=OP.mult, op1=OP.add),
            reads=("mk1F", "lgF"), writes=("w2F",))
        p.op("dve", lambda v: v.reduce_max(m2[:, :], w2[:, :], axis=AX.X),
             reads=("w2F",), writes=("m2F",))
        p.op("act", lambda s: s.activation(e1[:, :], m2[:, :], AF.Exp,
                                           bias=nm1[:, :]),
             reads=("m2F", "nm1F"), writes=("e1F",))
        p.op("dve", lambda v: v.tensor_scalar_add(dn1[:, :], e1[:, :], 1.0),
             reads=("e1F",), writes=("dn1F",))
        p.op("dve", lambda v: v.reciprocal(rc1[:, :], dn1[:, :]),
             reads=("dn1F",), writes=("rc1F",))
        p.op("dve", lambda v, tt=tt: v.tensor_mul(cbF[:, tt:tt + 1], dn1[:, :],
                                                  rc1[:, :]),
             reads=("dn1F", "rc1F"), writes=("cbF",))

    def get_x3_tile(tt):
        xb = xt[tt % 2]
        kx = f"xtF{tt % 2}"
        p.dma("sp", lambda sy, tt=tt, xb=xb: sy.dma_start(
            xb[:, :], x3_d[128 * tt:128 * (tt + 1), :]),
            reads=(kx, "x3d"), writes=(kx,), tag="ldx3")
        return xb, kx

    emit_ln_from_sbuf(p, c, get_x3_tile, NT, ln3_d, 0, "ln3",
                      post_tile=routerF)
    p.pop_scope()

    p.push_scope()
    lnF = p.sbuf("lnF", [128, 16 * 512], BF16)
    wbg = [p.sbuf(f"wbgF{i}", [128, 16 * 512], BF16) for i in range(2)]
    wbu = [p.sbuf(f"wbuF{i}", [128, 16 * 512], BF16) for i in range(2)]
    wbd = [p.sbuf(f"wbdF{i}", [128, 16 * 512], BF16) for i in range(2)]
    actF = p.sbuf("actF", [128, 16 * 512], BF16)
    sgFs = [p.sbuf(f"sgFF{i}", [128, 512], BF16) for i in range(2)]
    acc = [p.sbuf(f"accF{i}", [128, DMODEL], F32) for i in range(4)]
    psG1 = p.psum("psG1F", [128, 512])
    psG2 = p.psum("psG2F", [128, 512])
    psU1 = p.psum("psU1F", [128, 512])
    psU2 = p.psum("psU2F", [128, 512])
    psD = [p.psum(f"psDF{i}", [128, 512]) for i in range(4)]

    def gate_quarter(sp, row0, n_ic, wsrc_g, wsrc_u, wpitch, klb):
        ng = (n_ic + 3) // 4
        for g2_ in range(ng):
            bg, bu = wbg[g2_ % 2], wbu[g2_ % 2]
            kbg, kbu = f"wbgF{g2_ % 2}", f"wbuF{g2_ % 2}"
            nthis = min(4, n_ic - 4 * g2_)
            for (wsrc, dstb, kdst) in ((wsrc_g, bg, kbg), (wsrc_u, bu, kbu)):
                def f_w(g, wsrc=wsrc, dstb=dstb, g2_=g2_, nthis=nthis,
                        row0=row0, wpitch=wpitch):
                    srcap = bass.AP(wsrc, row0 + 512 * g2_,
                                    [[wpitch, 128], [128 * wpitch, 16],
                                     [1, 128 * nthis]])
                    dst = bass.AP(dstb, 0, [[16 * 512, 128], [512, 16],
                                            [1, 128 * nthis]])
                    return g.dma_start(dst, srcap)
                p.dma("pool", f_w, reads=(kdst,), writes=(kdst,), tag="ldgw4")
            for li in range(nthis):
                il = 4 * g2_ + li
                pg = psG1 if il % 2 == 0 else psG2
                pu = psU1 if il % 2 == 0 else psU2
                kpg = "psG1F" if il % 2 == 0 else "psG2F"
                kpu = "psU1F" if il % 2 == 0 else "psU2F"
                for hc in range(16):
                    p.op("pe", lambda t, hc=hc, li=li, bg=bg, pg=pg: t.matmul(
                        pg[:, :], bg[:, 512 * hc + 128 * li:512 * hc + 128 * (li + 1)],
                        lnF[:, 512 * hc:512 * (hc + 1)],
                        start=(hc == 0), stop=(hc == 15)),
                        reads=(kbg, klb), writes=(kpg,))
                sgF = sgFs[il % 2]
                ksgF = f"sgFF{il % 2}"
                p.op("act", lambda s, pg=pg, sgF=sgF: s.activation(
                    sgF[:, :], pg[:, :], AF.Silu),
                     reads=(kpg,), writes=(ksgF,))
                for hc in range(16):
                    p.op("pe", lambda t, hc=hc, li=li, bu=bu, pu=pu: t.matmul(
                        pu[:, :], bu[:, 512 * hc + 128 * li:512 * hc + 128 * (li + 1)],
                        lnF[:, 512 * hc:512 * (hc + 1)],
                        start=(hc == 0), stop=(hc == 15)),
                        reads=(kbu, klb), writes=(kpu,))
                p.op("dve", lambda v, il=il, pu=pu, sgF=sgF: v.tensor_mul(
                    actF[:, 512 * il:512 * (il + 1)], sgF[:, :], pu[:, :]),
                    reads=(ksgF, kpu), writes=("actF",))

    def down_quarter(sp, row0, n_ic, wsrc_d, first, scale_cb=False):
        for fb in range(4):
            db = wbd[fb % 2]
            kdb = f"wbdF{fb % 2}"
            def f_d(g, fb=fb, db=db, row0=row0, n_ic=n_ic, wsrc_d=wsrc_d):
                srcap = bass.AP(wsrc_d, row0 * DMODEL + 512 * fb,
                                [[DMODEL, 128], [128 * DMODEL, n_ic], [1, 512]])
                dst = bass.AP(db, 0, [[16 * 512, 128], [512, n_ic], [1, 512]])
                return g.dma_start(dst, srcap)
            p.dma("pool", f_d, reads=(kdb,), writes=(kdb,), tag="lddw4")
            for t128 in range(4):
                ps = psD[t128]
                kps = f"psDF{t128}"
                for il in range(n_ic):
                    p.op("pe", lambda t, il=il, t128=t128, db=db, ps=ps: t.matmul(
                        ps[:, :], actF[:, 512 * il + 128 * t128:
                                       512 * il + 128 * (t128 + 1)],
                        db[:, 512 * il:512 * (il + 1)],
                        start=(il == 0), stop=(il == n_ic - 1)),
                        reads=("actF", kdb), writes=(kps,))
                tt = 4 * sp + t128
                if scale_cb:
                    p.op("dve", lambda v, fb=fb, t128=t128, tt=tt, ps=ps, kps=kps:
                         v.scalar_tensor_tensor(
                        acc[t128][:, 512 * fb:512 * (fb + 1)], ps[:, :],
                        cbF[:, tt:tt + 1], acc[t128][:, 512 * fb:512 * (fb + 1)],
                        op0=OP.mult, op1=OP.add),
                        reads=(kps, "cbF", f"accF{t128}"), writes=(f"accF{t128}",))
                elif first:
                    p.op("dve", lambda v, fb=fb, t128=t128, ps=ps, kps=kps:
                         v.tensor_copy(
                        acc[t128][:, 512 * fb:512 * (fb + 1)], ps[:, :]),
                        reads=(kps,), writes=(f"accF{t128}",))
                else:
                    p.op("dve", lambda v, fb=fb, t128=t128, ps=ps, kps=kps:
                         v.tensor_add(
                        acc[t128][:, 512 * fb:512 * (fb + 1)],
                        acc[t128][:, 512 * fb:512 * (fb + 1)], ps[:, :]),
                        reads=(f"accF{t128}", kps), writes=(f"accF{t128}",))

    for sp in range(2):
        def f_ln(sy, sp=sp):
            srcap = bass.AP(ln3_d, 512 * sp,
                            [[TOK_PER_CORE, 128], [128 * TOK_PER_CORE, 16], [1, 512]])
            dst = bass.AP(lnF, 0, [[16 * 512, 128], [512, 16], [1, 512]])
            return sy.dma_start(dst, srcap)
        p.dma("sp", f_ln, reads=("lnF",), writes=("lnF",), tag="ldln3")
        for qq in range(4):
            gate_quarter(sp, 2048 * qq, 16, shg_d, shu_d, 8192, "lnF")
            down_quarter(sp, 2048 * qq, 16, shd_d, first=(qq == 0))
        gate_quarter(sp, 0, 8, rg_d, ru_d, 1024, "lnF")
        down_quarter(sp, 0, 8, rd_d, first=False, scale_cb=True)
        for t128 in range(4):
            tt = 4 * sp + t128
            xb, kx = get_x3_tile(tt)
            p.op("dve", lambda v, t128=t128, xb=xb: v.tensor_add(
                xb[:, :], xb[:, :], acc[t128][:, :]),
                reads=(kx, f"accF{t128}"), writes=(kx,))
            p.dma("sp", lambda sy, tt=tt, xb=xb: sy.dma_start(
                out_d[128 * tt:128 * (tt + 1), :], xb[:, :]),
                reads=(kx,), writes=("outF",), tag="stout")
    p.pop_scope()
    if standalone:
        return p.finalize()


def run_launch_F(x2, GO2, ins):
    nc = _get_nc('F', build_launch_F)
    woN = _np(ins["L1_attn_Wo"])
    rwT = _np(ins["L1_ffn_router_weight"].T)
    shgT = _np(ins["L1_ffn_sh_gate"].T)
    shuT = _np(ins["L1_ffn_sh_up"].T)
    shdT = _np(ins["L1_ffn_sh_down"].T)
    rgT = _np(ins["L1_ffn_r_gate"].T)
    ruT = _np(ins["L1_ffn_r_up"].T)
    rdT = _np(ins["L1_ffn_r_down"].T)
    in_maps = []
    for cidx in range(NCORE):
        b = cidx // 4
        t0 = 1024 * cidx
        tl0 = t0 - 4096 * b
        in_maps.append({
            "x2sl": _np(x2[t0:t0 + 1024]),
            "go2sl": _np(GO2[b][:, tl0:tl0 + 1024]),
            "wo": woN, "rw": rwT, "shg": shgT, "shu": shuT, "shd": shdT,
            "rg": rgT, "ru": ruT, "rd": rdT,
        })
    res = run_bass_kernel_spmd(nc, in_maps, core_ids=list(range(NCORE)))
    return np.concatenate([res.results[i]["out"] for i in range(NCORE)], axis=0)



def _np(a):
    return np.ascontiguousarray(a)


def run_launch_A(x, w_qkv):
    wT = _np(w_qkv.T)
    nc = _get_nc('A', build_launch_A)
    in_maps = []
    for cidx in range(NCORE):
        b, m = cidx // 4, cidx % 4
        cols = np.concatenate([
            wT[:, 512 * m:512 * (m + 1)],
            wT[:, 2048 + 512 * m:2048 + 512 * (m + 1)],
            wT[:, 4096 + 512 * m:4096 + 512 * (m + 1)]], axis=1)
        in_maps.append({"xb": _np(x[b]), "wsl": _np(cols)})
    res = run_bass_kernel_spmd(nc, in_maps, core_ids=list(range(NCORE)))
    GO = np.zeros((B, DMODEL, S), dtype=ml_dtypes.bfloat16)
    for cidx in range(NCORE):
        b, m = cidx // 4, cidx % 4
        GO[b, 512 * m:512 * (m + 1), :] = res.results[cidx]["oTn"]
    return GO


def run_launch_B(x_flat, GO, w_o, gate_w):
    woT = _np(w_o.T)
    gwT = _np(gate_w.T)
    nc = _get_nc('B', build_launch_B)
    in_maps = []
    for cidx in range(NCORE):
        b = cidx // 4
        t0 = 1024 * cidx
        tl0 = t0 - 4096 * b
        in_maps.append({
            "xsl": _np(x_flat[t0:t0 + 1024]),
            "gosl": _np(GO[b][:, tl0:tl0 + 1024]),
            "woT": woT, "gwT": gwT,
        })
    res = run_bass_kernel_spmd(nc, in_maps, core_ids=list(range(NCORE)))
    x1 = np.concatenate([res.results[i]["x1"] for i in range(NCORE)], axis=0)
    G1 = np.concatenate([res.results[i]["ln1T"] for i in range(NCORE)], axis=1)
    comb = np.concatenate([res.results[i]["comb"] for i in range(NCORE)], axis=0)
    return x1, G1, comb


def run_launch_C(G1, ins, comb):
    nc = _get_nc('C', build_launch_C)
    in_maps = []
    for cidx in range(NCORE):
        e0 = 2 * cidx
        in_maps.append({
            "g1": G1,
            "pkgu": _np(ins["L0_ffn_gate_up_packed"][e0:e0 + 2]),
            "scgu": _np(ins["L0_ffn_gate_up_scales"][e0:e0 + 2]),
            "pkdn": _np(ins["L0_ffn_down_packed"][e0:e0 + 2]),
            "scdn": _np(ins["L0_ffn_down_scales"][e0:e0 + 2]),
            "combsl": _np(comb[:, e0:e0 + 2]),
            "pksguc": _np(ins["L0_ffn_shared_gate_up_packed"][:, 384 * cidx:384 * (cidx + 1)]),
            "scsguc": _np(ins["L0_ffn_shared_gate_up_scales"][:, 384 * cidx:384 * (cidx + 1)]),
            "pksdnc": _np(ins["L0_ffn_shared_down_packed"][:, 256 * cidx:256 * (cidx + 1)]),
            "scsdnc": _np(ins["L0_ffn_shared_down_scales"][:, 256 * cidx:256 * (cidx + 1)]),
        })
    res = run_bass_kernel_spmd(nc, in_maps, core_ids=list(range(NCORE)))
    parts = [res.results[i]["partial"] for i in range(NCORE)]
    SGU = np.concatenate([res.results[i]["sgslo"] for i in range(NCORE)], axis=1)
    SDN = np.concatenate([res.results[i]["sdno"] for i in range(NCORE)], axis=1)
    return parts, SGU, SDN



def build_fused():
    p = Prog("FUSED")
    xb = p.dram("i_xb", [S, DMODEL], F32, kind="ExternalInput")
    xsl = p.dram("i_xsl", [TOK_PER_CORE, DMODEL], F32, kind="ExternalInput")
    wsl = p.dram("i_wsl", [DMODEL, 1536], F32, kind="ExternalInput")
    woT = p.dram("i_woT", [DMODEL, DMODEL], F32, kind="ExternalInput")
    gwTp = p.dram("i_gwTp", [DMODEL, 16], F32, kind="ExternalInput")
    pkgu = p.dram("i_pkgu", [2, 256, 3072], I32, kind="ExternalInput")
    scgu = p.dram("i_scgu", [2, 16, 3072], F32, kind="ExternalInput")
    pkdn = p.dram("i_pkdn", [2, 192, 2048], I32, kind="ExternalInput")
    scdn = p.dram("i_scdn", [2, 12, 2048], F32, kind="ExternalInput")
    pksgu = p.dram("i_pksgu", [256, 3072], I32, kind="ExternalInput")
    scsgu = p.dram("i_scsgu", [16, 3072], F32, kind="ExternalInput")
    pksdn = p.dram("i_pksdn", [192, 2048], I32, kind="ExternalInput")
    scsdn = p.dram("i_scsdn", [12, 2048], F32, kind="ExternalInput")
    wq = p.dram("i_wq", [DMODEL, 512], F32, kind="ExternalInput")
    wk = p.dram("i_wk", [DMODEL, 128], F32, kind="ExternalInput")
    wv = p.dram("i_wv", [DMODEL, 128], F32, kind="ExternalInput")
    wb = p.dram("i_wb", [DMODEL, 8], F32, kind="ExternalInput")
    bb = p.dram("i_bb", [8, 1], F32, kind="ExternalInput")
    qnw = p.dram("i_qnw", [64, 1], F32, kind="ExternalInput")
    knw = p.dram("i_knw", [64, 1], F32, kind="ExternalInput")
    wo = p.dram("i_wo", [DMODEL, DMODEL], F32, kind="ExternalInput")
    rw = p.dram("i_rw", [DMODEL, 8], F32, kind="ExternalInput")
    shg = p.dram("i_shg", [DMODEL, 8192], F32, kind="ExternalInput")
    shu = p.dram("i_shu", [DMODEL, 8192], F32, kind="ExternalInput")
    shd = p.dram("i_shd", [8192, DMODEL], F32, kind="ExternalInput")
    rg = p.dram("i_rg", [DMODEL, 1024], F32, kind="ExternalInput")
    ru = p.dram("i_ru", [DMODEL, 1024], F32, kind="ExternalInput")
    rd = p.dram("i_rd", [1024, DMODEL], F32, kind="ExternalInput")
    out = p.dram("i_out", [TOK_PER_CORE, DMODEL], F32, kind="ExternalOutput")
    lnT = p.dram("lnT", [DMODEL, S], BF16)
    oTn_c = p.dram("oTn_c", [512, S], BF16)
    go_gb = p.dram("go_gb", [2048, S], BF16)
    x1 = p.dram("x1i", [TOK_PER_CORE, DMODEL], F32)
    ln1T = p.dram("ln1Ti", [DMODEL, TOK_PER_CORE], BF16)
    g1_g = p.dram("g1_g", [NCORE * DMODEL, TOK_PER_CORE], BF16)
    prt_c = p.dram("prt_c", [NTOK, DMODEL], BF16)
    rs_out = p.dram("rs_out", [TOK_PER_CORE, DMODEL], BF16)
    wgu2 = p.dram("wgu2i", [2048, 1536], BF16)
    sgu_loc = p.dram("sgu_loc", [DMODEL, 3072], BF16)
    sdn_loc = p.dram("sdn_loc", [1536, DMODEL], BF16)
    x2 = p.dram("x2i", [TOK_PER_CORE, DMODEL], F32)
    ln2T = p.dram("ln2Ti", [DMODEL, TOK_PER_CORE], BF16)
    g2_gb = p.dram("g2_gb", [4 * DMODEL, TOK_PER_CORE], BF16)
    oT2_c = p.dram("oT2_c", [512, S], BF16)
    go2_gb = p.dram("go2_gb", [2048, S], BF16)
    betad = p.dram("betadi", [8, S], F32)
    x3d = p.dram("x3di", [TOK_PER_CORE, DMODEL], F32)
    ln3d = p.dram("ln3di", [DMODEL, TOK_PER_CORE], BF16)

    c = emit_consts(p)
    for v in (1e-5, -2.0, -6.0, 1e-6, float(np.pi / 2.0)):
        p.const_col(v)

    def coll(kind, op_, groups, in_h, out_h, reads, writes, tag):
        def f(g):
            return g.collective_compute(kind, op_, replica_groups=groups,
                                        ins=[in_h.ap().opt()],
                                        outs=[out_h.ap().opt()])
        p.op("pool", f, reads=reads, writes=writes, tag=tag)
        p.fence()

    GRP4 = [[0, 1, 2, 3], [4, 5, 6, 7]]
    GRP8 = [list(range(8))]

    p.push_scope()
    build_launch_A(p=p, io=dict(xb=xb, wsl=wsl, oTn=oTn_c, lnT=lnT, consts=c))
    p.pop_scope()
    coll("AllGather", OP.bypass, GRP4, oTn_c, go_gb,
         ("oTn_out0", "oTn_out1", "oTn_out2", "oTn_out3"), ("go_gb",), "cGO")

    p.push_scope()
    build_launch_B(p=p, io=dict(xsl=xsl, gosl=go_gb, woT=woT, x1=x1,
                                ln1T=ln1T, consts=c))
    p.pop_scope()
    coll("AllGather", OP.bypass, GRP8, ln1T, g1_g, ("ln1_all",), ("g1_g",), "cG1")

    p.push_scope()
    build_launch_C(p=p, io=dict(g1=g1_g, pkgu=pkgu, scgu=scgu, pkdn=pkdn,
                                scdn=scdn, pksgu=pksgu, scsgu=scsgu,
                                pksdn=pksdn, scsdn=scsdn, gwTp=gwTp,
                                partial=prt_c, sgu_loc=sgu_loc,
                                sdn_loc=sdn_loc, wgu2=wgu2, consts=c))
    p.pop_scope()
    coll("ReduceScatter", OP.add, GRP8, prt_c, rs_out, ("partial",),
         ("rs_out",), "cRS")

    p.push_scope()
    build_launch_D(p=p, io=dict(x1=x1, ln1T=ln1T, rs_out=rs_out,
                                sgu_loc=sgu_loc, sdn_loc=sdn_loc, x2=x2,
                                ln2T=ln2T, consts=c))
    p.pop_scope()
    coll("AllGather", OP.bypass, GRP4, ln2T, g2_gb, ("ln2_all",), ("g2_gb",), "cG2")

    p.push_scope()
    build_launch_E(p=p, io=dict(g2=g2_gb, wq=wq, wk=wk, wv=wv, wb=wb, bb=bb,
                                qnw=qnw, knw=knw, oT2=oT2_c, betad=betad))
    p.pop_scope()
    coll("AllGather", OP.bypass, GRP4, oT2_c, go2_gb, ("oT2o",), ("go2_gb",), "cGO2")

    p.push_scope()
    build_launch_F(p=p, io=dict(x2=x2, go2=go2_gb, wo=wo, rw=rw, shg=shg,
                                shu=shu, shd=shd, rg=rg, ru=ru, rd=rd, out=out,
                                x3d=x3d, ln3d=ln3d, consts=c))
    p.pop_scope()
    return p.finalize()


def run_fused(ins):
    nc = _get_nc("FUSED", build_fused)
    x = ins["L0_attn_hidden_states"].reshape(NTOK, DMODEL)
    wT = _np(ins["L0_attn_w_qkv"].T)
    woT = _np(ins["L0_attn_w_o"].T)
    gwT = ins["L0_ffn_gate_w"].T
    woN = _np(ins["L1_attn_Wo"])
    rwT = _np(ins["L1_ffn_router_weight"].T)
    shgT = _np(ins["L1_ffn_sh_gate"].T)
    shuT = _np(ins["L1_ffn_sh_up"].T)
    shdT = _np(ins["L1_ffn_sh_down"].T)
    rgT = _np(ins["L1_ffn_r_gate"].T)
    ruT = _np(ins["L1_ffn_r_up"].T)
    rdT = _np(ins["L1_ffn_r_down"].T)
    Wq, Wk, Wv, Wb = (ins["L1_attn_Wq"], ins["L1_attn_Wk"], ins["L1_attn_Wv"],
                      ins["L1_attn_Wb"])
    in_maps = []
    for cidx in range(NCORE):
        b, m = cidx // 4, cidx % 4
        t0 = 1024 * cidx
        e0 = 2 * cidx
        cols = np.concatenate([
            wT[:, 512 * m:512 * (m + 1)],
            wT[:, 2048 + 512 * m:2048 + 512 * (m + 1)],
            wT[:, 4096 + 512 * m:4096 + 512 * (m + 1)]], axis=1)
        eperm = [e0, e0 + 1] + [e for e in range(16) if e not in (e0, e0 + 1)]
        in_maps.append({
            "i_xb": _np(x[4096 * b:4096 * (b + 1)]),
            "i_xsl": _np(x[t0:t0 + 1024]),
            "i_wsl": _np(cols),
            "i_woT": woT,
            "i_gwTp": _np(gwT[:, eperm]),
            "i_pkgu": _np(ins["L0_ffn_gate_up_packed"][e0:e0 + 2]),
            "i_scgu": _np(ins["L0_ffn_gate_up_scales"][e0:e0 + 2]),
            "i_pkdn": _np(ins["L0_ffn_down_packed"][e0:e0 + 2]),
            "i_scdn": _np(ins["L0_ffn_down_scales"][e0:e0 + 2]),
            "i_pksgu": _np(ins["L0_ffn_shared_gate_up_packed"]),
            "i_scsgu": _np(ins["L0_ffn_shared_gate_up_scales"]),
            "i_pksdn": _np(ins["L0_ffn_shared_down_packed"]),
            "i_scsdn": _np(ins["L0_ffn_shared_down_scales"]),
            "i_wq": _np(Wq[:, 512 * m:512 * (m + 1)]),
            "i_wk": _np(Wk[:, 128 * m:128 * (m + 1)]),
            "i_wv": _np(Wv[:, 128 * m:128 * (m + 1)]),
            "i_wb": _np(Wb[:, 8 * m:8 * (m + 1)]),
            "i_bb": _np(ins["L1_attn_bb"][8 * m:8 * (m + 1)].reshape(8, 1)),
            "i_qnw": _np(ins["L1_attn_q_norm_w"].reshape(64, 1)),
            "i_knw": _np(ins["L1_attn_k_norm_w"].reshape(64, 1)),
            "i_wo": woN, "i_rw": rwT, "i_shg": shgT, "i_shu": shuT, "i_shd": shdT,
            "i_rg": rgT, "i_ru": ruT, "i_rd": rdT,
        })
    res = run_bass_kernel_spmd(nc, in_maps, core_ids=list(range(NCORE)))
    return np.concatenate([res.results[i]["i_out"] for i in range(NCORE)], axis=0)


_NC_CACHE = {}


def _get_nc(name, builder):
    if name not in _NC_CACHE:
        _NC_CACHE[name] = builder()
    return _NC_CACHE[name]


def kernel(**ins):
    out = run_fused(ins)
    return out.reshape(B, S, DMODEL).astype(np.float32)


# version 06 (final)
# speedup vs baseline: 1.4472x; 1.0360x over previous
import numpy as np
import ml_dtypes
import concourse.bass as bass
import concourse.mybir as mybir
from concourse.bass_utils import run_bass_kernel_spmd
from contextlib import ExitStack

F32 = mybir.dt.float32
BF16 = mybir.dt.bfloat16
I32 = mybir.dt.int32
AF = mybir.ActivationFunctionType
OP = mybir.AluOpType
AX = mybir.AxisListType

NCORE = 8
B = 2
S = 4096
DMODEL = 2048
NTOK = B * S
TOK_PER_CORE = NTOK // NCORE

INV_SQRT_D0 = 1.0 / float(np.sqrt(128.0))


ENGS = ("pe", "dve", "act", "pool", "sp")


class Rec:
    __slots__ = ("eng", "fn", "reads", "writes", "queue", "deps", "inc", "idx",
                 "dma_ord", "tag", "waits", "extra", "lane")

    def __init__(self, eng, fn, reads, writes, queue, tag):
        self.eng = eng
        self.fn = fn
        self.reads = reads
        self.writes = writes
        self.queue = queue
        self.deps = None
        self.inc = False
        self.idx = -1
        self.dma_ord = -1
        self.tag = tag


class Prog:

    def __init__(self, name="prog"):
        self.nc = bass.Bass("TRN2", target_bir_lowering=False,
                            detect_race_conditions=False)
        self.name = name
        self.recs = []
        self.stack = ExitStack()
        self._scopes = []
        self._fence = ()
        self._sbuf_bytes = 0
        self._uid = 0

    def push_scope(self):
        self._scopes.append(ExitStack())

    def pop_scope(self):
        self._scopes.pop().close()
        self.fence()

    def dram(self, name, shape, dtype, kind="Internal"):
        return self.nc.dram_tensor(name, list(shape), dtype, kind=kind)

    def _uname(self, name):
        if not hasattr(self, "_names"):
            self._names = set()
        if name in self._names:
            self._uid += 1
            name = f"{name}_u{self._uid}"
        self._names.add(name)
        return name

    def sbuf(self, name, shape, dtype, scoped=True):
        st = self._scopes[-1] if (scoped and self._scopes) else self.stack
        h = st.enter_context(self.nc.sbuf_tensor(self._uname(name), list(shape), dtype))
        return h

    def sbuf_p(self, name, shape, dtype):
        return self.stack.enter_context(
            self.nc.sbuf_tensor(self._uname(name), list(shape), dtype))

    def psum(self, name, shape, dtype=F32):
        st = self._scopes[-1] if self._scopes else self.stack
        return st.enter_context(self.nc.psum_tensor(self._uname(name), list(shape), dtype))

    def sem(self, name):
        return self.stack.enter_context(self.nc.semaphore(name))

    def const_col(self, value, dtype=F32):
        key = ("cc", float(value), dtype)
        if not hasattr(self, "_ccols"):
            self._ccols = {}
        if key not in self._ccols:
            self._uid += 1
            name = f"ccol{self._uid}"
            h = self.sbuf_p(name, [128, 1], dtype)
            self.op("pool", lambda g, h=h, value=value: g.memset(h[:, :], value),
                    reads=(), writes=(name,), tag="ccol")
            self._ccols[key] = (h, name)
        return self._ccols[key]

    def op(self, eng, fn, reads=(), writes=(), tag=""):
        r = Rec(eng, fn, tuple(reads), tuple(writes), None, tag)
        r.extra = self._fence
        self.recs.append(r)
        return r

    def dma(self, eng, fn, reads=(), writes=(), tag=""):
        r = Rec(eng, fn, tuple(reads), tuple(writes), eng, tag)
        r.extra = self._fence
        if not hasattr(self, "_qrr"):
            self._qrr = {}
        self._qrr[eng] = self._qrr.get(eng, 0) + 1
        r.lane = (self._qrr[eng] - 1) % 8
        self.recs.append(r)
        return r

    def fence(self):
        lastper = {}
        for r in self.recs:
            key = (r.eng, r.queue, getattr(r, "lane", None))
            lastper[key] = r
        self._fence = tuple(lastper.values())

    def finalize(self):
        last_w = {}
        readers = {}
        def needs_edge(prod, cons):
            if prod.queue is not None:
                return True
            if prod.eng != cons.eng:
                return True
            return prod.eng != "pe" or cons.queue is not None

        for i, r in enumerate(self.recs):
            r.idx = i
            deps = set()
            for k in r.reads:
                lw = last_w.get(k)
                if lw is not None and needs_edge(lw, r):
                    deps.add(lw)
            for k in r.writes:
                lw = last_w.get(k)
                if lw is not None and needs_edge(lw, r):
                    deps.add(lw)
                for rd in readers.get(k, ()):
                    if rd is not r and needs_edge(rd, r):
                        deps.add(rd)
            for fr in r.extra:
                if fr is not r and fr.idx < i and needs_edge(fr, r):
                    deps.add(fr)
            for k in r.reads:
                readers.setdefault(k, []).append(r)
            for k in r.writes:
                last_w[k] = r
                readers[k] = []
            r.deps = deps
            for d in deps:
                if d.queue is None:
                    d.inc = True
        NLANES = 8
        eng_inc_count = {e: 0 for e in ENGS}
        lane_count = {}
        lane_last = {}
        inc_val = {}
        dma_key = {}
        for r in self.recs:
            if r.queue is not None:
                lk = (r.queue, r.lane)
                prev = lane_last.get(lk)
                if prev is not None:
                    r.deps.add(prev)
                lane_last[lk] = r
                lane_count[lk] = lane_count.get(lk, 0) + 1
                dma_key[r.idx] = (lk, lane_count[lk] * 16)
            elif r.inc:
                eng_inc_count[r.eng] += 1
                inc_val[r.idx] = eng_inc_count[r.eng]

        nc = self.nc
        esems = {e: self.stack.enter_context(nc.semaphore(f"se_{e}")) for e in ENGS}
        dsems = {lk: self.stack.enter_context(nc.semaphore(f"sd_{lk[0]}_{lk[1]}"))
                 for lk in lane_count}

        watermark = {e: {} for e in ENGS}
        for r in self.recs:
            needs = {}
            for d in r.deps:
                if d.queue is not None:
                    semk, v = dma_key[d.idx]
                else:
                    semk = d.eng
                    v = inc_val[d.idx]
                if needs.get(semk, 0) < v:
                    needs[semk] = v
            wm = watermark[r.eng]
            waits = []
            for semk, v in needs.items():
                if wm.get(semk, 0) < v:
                    wm[semk] = v
                    waits.append((semk, v))
            r.waits = waits

        by_eng = {e: [r for r in self.recs if r.eng == e] for e in ENGS}

        def replay(eng, proxy):
            for r in by_eng[eng]:
                for semk, v in r.waits:
                    proxy.wait_ge(dsems[semk] if isinstance(semk, tuple)
                                  else esems[semk], v)
                inst = r.fn(proxy)
                if r.queue is not None:
                    inst.then_inc(dsems[dma_key[r.idx][0]], 16)
                elif r.inc:
                    inst.then_inc(esems[r.eng], 1)

        with nc.Block() as block:
            @block.tensor
            def _(t):
                replay("pe", t)

            @block.vector
            def _(v):
                replay("dve", v)

            @block.scalar
            def _(s):
                replay("act", s)

            @block.gpsimd
            def _(g):
                replay("pool", g)

            @block.sync
            def _(sy):
                replay("sp", sy)
                for lk, cnt in lane_count.items():
                    sy.wait_ge(dsems[lk], 16 * cnt)

        self.stack.close()
        return self.nc



def emit_consts(p: Prog):
    c = {}
    c["ident"] = p.sbuf_p("c_ident", [128, 128], BF16)
    c["ones_col"] = p.sbuf_p("c_ones_col", [128, 1], BF16)
    c["ones_row"] = p.sbuf_p("c_ones_row", [1, 128], BF16)
    c["ones_row_f"] = p.sbuf_p("c_ones_row_f", [1, 128], F32)
    c["masks"] = p.sbuf_p("c_masks", [128, 4 * 512], BF16)

    p.op("pool", lambda g: g.memset(c["ident"][:, :], 1.0),
         writes=("c_identraw",), tag="consts")
    p.op("pool", lambda g: g.memset(c["ones_col"][:, :], 1.0),
         writes=("c_ones",), tag="consts")
    p.op("pool", lambda g: g.memset(c["ones_row"][:, :], 1.0),
         writes=("c_ones",), tag="consts")
    p.op("pool", lambda g: g.memset(c["ones_row_f"][:, :], 1.0),
         writes=("c_ones",), tag="consts")
    p.op("pool", lambda g: g.memset(c["masks"][:, :], 1.0),
         writes=("c_masksraw",), tag="consts")
    p.op("pool", lambda g: g.affine_select(c["ident"][:, :], c["ident"][:, :],
                                           [[1, 128]], OP.is_equal, 0.0,
                                           channel_multiplier=-1),
         reads=("c_identraw",), writes=("c_ident",), tag="consts")
    for k in range(4):
        def f_mask(g, k=k):
            return g.affine_select(c["masks"][:, 512 * k:512 * (k + 1)],
                                   c["masks"][:, 512 * k:512 * (k + 1)],
                                   [[1, 512]], OP.is_ge, 0.0,
                                   base=-128 * k, channel_multiplier=-1)
        p.op("pool", f_mask, reads=("c_masksraw",), writes=("c_masks",), tag="mask")
    return c


def emit_ln_to_featmajor(p: Prog, c, xsrc_ap_fn, n_tiles, dst_dram, dst_col0,
                         pfx, eps=1e-5, extra_reads=()):
    xt = [p.sbuf(f"{pfx}_x{i}", [128, DMODEL], F32) for i in range(2)]
    sq = p.sbuf(f"{pfx}_sq", [128, DMODEL], F32)
    s1 = p.sbuf(f"{pfx}_s1", [128, 1], F32)
    s2 = p.sbuf(f"{pfx}_s2", [128, 1], F32)
    mu = p.sbuf(f"{pfx}_mu", [128, 1], F32)
    msq = p.sbuf(f"{pfx}_msq", [128, 1], F32)
    var = p.sbuf(f"{pfx}_var", [128, 1], F32)
    stdv = p.sbuf(f"{pfx}_std", [128, 1], F32)
    rstd = p.sbuf(f"{pfx}_rstd", [128, 1], F32)
    ln = p.sbuf(f"{pfx}_ln", [128, DMODEL], BF16)
    stg = [p.sbuf(f"{pfx}_stg{i}", [128, DMODEL], BF16) for i in range(2)]
    pst = [p.psum(f"{pfx}_pst{i}", [128, 128], BF16) for i in range(2)]

    for tt in range(n_tiles):
        xb = xt[tt % 2]
        kx = f"{pfx}_x{tt % 2}"
        p.dma("sp", lambda sy, tt=tt, xb=xb: sy.dma_start(xb[:, :], xsrc_ap_fn(tt)),
              reads=extra_reads, writes=(kx,), tag=f"{pfx}_ldx{tt}")
        p.op("dve", lambda v, xb=xb: v.reduce_sum(s1[:, :], xb[:, :], axis=AX.X),
             reads=(kx,), writes=(f"{pfx}_s1",))
        p.op("act", lambda s, xb=xb: s.activation(sq[:, :], xb[:, :], AF.Square),
             reads=(kx,), writes=(f"{pfx}_sq",))
        p.op("dve", lambda v: v.reduce_sum(s2[:, :], sq[:, :], axis=AX.X),
             reads=(f"{pfx}_sq",), writes=(f"{pfx}_s2",))
        p.op("dve", lambda v: v.tensor_scalar_mul(mu[:, :], s1[:, :], 1.0 / DMODEL),
             reads=(f"{pfx}_s1",), writes=(f"{pfx}_mu",))
        p.op("dve", lambda v: v.tensor_mul(msq[:, :], mu[:, :], mu[:, :]),
             reads=(f"{pfx}_mu",), writes=(f"{pfx}_msq",))
        p.op("dve", lambda v: v.scalar_tensor_tensor(
            var[:, :], s2[:, :], 1.0 / DMODEL, msq[:, :], op0=OP.mult, op1=OP.subtract),
            reads=(f"{pfx}_s2", f"{pfx}_msq"), writes=(f"{pfx}_var",))
        epsc, epsk = p.const_col(eps)
        p.op("act", lambda s, epsc=epsc: s.activation(stdv[:, :], var[:, :], AF.Sqrt,
                                                      bias=epsc[:, :]),
             reads=(f"{pfx}_var", epsk), writes=(f"{pfx}_std",))
        p.op("dve", lambda v: v.reciprocal(rstd[:, :], stdv[:, :]),
             reads=(f"{pfx}_std",), writes=(f"{pfx}_rstd",))
        p.op("dve", lambda v, xb=xb: v.tensor_scalar(
            ln[:, :], xb[:, :], mu[:, :], rstd[:, :], op0=OP.subtract, op1=OP.mult),
            reads=(kx, f"{pfx}_mu", f"{pfx}_rstd"), writes=(f"{pfx}_ln",))
        st = stg[tt % 2]
        kst = f"{pfx}_stg{tt % 2}"
        for hc in range(16):
            pb_ = pst[hc % 2]
            kpb_ = f"{pfx}_pst{hc % 2}"
            p.op("pe", lambda t, hc=hc, pb_=pb_: t.transpose(
                pb_[:, :], ln[:, 128 * hc:128 * (hc + 1)], c["ident"][:, :]),
                reads=(f"{pfx}_ln", "c_ident"), writes=(kpb_,))
            p.op("dve", lambda v, hc=hc, st=st, pb_=pb_: v.tensor_copy(
                st[:, 128 * hc:128 * (hc + 1)], pb_[:, :]),
                reads=(kpb_,), writes=(kst,))
        def f_st(sy, tt=tt, st=st):
            W = dst_dram.shape[1]
            dst = bass.AP(dst_dram, (dst_col0 + 128 * tt) + 0,
                          [[W, 128], [128 * W, 16], [1, 128]])
            src = bass.AP(st, 0, [[DMODEL, 128], [128, 16], [1, 128]])
            return sy.dma_start(dst, src)
        p.dma("act", f_st, reads=(kst,), writes=(f"{pfx}_all",),
              tag=f"{pfx}_st{tt}")
    return ln



def build_launch_A(debug=False, p=None, io=None):
    standalone = p is None
    if standalone:
        p = Prog("A")
        xb_d = p.dram("xb", [S, DMODEL], F32, kind="ExternalInput")
        wsl_d = p.dram("wsl", [DMODEL, 1536], F32, kind="ExternalInput")
        oTn_d = p.dram("oTn", [512, S], BF16, kind="ExternalOutput")
        lnT_d = p.dram("lnT", [DMODEL, S], BF16)
        a2a_blocks = False
    else:
        xb_d, wsl_d, oTn_d, lnT_d = io["xb"], io["wsl"], io["oTn"], io["lnT"]
        a2a_blocks = False
    if debug:
        dbg_lnT = p.dram("dbg_lnT", [DMODEL, S], BF16, kind="ExternalOutput")
        dbg_q = p.dram("dbg_q", [128, S], BF16, kind="ExternalOutput")
        dbg_v = p.dram("dbg_v", [128, S], BF16, kind="ExternalOutput")
        dbg_n2 = p.dram("dbg_n2", [1, 2 * S], F32, kind="ExternalOutput")
        dbg_negMb = p.dram("dbg_negMb", [128, 1], F32, kind="ExternalOutput")
        dbg_den = p.dram("dbg_den", [1, 512], F32, kind="ExternalOutput")
        dbg_eT = p.dram("dbg_eT", [128, 512], BF16, kind="ExternalOutput")

    c = io["consts"] if io else emit_consts(p)

    emit_ln_to_featmajor(p, c, lambda tt: xb_d[128 * tt:128 * (tt + 1), :],
                         S // 128, lnT_d, 0, "ln0")

    wq = p.sbuf("wq", [128, DMODEL], BF16)
    wk = p.sbuf("wk", [128, DMODEL], BF16)
    wv = p.sbuf("wv", [128, DMODEL], BF16)
    lnG = p.sbuf("lnG", [128, 16 * 512], BF16)
    qsb = p.sbuf("qsb", [128, S], BF16)
    ksb = p.sbuf("ksb", [128, S], BF16)
    vsb = p.sbuf("vsb", [128, S], BF16)
    sqs = p.sbuf("sqs", [128, S], BF16)
    n2 = p.sbuf("n2", [1, 2 * S], F32)
    mx = p.sbuf("mx", [1, 2], F32)
    mprod = p.sbuf("mprod", [1, 1], F32)
    mroot = p.sbuf("mroot", [1, 1], F32)
    negM = p.sbuf("negM", [1, 1], F32)
    negMb = p.sbuf("negMb", [128, 1], F32)
    eT = [p.sbuf(f"eT{i}", [128, 512], BF16) for i in range(2)]
    den = p.sbuf("den", [1, 512], F32)
    denr = p.sbuf("denr", [1, 512], F32)
    o_un = p.sbuf("o_un", [128, 512], BF16)
    oTn = p.sbuf("oTn_sb", [128, S], BF16)

    psQ = p.psum("psQ", [128, 512])
    psS = [p.psum(f"psS{i}", [128, 512]) for i in range(2)]
    psO = p.psum("psO", [128, 512])
    psD = p.psum("psD", [1, 512])
    psB = p.psum("psB", [128, 512])
    psV = psQ
    psN = psD

    n_span = S // 512

    for head in range(4):
        for wbuf, key, col0 in ((wq, "wq", 128 * head), (wk, "wk", 512 + 128 * head),
                                (wv, "wv", 1024 + 128 * head)):
            def f_ldw(g, wbuf=wbuf, col0=col0):
                src = bass.AP(wsl_d, col0, [[1536, 128], [128 * 1536, 16], [1, 128]])
                dst = bass.AP(wbuf, 0, [[DMODEL, 128], [128, 16], [1, 128]])
                return g.dma_start(dst, src)
            p.dma("pool", f_ldw, reads=(), writes=(key,), tag=f"ldw_h{head}")

        for sp in range(n_span):
            def f_ldln(sy, sp=sp):
                src = bass.AP(lnT_d, 512 * sp, [[S, 128], [128 * S, 16], [1, 512]])
                dst = bass.AP(lnG, 0, [[16 * 512, 128], [512, 16], [1, 512]])
                return sy.dma_start(dst, src)
            p.dma("sp", f_ldln, reads=("ln0_all",),
                  writes=("lnG",), tag=f"ldln{sp}")
            for (wbuf, wkey, dst, dkey) in ((wq, "wq", qsb, "qsb"), (wk, "wk", ksb, "ksb")):
                for hc in range(16):
                    p.op("pe", lambda t, hc=hc, sp=sp, wbuf=wbuf: t.matmul(
                        psQ[:, :], wbuf[:, 128 * hc:128 * (hc + 1)],
                        lnG[:, 512 * hc:512 * (hc + 1)],
                        start=(hc == 0), stop=(hc == 15)),
                        reads=(wkey, "lnG"), writes=("psQ",))
                p.op("dve", lambda v, sp=sp, dst=dst: v.tensor_copy(
                    dst[:, 512 * sp:512 * (sp + 1)], psQ[:, :]),
                    reads=("psQ",), writes=(dkey,))
            for tc in range(4):
                for hc in range(16):
                    p.op("pe", lambda t, hc=hc, sp=sp, tc=tc: t.matmul(
                        psV[:, 0:128],
                        lnG[:, 512 * hc + 128 * tc:512 * hc + 128 * (tc + 1)],
                        wv[:, 128 * hc:128 * (hc + 1)],
                        start=(hc == 0), stop=(hc == 15)),
                        reads=("wv", "lnG"), writes=("psQ",))
                p.op("dve", lambda v, sp=sp, tc=tc: v.tensor_copy(
                    vsb[:, 512 * sp + 128 * tc:512 * sp + 128 * (tc + 1)], psV[:, 0:128]),
                    reads=("psQ",), writes=("vsb",))

        for (src, skey, off) in ((qsb, "qsb", 0), (ksb, "ksb", S)):
            p.op("act", lambda s, src=src: s.activation(sqs[:, :], src[:, :], AF.Square),
                 reads=(skey,), writes=("sqs",))
            for sp in range(n_span):
                p.op("pe", lambda t, sp=sp: t.matmul(
                    psN[:, :], c["ones_col"][:, :], sqs[:, 512 * sp:512 * (sp + 1)],
                    start=True, stop=True),
                    reads=("sqs", "c_ones"), writes=("psD",))
                p.op("dve", lambda v, sp=sp, off=off: v.tensor_copy(
                    n2[:, off + 512 * sp:off + 512 * (sp + 1)], psN[:, :]),
                    reads=("psD",), writes=("n2",))
        p.op("dve", lambda v: v.reduce_max(mx[:, 0:1], n2[:, 0:S], axis=AX.X),
             reads=("n2",), writes=("mx",))
        p.op("dve", lambda v: v.reduce_max(mx[:, 1:2], n2[:, S:2 * S], axis=AX.X),
             reads=("n2",), writes=("mx",))
        p.op("dve", lambda v: v.tensor_mul(mprod[:, :], mx[:, 0:1], mx[:, 1:2]),
             reads=("mx",), writes=("mprod",))
        p.op("act", lambda s: s.activation(mroot[:, :], mprod[:, :], AF.Sqrt,
                                           scale=1.0 / 128.0),
             reads=("mprod",), writes=("mroot",))
        p.op("dve", lambda v: v.tensor_scalar_mul(negM[:, :], mroot[:, :], -1.0),
             reads=("mroot",), writes=("negM",))
        p.op("pe", lambda t: t.matmul(psB[:, 0:1], c["ones_row_f"][:, :], negM[:, :],
                                      start=True, stop=True),
             reads=("negM", "c_ones"), writes=("psB",))
        p.op("dve", lambda v: v.tensor_copy(negMb[:, :], psB[:, 0:1]),
             reads=("psB",), writes=("negMb",))

        for sp in range(n_span):
            n_ch = 4 * (sp + 1)
            pend = []
            for ch in range(n_ch):
                pss = psS[ch % 2]
                kps = f"psS{ch % 2}"
                ket = f"eT{ch % 2}"
                ebuf = eT[ch % 2]
                p.op("pe", lambda t, ch=ch, sp=sp, pss=pss: t.matmul(
                    pss[:, :], ksb[:, 128 * ch:128 * (ch + 1)],
                    qsb[:, 512 * sp:512 * (sp + 1)], start=True, stop=True),
                    reads=("ksb", "qsb"), writes=(kps,))
                p.op("act", lambda s, pss=pss, ebuf=ebuf: s.activation(
                    ebuf[:, :], pss[:, :], AF.Exp, bias=negMb[:, :],
                    scale=INV_SQRT_D0),
                    reads=(kps, "negMb"), writes=(ket,))
                off = 128 * ch - 512 * sp
                if off >= 0:
                    k = off // 128
                    p.op("dve", lambda v, ebuf=ebuf, k=k: v.tensor_mul(
                        ebuf[:, :], ebuf[:, :], c["masks"][:, 512 * k:512 * (k + 1)]),
                        reads=(ket, "c_masks"), writes=(ket,))
                def f_acc(ch=ch, ebuf=ebuf, ket=ket, n_ch=n_ch):
                    p.op("pe", lambda t, ch=ch, ebuf=ebuf, n_ch=n_ch: t.matmul(
                        psD[:, :], c["ones_col"][:, :], ebuf[:, :],
                        start=(ch == 0), stop=(ch == n_ch - 1)),
                        reads=(ket, "c_ones"), writes=("psD",))
                    p.op("pe", lambda t, ch=ch, ebuf=ebuf, n_ch=n_ch: t.matmul(
                        psO[:, :], vsb[:, 128 * ch:128 * (ch + 1)], ebuf[:, :],
                        start=(ch == 0), stop=(ch == n_ch - 1)),
                        reads=(ket, "vsb"), writes=("psO",))
                for f in pend:
                    f()
                pend = [f_acc]
            for f in pend:
                f()
            p.op("dve", lambda v: v.tensor_copy(den[:, :], psD[:, :]),
                 reads=("psD",), writes=("den",))
            p.op("dve", lambda v: v.reciprocal(denr[:, :], den[:, :]),
                 reads=("den",), writes=("denr",))
            p.op("pe", lambda t: t.matmul(psB[:, :], c["ones_row_f"][:, :], denr[:, :],
                                          start=True, stop=True),
                 reads=("denr", "c_ones"), writes=("psB",))
            p.op("dve", lambda v: v.tensor_copy(o_un[:, :], psO[:, :]),
                 reads=("psO",), writes=("o_un",))
            p.op("dve", lambda v, sp=sp: v.tensor_mul(
                oTn[:, 512 * sp:512 * (sp + 1)], o_un[:, :], psB[:, :]),
                reads=("o_un", "psB"), writes=("oTn_sb",))
        if a2a_blocks:
            for jb in range(4):
                p.dma("sp", lambda sy, head=head, jb=jb: sy.dma_start(
                    oTn_d[512 * jb + 128 * head:512 * jb + 128 * (head + 1), :],
                    oTn[:, 1024 * jb:1024 * (jb + 1)]),
                    reads=("oTn_sb",), writes=(f"oTn_out{head}",), tag="st_o")
        else:
            p.dma("sp", lambda sy, head=head: sy.dma_start(
                oTn_d[128 * head:128 * (head + 1), :], oTn[:, :]),
                reads=("oTn_sb",), writes=(f"oTn_out{head}",), tag=f"st_o{head}")
        if debug and head == 0:
            p.dma("sp", lambda sy: sy.dma_start(dbg_lnT[:, :], lnT_d[:, :]),
                  reads=("ln0_all",), writes=("dbg1",))
            p.dma("sp", lambda sy: sy.dma_start(dbg_q[:, :], qsb[:, :]),
                  reads=("qsb",), writes=("dbg2",))
            p.dma("sp", lambda sy: sy.dma_start(dbg_v[:, :], vsb[:, :]),
                  reads=("vsb",), writes=("dbg3",))
            p.dma("sp", lambda sy: sy.dma_start(dbg_n2[:, :], n2[:, :]),
                  reads=("n2",), writes=("dbg4",))
            p.dma("sp", lambda sy: sy.dma_start(dbg_negMb[:, :], negMb[:, :]),
                  reads=("negMb",), writes=("dbg5",))
            p.dma("sp", lambda sy: sy.dma_start(dbg_den[:, :], den[:, :]),
                  reads=("den",), writes=("dbg6",))
            p.dma("sp", lambda sy: sy.dma_start(dbg_eT[:, :], eT[1][:, :]),
                  reads=("eT1",), writes=("dbg7",))
    if standalone:
        return p.finalize()



def emit_ln_from_sbuf(p, c, get_tile, n_tiles, dst_dram, dst_col0, pfx,
                      eps=1e-5, post_tile=None):
    sq = p.sbuf(f"{pfx}_sq", [128, DMODEL], F32)
    s1 = p.sbuf(f"{pfx}_s1", [128, 1], F32)
    s2 = p.sbuf(f"{pfx}_s2", [128, 1], F32)
    mu = p.sbuf(f"{pfx}_mu", [128, 1], F32)
    msq = p.sbuf(f"{pfx}_msq", [128, 1], F32)
    var = p.sbuf(f"{pfx}_var", [128, 1], F32)
    stdv = p.sbuf(f"{pfx}_std", [128, 1], F32)
    rstd = p.sbuf(f"{pfx}_rstd", [128, 1], F32)
    ln = p.sbuf(f"{pfx}_ln", [128, DMODEL], BF16)
    stg = [p.sbuf(f"{pfx}_stg{i}", [128, DMODEL], BF16) for i in range(2)]
    pst = [p.psum(f"{pfx}_pst{i}", [128, 128], BF16) for i in range(2)]
    epsc, epsk = p.const_col(eps)

    for tt in range(n_tiles):
        xb, kx = get_tile(tt)
        p.op("dve", lambda v, xb=xb: v.reduce_sum(s1[:, :], xb[:, :], axis=AX.X),
             reads=(kx,), writes=(f"{pfx}_s1",))
        p.op("act", lambda s, xb=xb: s.activation(sq[:, :], xb[:, :], AF.Square),
             reads=(kx,), writes=(f"{pfx}_sq",))
        p.op("dve", lambda v: v.reduce_sum(s2[:, :], sq[:, :], axis=AX.X),
             reads=(f"{pfx}_sq",), writes=(f"{pfx}_s2",))
        p.op("dve", lambda v: v.tensor_scalar_mul(mu[:, :], s1[:, :], 1.0 / DMODEL),
             reads=(f"{pfx}_s1",), writes=(f"{pfx}_mu",))
        p.op("dve", lambda v: v.tensor_mul(msq[:, :], mu[:, :], mu[:, :]),
             reads=(f"{pfx}_mu",), writes=(f"{pfx}_msq",))
        p.op("dve", lambda v: v.scalar_tensor_tensor(
            var[:, :], s2[:, :], 1.0 / DMODEL, msq[:, :], op0=OP.mult, op1=OP.subtract),
            reads=(f"{pfx}_s2", f"{pfx}_msq"), writes=(f"{pfx}_var",))
        p.op("act", lambda s: s.activation(stdv[:, :], var[:, :], AF.Sqrt,
                                           bias=epsc[:, :]),
             reads=(f"{pfx}_var", epsk), writes=(f"{pfx}_std",))
        p.op("dve", lambda v: v.reciprocal(rstd[:, :], stdv[:, :]),
             reads=(f"{pfx}_std",), writes=(f"{pfx}_rstd",))
        p.op("dve", lambda v, xb=xb: v.tensor_scalar(
            ln[:, :], xb[:, :], mu[:, :], rstd[:, :], op0=OP.subtract, op1=OP.mult),
            reads=(kx, f"{pfx}_mu", f"{pfx}_rstd"), writes=(f"{pfx}_ln",))
        st = stg[tt % 2]
        kst = f"{pfx}_stg{tt % 2}"
        for hc in range(16):
            pb_ = pst[hc % 2]
            kpb_ = f"{pfx}_pst{hc % 2}"
            p.op("pe", lambda t, hc=hc, pb_=pb_: t.transpose(
                pb_[:, :], ln[:, 128 * hc:128 * (hc + 1)], c["ident"][:, :]),
                reads=(f"{pfx}_ln", "c_ident"), writes=(kpb_,))
            p.op("dve", lambda v, hc=hc, st=st, pb_=pb_: v.tensor_copy(
                st[:, 128 * hc:128 * (hc + 1)], pb_[:, :]),
                reads=(kpb_,), writes=(kst,))
        def f_st(sy, tt=tt, st=st):
            W = dst_dram.shape[1]
            dst = bass.AP(dst_dram, (dst_col0 + 128 * tt) + 0,
                          [[W, 128], [128 * W, 16], [1, 128]])
            src2 = bass.AP(st, 0, [[DMODEL, 128], [128, 16], [1, 128]])
            return sy.dma_start(dst, src2)
        p.dma("act", f_st, reads=(kst,), writes=(f"{pfx}_all",), tag=f"{pfx}_st{tt}")
        if post_tile is not None:
            post_tile(tt, st, kst)


def build_launch_B(p=None, io=None):
    standalone = p is None
    with_router = standalone
    if standalone:
        p = Prog("B")
        xsl_d = p.dram("xsl", [TOK_PER_CORE, DMODEL], F32, kind="ExternalInput")
        gosl_d = p.dram("gosl", [DMODEL, TOK_PER_CORE], BF16, kind="ExternalInput")
        woT_d = p.dram("woT", [DMODEL, DMODEL], F32, kind="ExternalInput")
        gwT_d = p.dram("gwT", [DMODEL, 16], F32, kind="ExternalInput")
        x1_d = p.dram("x1", [TOK_PER_CORE, DMODEL], F32, kind="ExternalOutput")
        ln1T_d = p.dram("ln1T", [DMODEL, TOK_PER_CORE], BF16, kind="ExternalOutput")
        comb_d = p.dram("comb", [TOK_PER_CORE, 16], F32, kind="ExternalOutput")
        c = emit_consts(p)
    else:
        xsl_d, gosl_d, woT_d = io["xsl"], io["gosl"], io["woT"]
        x1_d, ln1T_d = io["x1"], io["ln1T"]
        c = io["consts"]
    NT = TOK_PER_CORE // 128

    go = p.sbuf("go", [128, 16 * TOK_PER_CORE // 1], BF16)
    xs = [p.sbuf(f"xs{i}", [128, DMODEL], F32) for i in range(NT)]
    wo = [p.sbuf(f"wo{i}", [128, 16 * 512], BF16) for i in range(2)]
    gw = p.sbuf("gw", [128, 16 * 16], BF16)
    psO = p.psum("psO", [128, 512])
    psL = p.psum("psL", [128, 16])

    if standalone:
        def f_go(sy):
            src = bass.AP(gosl_d, 0, [[TOK_PER_CORE, 128], [128 * TOK_PER_CORE, 16],
                                      [1, TOK_PER_CORE]])
            dst = bass.AP(go, 0, [[16 * TOK_PER_CORE, 128], [TOK_PER_CORE, 16],
                                  [1, TOK_PER_CORE]])
            return sy.dma_start(dst, src)
    else:
        _pidc = {}
        def f_go(sy):
            if "c" not in _pidc:
                _pidc["c"] = (sy.partition_id() % 4) * 1024
            src = bass.AP(gosl_d, _pidc["c"], [[S, 128], [128 * S, 16],
                                               [1, TOK_PER_CORE]])
            dst = bass.AP(go, 0, [[16 * TOK_PER_CORE, 128], [TOK_PER_CORE, 16],
                                  [1, TOK_PER_CORE]])
            return sy.dma_start(dst, src)
    p.dma("sp", f_go, writes=("go",), tag="ldgo")
    for tt in range(NT):
        p.dma("sp", lambda sy, tt=tt: sy.dma_start(
            xs[tt][:, :], xsl_d[128 * tt:128 * (tt + 1), :]),
            writes=(f"xs{tt}",), tag="ldx")
    if with_router:
        def f_gw(g):
            src = bass.AP(gwT_d, 0, [[16, 128], [128 * 16, 16], [1, 16]])
            dst = bass.AP(gw, 0, [[16 * 16, 128], [16, 16], [1, 16]])
            return g.dma_start(dst, src)
        p.dma("pool", f_gw, writes=("gw",), tag="ldgw")

    for fb in range(4):
        wob = wo[fb % 2]
        kwo = f"wo{fb % 2}"
        def f_wo(g, fb=fb, wob=wob):
            src = bass.AP(woT_d, 512 * fb, [[DMODEL, 128], [128 * DMODEL, 16], [1, 512]])
            dst = bass.AP(wob, 0, [[16 * 512, 128], [512, 16], [1, 512]])
            return g.dma_start(dst, src)
        p.dma("pool", f_wo, writes=(kwo,), tag=f"ldwo{fb}")
        for tt in range(NT):
            for hc in range(16):
                p.op("pe", lambda t, hc=hc, tt=tt, wob=wob: t.matmul(
                    psO[:, :], go[:, TOK_PER_CORE * hc + 128 * tt:
                                  TOK_PER_CORE * hc + 128 * (tt + 1)],
                    wob[:, 512 * hc:512 * (hc + 1)],
                    start=(hc == 0), stop=(hc == 15)),
                    reads=("go", kwo), writes=("psO",))
            p.op("dve", lambda v, tt=tt, fb=fb: v.tensor_add(
                xs[tt][:, 512 * fb:512 * (fb + 1)],
                xs[tt][:, 512 * fb:512 * (fb + 1)], psO[:, :]),
                reads=("psO", f"xs{tt}"), writes=(f"xs{tt}",))
    for tt in range(NT):
        p.dma("sp", lambda sy, tt=tt: sy.dma_start(
            x1_d[128 * tt:128 * (tt + 1), :], xs[tt][:, :]),
            reads=(f"xs{tt}",), writes=("x1o",), tag="stx1")

    lg = p.sbuf("lg", [128, 16], F32)
    ee = p.sbuf("ee", [128, 16], F32)
    ww = p.sbuf("ww", [128, 16], F32)
    macc = p.sbuf("macc", [128, 16], F32)
    mask1 = p.sbuf("mask1", [128, 16], F32)
    mrow = p.sbuf("mrow", [128, 1], F32)
    negm = p.sbuf("negm", [128, 1], F32)
    s4 = p.sbuf("s4", [128, 1], F32)
    r4 = p.sbuf("r4", [128, 1], F32)
    combt = p.sbuf("combt", [128, 16], F32)

    def router(tt, st, kst):
        for hc in range(16):
            p.op("pe", lambda t, hc=hc, st=st: t.matmul(
                psL[:, :], st[:, 128 * hc:128 * (hc + 1)],
                gw[:, 16 * hc:16 * (hc + 1)],
                start=(hc == 0), stop=(hc == 15)),
                reads=(kst, "gw"), writes=("psL",))
        p.op("dve", lambda v: v.tensor_copy(lg[:, :], psL[:, :]),
             reads=("psL",), writes=("lg",))
        p.op("dve", lambda v: v.reduce_max(negm[:, :], lg[:, :], axis=AX.X,
                                           negate=True),
             reads=("lg",), writes=("negm",))
        p.op("act", lambda s: s.activation(ee[:, :], lg[:, :], AF.Exp,
                                           bias=negm[:, :]),
             reads=("lg", "negm"), writes=("ee",))
        p.op("dve", lambda v: v.tensor_copy(ww[:, :], ee[:, :]),
             reads=("ee",), writes=("ww",))
        p.op("pool", lambda g: g.memset(macc[:, :], 0.0),
             writes=("macc",))
        for it in range(4):
            p.op("dve", lambda v: v.reduce_max(mrow[:, :], ww[:, :], axis=AX.X),
                 reads=("ww",), writes=("mrow",))
            p.op("dve", lambda v: v.tensor_single_scalar(
                mask1[:, :], ww[:, :], mrow[:, :], OP.is_ge),
                reads=("ww", "mrow"), writes=("mask1",))
            p.op("dve", lambda v: v.tensor_max(macc[:, :], macc[:, :], mask1[:, :]),
                 reads=("macc", "mask1"), writes=("macc",))
            if it < 3:
                p.op("dve", lambda v: v.scalar_tensor_tensor(
                    ww[:, :], mask1[:, :], -1e30, ww[:, :], op0=OP.mult, op1=OP.add),
                    reads=("mask1", "ww"), writes=("ww",))
        p.op("dve", lambda v: v.tensor_mul(combt[:, :], macc[:, :], ee[:, :]),
             reads=("macc", "ee"), writes=("combt",))
        p.op("dve", lambda v: v.reduce_sum(s4[:, :], combt[:, :], axis=AX.X),
             reads=("combt",), writes=("s4",))
        p.op("dve", lambda v: v.reciprocal(r4[:, :], s4[:, :]),
             reads=("s4",), writes=("r4",))
        p.op("dve", lambda v: v.tensor_scalar_mul(combt[:, :], combt[:, :],
                                                  r4[:, :]),
             reads=("combt", "r4"), writes=("combt",))
        p.dma("sp", lambda sy, tt=tt: sy.dma_start(
            comb_d[128 * tt:128 * (tt + 1), :], combt[:, :]),
            reads=("combt",), writes=("combo",), tag="stcomb")

    emit_ln_from_sbuf(p, c, lambda tt: (xs[tt], f"xs{tt}"), NT, ln1T_d, 0,
                      "ln1", post_tile=router if with_router else None)
    if standalone:
        return p.finalize()



def emit_consts_min(p):
    c = {}
    c["ones_col"] = p.sbuf("c_ones_col", [128, 1], BF16)
    c["ones_row_f"] = p.sbuf("c_ones_row_f", [1, 128], F32)
    p.op("pool", lambda g: g.memset(c["ones_col"][:, :], 1.0), writes=("c_ones",))
    p.op("pool", lambda g: g.memset(c["ones_row_f"][:, :], 1.0), writes=("c_ones",))
    return c


def emit_dq_strip(p, wt, kwt, wt_off, stile, kst, st_off, shift, width,
                  final_writes, psub=None):
    t = p._dq_tmp
    lo, hi = (0, 128) if psub is None else psub
    P = slice(lo, hi)
    n2c, n2k = p.const_col(-2.0)
    n6c, n6k = p.const_col(-6.0)
    W = width
    f, e8, d, r1, r2, m = t["f"], t["e8"], t["d"], t["r1"], t["r2"], t["m"]
    fi = t["fi"]
    q, G = t["f"], t["e8"]
    p.op("dve", lambda v: v.tensor_scalar(
        fi[P, 0:W], bass.AP(wt, lo * wt.shape[1] + wt_off,
                            [[wt.shape[1], hi - lo], [1, W]]),
        int(shift), 15, op0=OP.logical_shift_right, op1=OP.bitwise_and),
        reads=(kwt,), writes=("dq_fi",))
    p.op("dve", lambda v: v.tensor_copy(f[P, 0:W], fi[P, 0:W]),
         reads=("dq_fi",), writes=("dq_f",))
    p.op("dve", lambda v: v.tensor_scalar(
        e8[P, 0:W], f[P, 0:W], 8.0, 8.0, op0=OP.is_ge, op1=OP.mult),
        reads=("dq_f",), writes=("dq_e8",))
    p.op("dve", lambda v: v.scalar_tensor_tensor(
        d[P, 0:W], e8[P, 0:W], -1.0, f[P, 0:W], op0=OP.mult, op1=OP.add),
        reads=("dq_e8", "dq_f"), writes=("dq_d",))
    p.op("act", lambda s: s.activation(r1[P, 0:W], d[P, 0:W], AF.Relu,
                                       bias=n2c[P, :], scale=0.5),
         reads=("dq_d", n2k), writes=("dq_r1",))
    p.op("act", lambda s: s.activation(r2[P, 0:W], d[P, 0:W], AF.Relu,
                                       bias=n6c[P, :]),
         reads=("dq_d", n6k), writes=("dq_r2",))
    p.op("dve", lambda v: v.scalar_tensor_tensor(
        m[P, 0:W], d[P, 0:W], 0.5, r1[P, 0:W], op0=OP.mult, op1=OP.add),
        reads=("dq_d", "dq_r1"), writes=("dq_m",))
    p.op("dve", lambda v: v.tensor_add(m[P, 0:W], m[P, 0:W], r2[P, 0:W]),
         reads=("dq_m", "dq_r2"), writes=("dq_m",))
    p.op("dve", lambda v: v.tensor_scalar(
        q[P, 0:W], e8[P, 0:W], -0.25, 1.0, op0=OP.mult, op1=OP.add),
        reads=("dq_e8",), writes=("dq_q",))
    p.op("dve", lambda v: v.tensor_mul(
        G[P, 0:W], q[P, 0:W],
        bass.AP(stile, lo * stile.shape[1] + st_off,
                [[stile.shape[1], hi - lo], [1, W]])),
        reads=("dq_q", kst), writes=("dq_G",))
    for (src_ap_fn, out_ap_fn, out_key) in final_writes:
        p.op("dve", lambda v, src_ap_fn=src_ap_fn, out_ap_fn=out_ap_fn:
             v.tensor_mul(out_ap_fn(), src_ap_fn(m), src_ap_fn(G)),
             reads=("dq_m", "dq_G"), writes=(out_key,))


def emit_router_C(p, c, g1_d, gwTp_d, cbuf):
    gw = p.sbuf("gwC", [128, 16 * 16], BF16)
    lg = p.sbuf("lgC", [128, 16], F32)
    ee = p.sbuf("eeC", [128, 16], F32)
    ww = p.sbuf("wwC", [128, 16], F32)
    macc = p.sbuf("maccC", [128, 16], F32)
    mask1 = p.sbuf("mask1C", [128, 16], F32)
    mrow = p.sbuf("mrowC", [128, 1], F32)
    negm = p.sbuf("negmC", [128, 1], F32)
    s4 = p.sbuf("s4C", [128, 1], F32)
    r4 = p.sbuf("r4C", [128, 1], F32)
    combt = p.sbuf("combtC", [128, 16], F32)
    g1t = p.sbuf("g1tC", [128, 16 * 128], BF16)
    psR = p.psum("psRC", [128, 16])

    def f_gw(g):
        srcap = bass.AP(gwTp_d, 0, [[16, 128], [128 * 16, 16], [1, 16]])
        dst = bass.AP(gw, 0, [[16 * 16, 128], [16, 16], [1, 16]])
        return g.dma_start(dst, srcap)
    p.dma("pool", f_gw, writes=("gwC",), tag="ldgwC")
    for tt in range(NTOK // 128):
        r, tc = tt // 8, 128 * (tt % 8)
        def f_g1t(sy, r=r, tc=tc):
            srcap = bass.AP(g1_d, 2048 * r * 1024 + tc,
                            [[1024, 128], [128 * 1024, 16], [1, 128]])
            dst = bass.AP(g1t, 0, [[16 * 128, 128], [128, 16], [1, 128]])
            return sy.dma_start(dst, srcap)
        p.dma("sp", f_g1t, reads=("g1tC",), writes=("g1tC",), tag="ldg1t")
        for hc in range(16):
            p.op("pe", lambda t, hc=hc: t.matmul(
                psR[:, :], g1t[:, 128 * hc:128 * (hc + 1)],
                gw[:, 16 * hc:16 * (hc + 1)],
                start=(hc == 0), stop=(hc == 15)),
                reads=("g1tC", "gwC"), writes=("psRC",))
        p.op("dve", lambda v: v.tensor_copy(lg[:, :], psR[:, :]),
             reads=("psRC",), writes=("lgC",))
        p.op("dve", lambda v: v.reduce_max(negm[:, :], lg[:, :], axis=AX.X,
                                           negate=True),
             reads=("lgC",), writes=("negmC",))
        p.op("act", lambda s: s.activation(ee[:, :], lg[:, :], AF.Exp,
                                           bias=negm[:, :]),
             reads=("lgC", "negmC"), writes=("eeC",))
        p.op("dve", lambda v: v.tensor_copy(ww[:, :], ee[:, :]),
             reads=("eeC",), writes=("wwC",))
        p.op("pool", lambda g: g.memset(macc[:, :], 0.0), reads=("maccC",),
             writes=("maccC",))
        for it in range(4):
            p.op("dve", lambda v: v.reduce_max(mrow[:, :], ww[:, :], axis=AX.X),
                 reads=("wwC",), writes=("mrowC",))
            p.op("dve", lambda v: v.tensor_single_scalar(
                mask1[:, :], ww[:, :], mrow[:, :], OP.is_ge),
                reads=("wwC", "mrowC"), writes=("mask1C",))
            p.op("dve", lambda v: v.tensor_max(macc[:, :], macc[:, :], mask1[:, :]),
                 reads=("maccC", "mask1C"), writes=("maccC",))
            if it < 3:
                p.op("dve", lambda v: v.scalar_tensor_tensor(
                    ww[:, :], mask1[:, :], -1e30, ww[:, :], op0=OP.mult, op1=OP.add),
                    reads=("mask1C", "wwC"), writes=("wwC",))
        p.op("dve", lambda v: v.tensor_mul(combt[:, :], macc[:, :], ee[:, :]),
             reads=("maccC", "eeC"), writes=("combtC",))
        p.op("dve", lambda v: v.reduce_sum(s4[:, :], combt[:, :], axis=AX.X),
             reads=("combtC",), writes=("s4C",))
        p.op("dve", lambda v: v.reciprocal(r4[:, :], s4[:, :]),
             reads=("s4C",), writes=("r4C",))
        p.op("dve", lambda v, tt=tt: v.tensor_scalar_mul(
            cbuf[:, 2 * tt:2 * tt + 2], combt[:, 0:2], r4[:, :]),
            reads=("combtC", "r4C"), writes=("cbuf",))


def build_launch_C(p=None, io=None):
    standalone = p is None
    if standalone:
        p = Prog("C")
        g1_d = p.dram("g1", [DMODEL, NTOK], BF16, kind="ExternalInput")
        pkgu_d = p.dram("pkgu", [2, 256, 3072], I32, kind="ExternalInput")
        scgu_d = p.dram("scgu", [2, 16, 3072], F32, kind="ExternalInput")
        pkdn_d = p.dram("pkdn", [2, 192, 2048], I32, kind="ExternalInput")
        scdn_d = p.dram("scdn", [2, 12, 2048], F32, kind="ExternalInput")
        combsl_d = p.dram("combsl", [NTOK, 2], F32, kind="ExternalInput")
        pksguc_d = p.dram("pksguc", [256, 384], I32, kind="ExternalInput")
        scsguc_d = p.dram("scsguc", [16, 384], F32, kind="ExternalInput")
        pksdnc_d = p.dram("pksdnc", [192, 256], I32, kind="ExternalInput")
        scsdnc_d = p.dram("scsdnc", [12, 256], F32, kind="ExternalInput")
        partial_d = p.dram("partial", [NTOK, DMODEL], BF16, kind="ExternalOutput")
        sgslo_d = p.dram("sgslo", [2048, 384], BF16, kind="ExternalOutput")
        sdno_d = p.dram("sdno", [1536, 256], BF16, kind="ExternalOutput")
        wgu2_d = p.dram("wgu2", [2048, 1536], BF16)
        c = emit_consts_min(p)
        g1ag = False
    else:
        g1_d = io["g1"]
        pkgu_d, scgu_d = io["pkgu"], io["scgu"]
        pkdn_d, scdn_d = io["pkdn"], io["scdn"]
        pksgu_d, scsgu_d = io["pksgu"], io["scsgu"]
        pksdn_d, scsdn_d = io["pksdn"], io["scsdn"]
        gwTp_d = io["gwTp"]
        partial_d = io["partial"]
        sgu_loc_d, sdn_loc_d = io["sgu_loc"], io["sdn_loc"]
        wgu2_d = io["wgu2"]
        c = io["consts"]
        g1ag = True

    wgu = p.sbuf("wgu", [128, 16 * 1536], BF16)
    wdn = p.sbuf("wdn", [128, 12 * 2048], BF16)
    g1s = [p.sbuf(f"g1s{i}", [128, 16 * 512], BF16) for i in range(2)]
    acts = p.sbuf("acts", [128, 6 * 512], BF16)
    sgs = [p.sbuf(f"sg{i}", [128, 512], BF16) for i in range(2)]
    po = p.sbuf("po", [128, DMODEL], BF16)
    cbuf = p.sbuf("cbuf", [128, 64 * 2], F32)
    wt = p.sbuf("wt", [128, 1536], I32)
    wtb = p.sbuf("wtb", [128, 1536], I32)
    stg = p.sbuf("stg", [128, 3072], BF16)
    h2st = p.sbuf("h2st", [128, 1536], BF16)
    shst = p.sbuf("shst", [128, 1536], BF16)
    dqt = {}
    for nm in ("f", "e8", "d", "r1", "r2", "m"):
        dqt[nm] = p.sbuf(f"dq_{nm}", [128, 1536], BF16)
    dqt["fi"] = p.sbuf("dq_fi", [128, 1536], I32)
    p._dq_tmp = dqt

    psGs = [p.psum(f"psG{i}", [128, 512]) for i in range(2)]
    psUs = [p.psum(f"psU{i}", [128, 512]) for i in range(2)]
    psD1 = p.psum("psD1", [128, 512])
    psD2 = p.psum("psD2", [128, 512])

    if not g1ag:
        def f_cb(sy):
            srcap = bass.AP(combsl_d, 0, [[2, 128], [256, 64], [1, 2]])
            dst = bass.AP(cbuf, 0, [[128, 128], [2, 64], [1, 2]])
            return sy.dma_start(dst, srcap)
        p.dma("sp", f_cb, writes=("cbuf",), tag="ldcomb")
    else:
        emit_router_C(p, c, g1_d, gwTp_d, cbuf)

    n_span = NTOK // 512

    def gu_writes(strip, ch):
        du = 768 * strip
        TW = 1536
        def r1_src(lo, n_ic):
            return lambda buf: bass.AP(buf, lo, [[TW, 128], [8, 128], [1, n_ic]])
        def r1_out(dbuf, off, n_ic):
            def f():
                w = dbuf.shape[1]
                return bass.AP(dbuf, off, [[w, 128], [1, 128], [128, n_ic]])
            return f
        def r2_src():
            return lambda buf: bass.AP(buf, 1024, [[TW, 128], [8, 64], [2, 4], [1, 2]])
        def r2_out(dbuf, off):
            def f():
                w = dbuf.shape[1]
                return bass.AP(dbuf, off, [[w, 128], [1, 64], [128, 4], [64, 2]])
            return f
        return [
            (r1_src(0, 6), r1_out(wgu, 1536 * ch + du, 6), "wgu"),
            (r1_src(6, 2), r1_out(h2st, du, 2), "h2st"),
            (r2_src(), r2_out(h2st, du + 128 * 2), "h2st"),
        ]

    def dequant_gu(e):
        for k8b in range(2):
            for r_ in range(8):
                def f_sc(g, e=e, k8b=k8b, r_=r_):
                    return g.dma_start(
                        stg[16 * r_:16 * (r_ + 1), :],
                        scgu_d[e, 8 * k8b + r_:8 * k8b + r_ + 1, :]
                        .to_broadcast((16, 3072)))
                p.dma("pool", f_sc, reads=("stg",), writes=("stg",), tag="ldsc")
            for strip, wbuf, kw in ((0, wt, "wt"), (1, wtb, "wtb")):
                p.dma("pool", lambda g, e=e, k8b=k8b, strip=strip, wbuf=wbuf:
                      g.dma_start(
                    wbuf[:, :], pkgu_d[e, 128 * k8b:128 * (k8b + 1),
                                       1536 * strip:1536 * (strip + 1)]),
                    reads=(kw,), writes=(kw,), tag="ldwt")
            for j in range(8):
                ch = 2 * j + k8b
                for strip, wbuf, kw in ((0, wt, "wt"), (1, wtb, "wtb")):
                    emit_dq_strip(p, wbuf, kw, 0, stg, "stg",
                                  1536 * strip, 4 * j, 1536,
                                  gu_writes(strip, ch))
                p.dma("act", lambda sy, ch=ch: sy.dma_start(
                    wgu2_d[128 * ch:128 * (ch + 1), :], h2st[:, :]),
                    reads=("h2st",), writes=("wgu2",), tag="sth2")

    def dn_w(ic, lo=0, hi=128, cw=2048, dst=None, coff=0):
        dbuf = wdn if dst is None else dst
        def f():
            return bass.AP(dbuf, cw * 0 + dbuf.shape[1] * lo + 2048 * ic + coff
                           if dst is None else dbuf.shape[1] * lo + coff,
                           [[dbuf.shape[1], hi - lo], [1, cw]])
        return f

    def dequant_dn(e):
        p.dma("pool", lambda g, e=e: g.dma_start(
            wt[:, 0:1024], pkdn_d[e, 0:128, 0:1024]),
            reads=("wt",), writes=("wt",), tag="lddnw")
        p.dma("pool", lambda g, e=e: g.dma_start(
            wtb[:, 0:1024], pkdn_d[e, 0:128, 1024:2048]),
            reads=("wtb",), writes=("wtb",), tag="lddnw")
        for r_ in range(8):
            def f_scf(g, e=e, r_=r_):
                return g.dma_start(
                    stg[16 * r_:16 * (r_ + 1), 0:2048],
                    scdn_d[e, r_:r_ + 1, :].to_broadcast((16, 2048)))
            p.dma("pool", f_scf, reads=("stg",), writes=("stg",), tag="lddns")
        for ic in range(8):
            for st2, wbuf, kw in ((0, wt, "wt"), (1, wtb, "wtb")):
                fw = [(lambda buf: bass.AP(buf, 0, [[1536, 128], [1, 1024]]),
                       (lambda ic=ic, st2=st2: bass.AP(
                           wdn, 2048 * ic + 1024 * st2, [[12 * 2048, 128], [1, 1024]])),
                       "wdn")]
                emit_dq_strip(p, wbuf, kw, 0, stg, "stg", 1024 * st2,
                              4 * ic, 1024, fw)
        for st2, wbuf, kw in ((0, wt, "wt"), (1, wtb, "wtb")):
            for half in range(2):
                def f_wp(g, e=e, st2=st2, wbuf=wbuf, half=half):
                    return g.dma_start(
                        wbuf[64 * half:64 * (half + 1), 0:1024],
                        pkdn_d[e, 128:192, 1024 * st2:1024 * (st2 + 1)])
                p.dma("pool", f_wp, reads=(kw,), writes=(kw,), tag="lddnwp")
        for half in range(2):
            for r_ in range(4):
                def f_scp(g, e=e, half=half, r_=r_):
                    return g.dma_start(
                        stg[64 * half + 16 * r_:64 * half + 16 * (r_ + 1), 0:2048],
                        scdn_d[e, 8 + r_:9 + r_, :].to_broadcast((16, 2048)))
                p.dma("pool", f_scp, reads=("stg",), writes=("stg",), tag="lddnsp")
        for q2 in range(4):
            ic = 8 + q2
            for half, jj in ((0, 2 * q2), (1, 2 * q2 + 1)):
                lo, hi = 64 * half, 64 * (half + 1)
                for st2, wbuf, kw in ((0, wt, "wt"), (1, wtb, "wtb")):
                    fw = [(lambda buf, lo=lo: bass.AP(
                              buf, lo * 1536, [[1536, 64], [1, 1024]]),
                           (lambda ic=ic, lo=lo, st2=st2: bass.AP(
                               wdn, lo * 12 * 2048 + 2048 * ic + 1024 * st2,
                               [[12 * 2048, 64], [1, 1024]])),
                           "wdn")]
                    emit_dq_strip(p, wbuf, kw, 0, stg, "stg",
                                  1024 * st2, 4 * jj, 1024, fw, psub=(lo, hi))

    def expert_mms(e, half):
        first_write = (e == 0 and half == 0)
        for sp in range(n_span):
            gs = g1s[sp % 2]
            kg = f"g1s{sp % 2}"
            if g1ag:
                def f_g1(sy, sp=sp, gs=gs):
                    r = sp // 2
                    srcap = bass.AP(g1_d, 2048 * r * 1024 + 512 * (sp % 2),
                                    [[8 * 1024, 128], [1024, 8],
                                     [1024 * 1024, 2], [1, 512]])
                    dst = bass.AP(gs, 0, [[16 * 512, 128], [2 * 512, 8],
                                          [512, 2], [1, 512]])
                    return sy.dma_start(dst, srcap)
            else:
                def f_g1(sy, sp=sp, gs=gs):
                    srcap = bass.AP(g1_d, 512 * sp,
                                    [[8 * NTOK, 128], [NTOK, 8], [1024 * NTOK, 2], [1, 512]])
                    dst = bass.AP(gs, 0, [[16 * 512, 128], [2 * 512, 8], [512, 2], [1, 512]])
                    return sy.dma_start(dst, srcap)
            p.dma("sp", f_g1, reads=(kg,), writes=(kg,), tag=f"ldg1_{sp}")
            for il in range(6):
                psG, psU = psGs[il % 2], psUs[il % 2]
                kG, kU = f"psG{il % 2}", f"psU{il % 2}"
                for hc in range(16):
                    p.op("pe", lambda t, il=il, hc=hc, gs=gs, psG=psG: t.matmul(
                        psG[:, :], wgu[:, 1536 * hc + 128 * il:1536 * hc + 128 * (il + 1)],
                        gs[:, 512 * hc:512 * (hc + 1)],
                        start=(hc == 0), stop=(hc == 15)),
                        reads=("wgu", kg), writes=(kG,))
                sg = sgs[il % 2]
                ksg = f"sg{il % 2}"
                p.op("act", lambda s, psG=psG, sg=sg: s.activation(
                    sg[:, :], psG[:, :], AF.Silu),
                     reads=(kG,), writes=(ksg,))
                for hc in range(16):
                    p.op("pe", lambda t, il=il, hc=hc, gs=gs, psU=psU: t.matmul(
                        psU[:, :], wgu[:, 1536 * hc + 768 + 128 * il:
                                       1536 * hc + 768 + 128 * (il + 1)],
                        gs[:, 512 * hc:512 * (hc + 1)],
                        start=(hc == 0), stop=(hc == 15)),
                        reads=("wgu", kg), writes=(kU,))
                p.op("dve", lambda v, il=il, psU=psU, sg=sg: v.tensor_mul(
                    acts[:, 512 * il:512 * (il + 1)], sg[:, :], psU[:, :]),
                    reads=(ksg, kU), writes=("acts",))
            for t128 in range(4):
                for fb in range(4):
                    ps = psD1 if fb % 2 == 0 else psD2
                    kps = "psD1" if fb % 2 == 0 else "psD2"
                    for il in range(6):
                        icg = 6 * half + il
                        p.op("pe", lambda t, il=il, t128=t128, fb=fb, icg=icg,
                             ps=ps: t.matmul(
                            ps[:, :], acts[:, 512 * il + 128 * t128:
                                           512 * il + 128 * (t128 + 1)],
                            wdn[:, 2048 * icg + 512 * fb:2048 * icg + 512 * (fb + 1)],
                            start=(il == 0), stop=(il == 5)),
                            reads=("acts", "wdn"), writes=(kps,))
                    tg = sp * 4 + t128
                    p.op("dve", lambda v, fb=fb, tg=tg, e=e, ps=ps, kps=kps:
                         v.tensor_scalar_mul(
                        po[:, 512 * fb:512 * (fb + 1)], ps[:, :],
                        cbuf[:, 2 * tg + e:2 * tg + e + 1]),
                        reads=(kps, "cbuf"), writes=("po",))
                row0 = 512 * sp + 128 * t128
                if first_write:
                    p.dma("sp", lambda sy, row0=row0: sy.dma_start(
                        partial_d[row0:row0 + 128, :], po[:, :]),
                        reads=("po",), writes=("partial",), tag="stpo")
                else:
                    p.dma("pool", lambda g, row0=row0: g.dma_start(
                        partial_d[row0:row0 + 128, :], po[:, :], accum_op=OP.add),
                        reads=("po", "partial"), writes=("partial",), tag="stpoA")

    def load_half2():
        def f_h2(sy):
            srcap = bass.AP(wgu2_d, 0, [[1536, 128], [128 * 1536, 16], [1, 1536]])
            dst = bass.AP(wgu, 0, [[16 * 1536, 128], [1536, 16], [1, 1536]])
            return sy.dma_start(dst, srcap)
        p.dma("sp", f_h2, reads=("wgu2", "wgu"), writes=("wgu",), tag="ldh2")

    for e in range(2):
        dequant_gu(e)
        dequant_dn(e)
        expert_mms(e, 0)
        load_half2()
        expert_mms(e, 1)

    if not g1ag:
        for k8b in range(2):
            p.dma("pool", lambda g, k8b=k8b: g.dma_start(
                wt[:, 0:384], pksguc_d[128 * k8b:128 * (k8b + 1), :]),
                reads=("wt",), writes=("wt",), tag="ldswt")
            for r_ in range(8):
                def f_ssc(g, k8b=k8b, r_=r_):
                    return g.dma_start(
                        stg[16 * r_:16 * (r_ + 1), 0:384],
                        scsguc_d[8 * k8b + r_:8 * k8b + r_ + 1, :].to_broadcast((16, 384)))
                p.dma("pool", f_ssc, reads=("stg",), writes=("stg",), tag="ldssc")
            for j in range(8):
                ch = 2 * j + k8b
                fw = [(lambda buf: bass.AP(buf, 0, [[1536, 128], [1, 384]]),
                       lambda: bass.AP(shst, 0, [[1536, 128], [1, 384]]),
                       "shst")]
                emit_dq_strip(p, wt, "wt", 0, stg, "stg", 0, 4 * j, 384, fw)
                p.dma("act", lambda sy, ch=ch: sy.dma_start(
                    sgslo_d[128 * ch:128 * (ch + 1), :], shst[:, 0:384]),
                    reads=("shst",), writes=("sgslo",), tag="stsgu")
        p.dma("pool", lambda g: g.dma_start(wt[:, 0:256], pksdnc_d[0:128, :]),
              reads=("wt",), writes=("wt",), tag="ldsdw")
        for r_ in range(8):
            def f_sds(g, r_=r_):
                return g.dma_start(
                    stg[16 * r_:16 * (r_ + 1), 0:256],
                    scsdnc_d[r_:r_ + 1, :].to_broadcast((16, 256)))
            p.dma("pool", f_sds, reads=("stg",), writes=("stg",), tag="ldsds")
        for ic in range(8):
            fw = [(lambda buf: bass.AP(buf, 0, [[1536, 128], [1, 256]]),
                   lambda: bass.AP(shst, 0, [[1536, 128], [1, 256]]),
                   "shst")]
            emit_dq_strip(p, wt, "wt", 0, stg, "stg", 0, 4 * ic, 256, fw)
            p.dma("act", lambda sy, ic=ic: sy.dma_start(
                sdno_d[128 * ic:128 * (ic + 1), :], shst[:, 0:256]),
                reads=("shst",), writes=("sdno",), tag="stsdn")
        for half in range(2):
            def f_wp2(g, half=half):
                return g.dma_start(wt[64 * half:64 * (half + 1), 0:256],
                                   pksdnc_d[128:192, :])
            p.dma("pool", f_wp2, reads=("wt",), writes=("wt",), tag="ldsdwp")
        for half in range(2):
            for r_ in range(4):
                def f_sdsp(g, half=half, r_=r_):
                    return g.dma_start(
                        stg[64 * half + 16 * r_:64 * half + 16 * (r_ + 1), 0:256],
                        scsdnc_d[8 + r_:9 + r_, :].to_broadcast((16, 256)))
                p.dma("pool", f_sdsp, reads=("stg",), writes=("stg",), tag="ldsdsp")
        for q2 in range(4):
            ic = 8 + q2
            for half, jj in ((0, 2 * q2), (1, 2 * q2 + 1)):
                lo, hi = 64 * half, 64 * (half + 1)
                fw = [(lambda buf, lo=lo: bass.AP(buf, lo * 1536, [[1536, 64], [1, 256]]),
                       lambda lo=lo: bass.AP(shst, lo * 1536, [[1536, 64], [1, 256]]),
                       "shst")]
                emit_dq_strip(p, wt, "wt", 0, stg, "stg", 0, 4 * jj, 256, fw,
                              psub=(lo, hi))
            p.dma("act", lambda sy, ic=ic: sy.dma_start(
                sdno_d[128 * ic:128 * (ic + 1), :], shst[:, 0:256]),
                reads=("shst",), writes=("sdno",), tag="stsdnp")

    else:
        shstF = p.sbuf("shstF", [128, 3072], BF16)
        def sgu_w(strip):
            du = 1536 * strip
            TW = 1536
            return [
                (lambda buf: bass.AP(buf, 0, [[TW, 128], [8, 128], [1, 8]]),
                 (lambda du=du: bass.AP(shstF, du, [[3072, 128], [1, 128], [128, 8]])),
                 "shstF"),
                (lambda buf: bass.AP(buf, 1024, [[TW, 128], [8, 64], [2, 4], [1, 2]]),
                 (lambda du=du: bass.AP(shstF, du + 128 * 8,
                                        [[3072, 128], [1, 64], [128, 4], [64, 2]])),
                 "shstF"),
            ]
        for k8b in range(2):
            for r_ in range(8):
                def f_ssc(g, k8b=k8b, r_=r_):
                    return g.dma_start(
                        stg[16 * r_:16 * (r_ + 1), :],
                        scsgu_d[8 * k8b + r_:8 * k8b + r_ + 1, :]
                        .to_broadcast((16, 3072)))
                p.dma("pool", f_ssc, reads=("stg",), writes=("stg",), tag="ldssc")
            for strip, wbuf, kw in ((0, wt, "wt"), (1, wtb, "wtb")):
                p.dma("pool", lambda g, k8b=k8b, strip=strip, wbuf=wbuf:
                      g.dma_start(
                    wbuf[:, :], pksgu_d[128 * k8b:128 * (k8b + 1),
                                        1536 * strip:1536 * (strip + 1)]),
                    reads=(kw,), writes=(kw,), tag="ldswt")
            for j in range(8):
                ch = 2 * j + k8b
                for strip, wbuf, kw in ((0, wt, "wt"), (1, wtb, "wtb")):
                    emit_dq_strip(p, wbuf, kw, 0, stg, "stg", 1536 * strip,
                                  4 * j, 1536, sgu_w(strip))
                p.dma("act", lambda sy, ch=ch: sy.dma_start(
                    sgu_loc_d[128 * ch:128 * (ch + 1), :], shstF[:, :]),
                    reads=("shstF",), writes=("sgu_loc",), tag="stsgu")
        for r_ in range(8):
            def f_sdsF(g, r_=r_):
                return g.dma_start(
                    stg[16 * r_:16 * (r_ + 1), 0:2048],
                    scsdn_d[r_:r_ + 1, :].to_broadcast((16, 2048)))
            p.dma("pool", f_sdsF, reads=("stg",), writes=("stg",), tag="ldsds")
        p.dma("pool", lambda g: g.dma_start(wt[:, 0:1024], pksdn_d[0:128, 0:1024]),
              reads=("wt",), writes=("wt",), tag="ldsdw")
        p.dma("pool", lambda g: g.dma_start(wtb[:, 0:1024], pksdn_d[0:128, 1024:2048]),
              reads=("wtb",), writes=("wtb",), tag="ldsdw")
        for ic in range(8):
            for st2, wbuf, kw in ((0, wt, "wt"), (1, wtb, "wtb")):
                fw = [(lambda buf: bass.AP(buf, 0, [[1536, 128], [1, 1024]]),
                       (lambda st2=st2: bass.AP(shstF, 1024 * st2,
                                                [[3072, 128], [1, 1024]])),
                       "shstF")]
                emit_dq_strip(p, wbuf, kw, 0, stg, "stg", 1024 * st2,
                              4 * ic, 1024, fw)
            p.dma("act", lambda sy, ic=ic: sy.dma_start(
                sdn_loc_d[128 * ic:128 * (ic + 1), :], shstF[:, 0:2048]),
                reads=("shstF",), writes=("sdn_loc",), tag="stsdnF")
        for st2, wbuf, kw in ((0, wt, "wt"), (1, wtb, "wtb")):
            for half in range(2):
                def f_wpF(g, st2=st2, wbuf=wbuf, half=half):
                    return g.dma_start(
                        wbuf[64 * half:64 * (half + 1), 0:1024],
                        pksdn_d[128:192, 1024 * st2:1024 * (st2 + 1)])
                p.dma("pool", f_wpF, reads=(kw,), writes=(kw,), tag="ldsdwp")
        for half in range(2):
            for r_ in range(4):
                def f_sdspF(g, half=half, r_=r_):
                    return g.dma_start(
                        stg[64 * half + 16 * r_:64 * half + 16 * (r_ + 1), 0:2048],
                        scsdn_d[8 + r_:9 + r_, :].to_broadcast((16, 2048)))
                p.dma("pool", f_sdspF, reads=("stg",), writes=("stg",), tag="ldsdsp")
        for q2 in range(4):
            ic = 8 + q2
            for half, jj in ((0, 2 * q2), (1, 2 * q2 + 1)):
                lo, hi = 64 * half, 64 * (half + 1)
                for st2, wbuf, kw in ((0, wt, "wt"), (1, wtb, "wtb")):
                    fw = [(lambda buf, lo=lo: bass.AP(
                              buf, lo * 1536, [[1536, 64], [1, 1024]]),
                           (lambda lo=lo, st2=st2: bass.AP(
                               shstF, lo * 3072 + 1024 * st2,
                               [[3072, 64], [1, 1024]])),
                           "shstF")]
                    emit_dq_strip(p, wbuf, kw, 0, stg, "stg", 1024 * st2,
                                  4 * jj, 1024, fw, psub=(lo, hi))
            p.dma("act", lambda sy, ic=ic: sy.dma_start(
                sdn_loc_d[128 * ic:128 * (ic + 1), :], shstF[:, 0:2048]),
                reads=("shstF",), writes=("sdn_loc",), tag="stsdnF")

    if standalone:
        return p.finalize()




def build_launch_D(p=None, io=None):
    standalone = p is None
    if standalone:
        p = Prog("D")
        x1sl_d = p.dram("x1sl", [TOK_PER_CORE, DMODEL], F32, kind="ExternalInput")
        ln1sl_d = p.dram("ln1sl", [DMODEL, TOK_PER_CORE], BF16, kind="ExternalInput")
        parts_d = [p.dram(f"pp{k}", [TOK_PER_CORE, DMODEL], BF16, kind="ExternalInput")
                   for k in range(NCORE)]
        sgu_d = p.dram("sgu", [DMODEL, 3072], BF16, kind="ExternalInput")
        sdn_d = p.dram("sdn", [1536, DMODEL], BF16, kind="ExternalInput")
        x2_d = p.dram("x2", [TOK_PER_CORE, DMODEL], F32, kind="ExternalOutput")
        ln2T_d = p.dram("ln2T", [DMODEL, TOK_PER_CORE], BF16, kind="ExternalOutput")
        c = emit_consts(p)
        rs_d = None
    else:
        x1sl_d, ln1sl_d = io["x1"], io["ln1T"]
        rs_d = io["rs_out"]
        sgu_d, sdn_d = io["sgu_loc"], io["sdn_loc"]
        x2_d, ln2T_d = io["x2"], io["ln2T"]
        c = io["consts"]
    NT = TOK_PER_CORE // 128

    wsg = [p.sbuf(f"wsg{i}", [128, 16 * 256], BF16) for i in range(2)]
    sdns = [p.sbuf(f"sdns{i}", [128, 12 * 512], BF16) for i in range(2)]
    lg = p.sbuf("lgD", [128, 16 * 512], BF16)
    actD = p.sbuf("actD", [128, 12 * 512], BF16)
    sgD = p.sbuf("sgD", [128, 512], BF16)
    pt = [p.sbuf(f"pt{i}", [128, DMODEL], BF16) for i in range(2)]
    rsum = p.sbuf("rsum", [128, DMODEL], F32)
    xt = [p.sbuf(f"xtD{i}", [128, DMODEL], F32) for i in range(NT)]
    psG = p.psum("psGD", [128, 512])
    psU = p.psum("psUD", [128, 512])
    psD1 = p.psum("psDD1", [128, 512])
    psD2 = p.psum("psDD2", [128, 512])

    for sp in range(2):
        def f_lg(sy, sp=sp):
            srcap = bass.AP(ln1sl_d, 512 * sp,
                            [[8 * TOK_PER_CORE, 128], [TOK_PER_CORE, 8],
                             [1024 * TOK_PER_CORE, 2], [1, 512]])
            dst = bass.AP(lg, 0, [[16 * 512, 128], [2 * 512, 8], [512, 2], [1, 512]])
            return sy.dma_start(dst, srcap)
        p.dma("sp", f_lg, reads=("lgD",), writes=("lgD",), tag="ldlg")
        for ic in range(12):
            wb = wsg[ic % 2]
            kwb = f"wsg{ic % 2}"
            for gu in range(2):
                def f_w(sy, ic=ic, wb=wb, gu=gu):
                    srcap = bass.AP(sgu_d, 1536 * gu + 128 * ic,
                                    [[3072, 128], [128 * 3072, 16], [1, 128]])
                    dst = bass.AP(wb, 128 * gu,
                                  [[16 * 256, 128], [256, 16], [1, 128]])
                    return sy.dma_start(dst, srcap)
                p.dma("sp", f_w, reads=(kwb,), writes=(kwb,), tag="ldsgu")
            for hc in range(16):
                p.op("pe", lambda t, hc=hc, wb=wb: t.matmul(
                    psG[:, :], wb[:, 256 * hc:256 * hc + 128],
                    lg[:, 512 * hc:512 * (hc + 1)],
                    start=(hc == 0), stop=(hc == 15)),
                    reads=(kwb, "lgD"), writes=("psGD",))
            p.op("act", lambda s: s.activation(sgD[:, :], psG[:, :], AF.Silu),
                 reads=("psGD",), writes=("sgD",))
            for hc in range(16):
                p.op("pe", lambda t, hc=hc, wb=wb: t.matmul(
                    psU[:, :], wb[:, 256 * hc + 128:256 * (hc + 1)],
                    lg[:, 512 * hc:512 * (hc + 1)],
                    start=(hc == 0), stop=(hc == 15)),
                    reads=(kwb, "lgD"), writes=("psUD",))
            p.op("dve", lambda v, ic=ic: v.tensor_mul(
                actD[:, 512 * ic:512 * (ic + 1)], sgD[:, :], psU[:, :]),
                reads=("sgD", "psUD"), writes=("actD",))
        for fb in range(4):
            sb = sdns[fb % 2]
            ksb = f"sdns{fb % 2}"
            def f_sdn(sy, fb=fb, sb=sb):
                srcap = bass.AP(sdn_d, 512 * fb,
                                [[DMODEL, 128], [128 * DMODEL, 12], [1, 512]])
                dst = bass.AP(sb, 0, [[12 * 512, 128], [512, 12], [1, 512]])
                return sy.dma_start(dst, srcap)
            p.dma("sp", f_sdn, reads=(ksb,), writes=(ksb,), tag="ldsdn")
            for t128 in range(4):
                tt = 4 * sp + t128
                ps = psD1 if t128 % 2 == 0 else psD2
                kps = "psDD1" if t128 % 2 == 0 else "psDD2"
                for ic in range(12):
                    p.op("pe", lambda t, ic=ic, t128=t128, fb=fb, ps=ps, sb=sb:
                         t.matmul(
                        ps[:, :], actD[:, 512 * ic + 128 * t128:
                                       512 * ic + 128 * (t128 + 1)],
                        sb[:, 512 * ic:512 * (ic + 1)],
                        start=(ic == 0), stop=(ic == 11)),
                        reads=("actD", ksb), writes=(kps,))
                if fb == 0:
                    p.dma("sp", lambda sy, tt=tt: sy.dma_start(
                        xt[tt][:, :], x1sl_d[128 * tt:128 * (tt + 1), :]),
                        reads=(f"xtD{tt}",), writes=(f"xtD{tt}",), tag="ldx1")
                    if rs_d is not None:
                        pb = pt[0]
                        p.dma("act", lambda sy, tt=tt, pb=pb: sy.dma_start(
                            pb[:, :], rs_d[128 * tt:128 * (tt + 1), :]),
                            reads=("pt0",), writes=("pt0",), tag="ldrs")
                        p.op("dve", lambda v, pb=pb: v.tensor_copy(
                            rsum[:, :], pb[:, :]),
                            reads=("pt0",), writes=("rsum",))
                    else:
                        for k in range(NCORE):
                            pb = pt[k % 2]
                            kpb = f"pt{k % 2}"
                            p.dma("act", lambda sy, k=k, tt=tt, pb=pb: sy.dma_start(
                                pb[:, :], parts_d[k][128 * tt:128 * (tt + 1), :]),
                                reads=(kpb,), writes=(kpb,), tag="ldpt")
                            if k == 0:
                                p.op("dve", lambda v, pb=pb: v.tensor_copy(
                                    rsum[:, :], pb[:, :]),
                                    reads=(kpb,), writes=("rsum",))
                            else:
                                p.op("dve", lambda v, pb=pb: v.tensor_add(
                                    rsum[:, :], rsum[:, :], pb[:, :]),
                                    reads=("rsum", kpb), writes=("rsum",))
                    p.op("dve", lambda v, tt=tt: v.tensor_add(
                        xt[tt][:, :], xt[tt][:, :], rsum[:, :]),
                        reads=(f"xtD{tt}", "rsum"), writes=(f"xtD{tt}",))
                p.op("dve", lambda v, tt=tt, fb=fb, ps=ps: v.tensor_add(
                    xt[tt][:, 512 * fb:512 * (fb + 1)],
                    xt[tt][:, 512 * fb:512 * (fb + 1)], ps[:, :]),
                    reads=(f"xtD{tt}", kps), writes=(f"xtD{tt}",))
        for t128 in range(4):
            tt = 4 * sp + t128
            p.dma("sp", lambda sy, tt=tt: sy.dma_start(
                x2_d[128 * tt:128 * (tt + 1), :], xt[tt][:, :]),
                reads=(f"xtD{tt}",), writes=("x2o",), tag="stx2")

    emit_ln_from_sbuf(p, c, lambda tt: (xt[tt], f"xtD{tt}"), NT, ln2T_d, 0, "ln2")
    if standalone:
        return p.finalize()


def gu_colperm():
    perm = np.zeros(1536, np.int64)
    for ic in range(8):
        perm[128 * ic:128 * (ic + 1)] = 8 * np.arange(128) + ic
    for q in range(4):
        for r in range(2):
            j = 2 * q + r
            perm[128 * (8 + q) + 64 * r:128 * (8 + q) + 64 * (r + 1)] = \
                1024 + 8 * np.arange(64) + j
    return perm


def dn_rowperm():
    perm = np.zeros(1536, np.int64)
    for ch in range(8):
        perm[128 * ch:128 * (ch + 1)] = 8 * np.arange(128) + ch
    for q in range(4):
        for r in range(2):
            j = 2 * q + r
            perm[128 * (8 + q) + 64 * r:128 * (8 + q) + 64 * (r + 1)] = \
                8 * (128 + np.arange(64)) + j
    return perm


def run_launch_D(x1, G1, parts, SGU, SDN):
    cp = gu_colperm()
    colperm = np.concatenate([cp, 1536 + cp])
    SGUp = _np(SGU[:, colperm])
    nc = _get_nc('D', build_launch_D)
    in_maps = []
    for cidx in range(NCORE):
        t0 = 1024 * cidx
        m = {
            "x1sl": _np(x1[t0:t0 + 1024]),
            "ln1sl": _np(G1[:, t0:t0 + 1024]),
            "sgu": SGUp, "sdn": SDN,
        }
        for k in range(NCORE):
            m[f"pp{k}"] = _np(parts[k][t0:t0 + 1024])
        in_maps.append(m)
    res = run_bass_kernel_spmd(nc, in_maps, core_ids=list(range(NCORE)))
    x2 = np.concatenate([res.results[i]["x2"] for i in range(NCORE)], axis=0)
    G2 = np.concatenate([res.results[i]["ln2T"] for i in range(NCORE)], axis=1)
    return x2, G2



D1 = 64
NQH = 8
ROPE_THETA = 10000.0


def build_launch_E(debug=False, p=None, io=None):
    standalone = p is None
    if standalone:
        p = Prog("E")
        g2_d = p.dram("g2", [DMODEL, S], BF16, kind="ExternalInput")
        wq_d = p.dram("wq", [DMODEL, 512], F32, kind="ExternalInput")
        wk_d = p.dram("wk", [DMODEL, 128], F32, kind="ExternalInput")
        wv_d = p.dram("wv", [DMODEL, 128], F32, kind="ExternalInput")
        wb_d = p.dram("wb", [DMODEL, 8], F32, kind="ExternalInput")
        bb_d = p.dram("bb", [8, 1], F32, kind="ExternalInput")
        qnw_d = p.dram("qnw", [64, 1], F32, kind="ExternalInput")
        knw_d = p.dram("knw", [64, 1], F32, kind="ExternalInput")
        oT2_d = p.dram("oT2", [512, S], BF16, kind="ExternalOutput")
        beta_d = p.dram("betad", [8, S], F32)
        g2ag = False
    else:
        g2_d = io["g2"]
        wq_d, wk_d, wv_d, wb_d = io["wq"], io["wk"], io["wv"], io["wb"]
        bb_d, qnw_d, knw_d = io["bb"], io["qnw"], io["knw"]
        oT2_d = io["oT2"]
        beta_d = io["betad"]
        g2ag = True
    if debug:
        dbg = {}
        for nm, shape, dt in (("q0", [128, S], BF16), ("k0", [128, S], BF16),
                              ("v0", [128, S], BF16), ("beta", [8, S], F32),
                              ("eT0", [128, 512], BF16), ("rows", [1, 3072], F32),
                              ("u01", [128, 128], BF16), ("cos", [128, S], BF16),
                              ("sin", [128, S], BF16), ("perm", [128, 128], BF16),
                              ("negM", [128, 1], F32)):
            dbg[nm] = p.dram("dbg_" + nm, shape, dt, kind="ExternalOutput")

    NSP = S // 512

    wqs = p.sbuf("wqs", [128, 16 * 512], BF16)
    wks = p.sbuf("wks", [128, 16 * 128], BF16)
    wvs = p.sbuf("wvs", [128, 16 * 128], BF16)
    wbs = p.sbuf("wbs", [128, 16 * 8], BF16)
    g2s_single = p.sbuf("g2sE", [128, 16 * 512], BF16)
    g2s = [g2s_single, g2s_single]
    qsb = [p.sbuf(f"qsbE{i}", [128, S], BF16) for i in range(4)]
    ksb = p.sbuf("ksbE", [128, S], BF16)
    kswap = p.sbuf("kswapE", [128, S], BF16)
    perm64 = p.sbuf("perm64E", [128, 128], BF16)
    vsb = p.sbuf("vsbE", [128, S], BF16)
    eT = p.sbuf("eTE", [128, 32 * 512], BF16)
    u_sb = p.sbuf("u_sbE", [128, 32 * 64], BF16)
    oth = p.sbuf("othE", [128, S], BF16)
    cosb = p.sbuf("cosb", [128, S], BF16)
    sinb = p.sbuf("sinb", [128, S], BF16)
    permt = p.sbuf("permt", [128, 128], BF16)
    bo2 = p.sbuf("bo2", [128, 2], BF16)
    bo2t = p.sbuf("bo2t", [2, 128], F32)
    masks = p.sbuf("masksE", [128, 8 * 512], BF16)
    rows = p.sbuf("rowsE", [1, 6 * 512], F32)
    rstdr = p.sbuf("rstdr", [2, 512], F32)
    sqs = p.sbuf("sqsE", [128, 512], BF16)
    qe = p.sbuf("qeE", [128, 512], BF16)
    rt1 = p.sbuf("rt1", [128, 512], BF16)
    rt2 = p.sbuf("rt2", [128, 512], BF16)
    qnw2 = p.sbuf("qnw2", [128, 1], F32)
    knw2 = p.sbuf("knw2", [128, 1], F32)
    bbc = p.sbuf("bbc", [8, 1], F32)
    scal = p.sbuf("scalE", [1, 8], F32)
    negM2b = p.sbuf("negM2b", [128, 1], F32)
    brdcol = p.sbuf("brdcol", [128, 1], F32)
    rccol = p.sbuf("rccol", [128, 1], F32)
    dcol = p.sbuf("dcol", [128, 1], F32)
    NTs = p.sbuf("NTs", [128, 128], BF16)
    Ns = p.sbuf("Ns", [128, 128], BF16)
    Pj = [p.sbuf(f"PjE{i}", [128, 128], BF16) for i in range(2)]
    PTj = [p.sbuf(f"PTjE{i}", [128, 128], BF16) for i in range(2)]
    ucur = p.sbuf("ucur", [128, 64], BF16)
    edg = p.sbuf("edgE", [128, 512], BF16)
    edg2 = p.sbuf("edgE2", [128, 512], BF16)
    t1s = p.sbuf("t1sE", [128, 64], F32)
    brow = p.sbuf("browE", [1, 512], F32)
    iotaf = p.sbuf("iotaf", [128, 1024], F32)
    rr1 = p.sbuf("rr1E", [128, 1024], F32)
    angm = rr1
    invf = p.sbuf("invfE", [128, 1], F32)
    srow = p.sbuf("srowE", [128, 1], F32)
    swpc = p.sbuf("swpc", [128, 1], F32)
    colix = p.sbuf("colix", [128, 128], F32)
    pcol = p.sbuf("pcolE", [128, 1], F32)
    ident_e = p.sbuf("identE", [128, 128], BF16)
    betast = p.sbuf("betast", [8, 512], F32)
    oIb = p.sbuf("oIbE", [128, 64], BF16)

    pb = [p.psum(f"pbE{i}", [128, 512]) for i in range(7)]
    pbT = p.psum("pbTE", [128, 512], BF16)

    c = {"ones_col": p.sbuf("c_ones_col", [128, 1], BF16),
         "ones_row_f": p.sbuf("c_ones_row_f", [1, 128], F32),
         "ones_row": p.sbuf("c_ones_row", [1, 128], BF16)}
    p.op("pool", lambda g: g.memset(c["ones_col"][:, :], 1.0), writes=("c_ones",))
    p.op("pool", lambda g: g.memset(c["ones_row_f"][:, :], 1.0), writes=("c_ones",))
    p.op("pool", lambda g: g.memset(c["ones_row"][:, :], 1.0), writes=("c_ones",))

    p.op("pool", lambda g: g.memset(masks[:, :], 1.0), writes=("mraw",))
    for k in range(4):
        p.op("pool", lambda g, k=k: g.affine_select(
            masks[:, 512 * k:512 * (k + 1)], masks[:, 512 * k:512 * (k + 1)],
            [[1, 512]], OP.is_gt, 0.0, base=-128 * k, channel_multiplier=-1),
            reads=("mraw",), writes=("mstrict",), tag="mkstrict")
    for k in range(4):
        def f_md(g, k=k):
            return g.affine_select(
                masks[:, 512 * (4 + k):512 * (5 + k)],
                masks[:, 512 * (4 + k):512 * (5 + k)],
                [[1, 512]], OP.is_equal, 0.0, base=-128 * k,
                channel_multiplier=-1)
        p.op("pool", f_md, reads=("mraw",), writes=("mdiag",), tag="mkdiag")
    p.op("pool", lambda g: g.memset(bo2[:, :], 0.0), writes=("bo2r",))
    p.op("pool", lambda g: g.memset(bo2[0:64, 0:1], 1.0), reads=("bo2r",), writes=("bo2",))
    p.op("pool", lambda g: g.memset(bo2[64:128, 1:2], 1.0), reads=("bo2r",), writes=("bo2",))
    bo2i = p.sbuf("bo2iE", [2, 128], F32)
    p.op("pool", lambda g: g.iota(bo2i[:, :], [[1, 2], [0, 64]],
                                  channel_multiplier=0,
                                  allow_small_or_imprecise_dtypes=True),
         writes=("bo2i",))
    p.op("pool", lambda g: g.iota(bo2t[:, :], [[0, 128]], channel_multiplier=1,
                                  allow_small_or_imprecise_dtypes=True),
         writes=("bo2tr",))
    p.op("dve", lambda v: v.tensor_tensor(bo2t[:, :], bo2i[:, :], bo2t[:, :],
                                          OP.is_equal),
         reads=("bo2i", "bo2tr"), writes=("bo2t",))
    p.op("pool", lambda g: g.iota(swpc[:, :], [[0, 1]], channel_multiplier=1, allow_small_or_imprecise_dtypes=True),
         writes=("swp0",))
    ge64t = p.sbuf("ge64E", [128, 1], F32)
    p.op("dve", lambda v: v.tensor_scalar(ge64t[:, :], swpc[:, :], 64.0, -64.0,
                                          op0=OP.is_ge, op1=OP.mult),
         reads=("swp0",), writes=("ge64",))
    p.op("dve", lambda v: v.tensor_add(srow[:, :], swpc[:, :], ge64t[:, :]),
         reads=("swp0", "ge64"), writes=("pm64",))
    p.op("dve", lambda v: v.tensor_scalar(srow[:, :], srow[:, :], 32.0, None,
                                          op0=OP.is_ge),
         reads=("pm64",), writes=("srm",))
    p.op("dve", lambda v: v.scalar_tensor_tensor(
        swpc[:, :], srow[:, :], -64.0, swpc[:, :], op0=OP.mult, op1=OP.add),
        reads=("srm", "swp0"), writes=("swp1",))
    p.op("dve", lambda v: v.tensor_scalar_add(swpc[:, :], swpc[:, :], 32.0),
         reads=("swp1",), writes=("swpc",))
    p.op("pool", lambda g: g.iota(colix[:, :], [[1, 128]], channel_multiplier=0, allow_small_or_imprecise_dtypes=True),
         writes=("colix",))
    p.op("dve", lambda v: v.tensor_single_scalar(permt[:, :], colix[:, :],
                                                 swpc[:, :], OP.is_equal),
         reads=("colix", "swpc"), writes=("permt",))
    p.op("pool", lambda g: g.iota(pcol[:, :], [[0, 1]], channel_multiplier=1, allow_small_or_imprecise_dtypes=True),
         writes=("pcol0",))
    p.op("dve", lambda v: v.tensor_single_scalar(ident_e[:, :], colix[:, :],
                                                 pcol[:, :], OP.is_equal),
         reads=("colix", "pcol0"), writes=("identE",))
    sw64 = p.sbuf("sw64E", [128, 1], F32)
    p.op("dve", lambda v: v.tensor_scalar(sw64[:, :], pcol[:, :], 64.0, -128.0,
                                          op0=OP.is_ge, op1=OP.mult),
         reads=("pcol0",), writes=("sw64a",))
    p.op("dve", lambda v: v.tensor_add(sw64[:, :], sw64[:, :], pcol[:, :]),
         reads=("sw64a", "pcol0"), writes=("sw64b",))
    p.op("dve", lambda v: v.tensor_scalar_add(sw64[:, :], sw64[:, :], 64.0),
         reads=("sw64b",), writes=("sw64",))
    p.op("dve", lambda v: v.tensor_single_scalar(perm64[:, :], colix[:, :],
                                                 sw64[:, :], OP.is_equal),
         reads=("colix", "sw64"), writes=("perm64",))
    p.op("dve", lambda v: v.tensor_scalar(srow[:, :], srow[:, :], 2.0, -1.0,
                                          op0=OP.mult, op1=OP.add),
         reads=("srm",), writes=("srow",))
    p.op("pool", lambda g: g.iota(invf[:, :], [[0, 1]], channel_multiplier=1, allow_small_or_imprecise_dtypes=True),
         writes=("invf0",))
    p.op("dve", lambda v: v.tensor_scalar(rr1[:, 0:1], invf[:, :], 32.0, None,
                                          op0=OP.is_ge),
         reads=("invf0", "rr1b"), writes=("rr1b",))
    p.op("dve", lambda v: v.tensor_scalar(rr1[:, 1:2], invf[:, :], 64.0, None,
                                          op0=OP.is_ge),
         reads=("invf0", "rr1b"), writes=("rr1b",))
    p.op("dve", lambda v: v.tensor_add(rr1[:, 0:1], rr1[:, 0:1], rr1[:, 1:2]),
         reads=("rr1b",), writes=("rr1b",))
    p.op("dve", lambda v: v.tensor_scalar(rr1[:, 1:2], invf[:, :], 96.0, None,
                                          op0=OP.is_ge),
         reads=("invf0", "rr1b"), writes=("rr1b",))
    p.op("dve", lambda v: v.tensor_add(rr1[:, 0:1], rr1[:, 0:1], rr1[:, 1:2]),
         reads=("rr1b",), writes=("rr1b",))
    p.op("dve", lambda v: v.scalar_tensor_tensor(
        invf[:, :], rr1[:, 0:1], -32.0, invf[:, :], op0=OP.mult, op1=OP.add),
        reads=("rr1b", "invf0"), writes=("invf1",))
    p.op("act", lambda s: s.activation(invf[:, :], invf[:, :], AF.Exp,
                                       scale=-(2.0 / 64.0) * float(np.log(ROPE_THETA))),
         reads=("invf1",), writes=("invf",))
    TWO_PI = float(2.0 * np.pi)
    MAGIC = 12582912.0
    halfpi_c, halfpi_k = p.const_col(float(np.pi / 2.0))
    for blk in range(4):
        bs = slice(1024 * blk, 1024 * (blk + 1))
        p.op("pool", lambda g, blk=blk: g.iota(
            iotaf[:, :], [[1, 1024]], base=1024 * blk, channel_multiplier=0,
            allow_small_or_imprecise_dtypes=True),
            reads=("iotafb",), writes=("iotafb",))
        p.op("dve", lambda v: v.tensor_scalar_mul(iotaf[:, :], iotaf[:, :],
                                                  invf[:, :]),
             reads=("iotafb", "invf"), writes=("iotafb",))
        for trig, outb, outk in (("sin", sinb, "sinraw"), ("cos", cosb, "cosb")):
            extra = 0.25 if trig == "cos" else 0.0
            p.op("dve", lambda v, extra=extra: v.tensor_scalar(
                rr1[:, :], iotaf[:, :], 1.0 / TWO_PI, MAGIC + extra,
                op0=OP.mult, op1=OP.add),
                reads=("iotafb", "rr1b"), writes=("rr1b",))
            p.op("dve", lambda v: v.tensor_scalar_add(rr1[:, :], rr1[:, :], -MAGIC),
                 reads=("rr1b",), writes=("rr1b",))
            p.op("dve", lambda v: v.scalar_tensor_tensor(
                rr1[:, :], rr1[:, :], -TWO_PI, iotaf[:, :], op0=OP.mult, op1=OP.add),
                reads=("rr1b", "iotafb"), writes=("rr1b",))
            if trig == "sin":
                p.op("act", lambda s, outb=outb, bs=bs: s.activation(
                    outb[:, bs], angm[:, :], AF.Sin),
                    reads=("rr1b",), writes=(outk,))
            else:
                p.op("act", lambda s, outb=outb, bs=bs: s.activation(
                    outb[:, bs], angm[:, :], AF.Sin, bias=halfpi_c[:, :]),
                    reads=("rr1b", halfpi_k), writes=(outk,))
    p.op("dve", lambda v: v.tensor_scalar_mul(sinb[:, :], sinb[:, :], srow[:, :]),
         reads=("sinraw", "srow"), writes=("sinb",))

    p.dma("sp", lambda sy: sy.dma_start(bbc[:, :], bb_d[:, :]), writes=("bbc",))
    p.dma("sp", lambda sy: sy.dma_start(qnw2[0:64, :], qnw_d[:, :]), writes=("qnw2",))
    p.dma("sp", lambda sy: sy.dma_start(qnw2[64:128, :], qnw_d[:, :]),
          reads=("qnw2",), writes=("qnw2",))
    p.dma("sp", lambda sy: sy.dma_start(knw2[0:64, :], knw_d[:, :]), writes=("knw2",))
    p.dma("sp", lambda sy: sy.dma_start(knw2[64:128, :], knw_d[:, :]),
          reads=("knw2",), writes=("knw2",))
    qnwrow = p.sbuf("qnwrow", [1, 64], F32)
    knwrow = p.sbuf("knwrow", [1, 64], F32)
    p.dma("sp", lambda sy: sy.dma_start(
        qnwrow[:, :], bass.AP(qnw_d, 0, [[64, 1], [1, 64]])), writes=("qnwrow",))
    p.dma("sp", lambda sy: sy.dma_start(
        knwrow[:, :], bass.AP(knw_d, 0, [[64, 1], [1, 64]])), writes=("knwrow",))
    p.op("dve", lambda v: v.reduce_max(scal[:, 0:1], qnwrow[:, :], axis=AX.X,
                                       apply_absolute_value=True),
         reads=("qnwrow",), writes=("scal",))
    p.op("dve", lambda v: v.reduce_max(scal[:, 1:2], knwrow[:, :], axis=AX.X,
                                       apply_absolute_value=True),
         reads=("scal", "knwrow"), writes=("scal",))
    p.op("dve", lambda v: v.tensor_mul(scal[:, 2:3], scal[:, 0:1], scal[:, 1:2]),
         reads=("scal",), writes=("scal",))
    p.op("dve", lambda v: v.tensor_scalar_mul(scal[:, 3:4], scal[:, 2:3], -8.0),
         reads=("scal",), writes=("scal",))
    p.op("pe", lambda t: t.matmul(pb[5][:, 0:1], c["ones_row_f"][:, :],
                                  scal[:, 3:4], start=True, stop=True),
         reads=("scal", "c_ones"), writes=("pb5",))
    p.op("dve", lambda v: v.tensor_copy(negM2b[:, :], pb[5][:, 0:1]),
         reads=("pb5",), writes=("negM2b",))

    for (wd, ws, key, wth) in ((wq_d, wqs, "wqs", 512), (wk_d, wks, "wks", 128),
                               (wv_d, wvs, "wvs", 128), (wb_d, wbs, "wbs", 8)):
        def f_ldw(g, wd=wd, ws=ws, wth=wth):
            srcap = bass.AP(wd, 0, [[wth, 128], [128 * wth, 16], [1, wth]])
            dst = bass.AP(ws, 0, [[16 * wth, 128], [wth, 16], [1, wth]])
            return g.dma_start(dst, srcap)
        p.dma("pool", f_ldw, writes=(key,), tag="ldw")

    for sp in range(NSP):
        gs = g2s[sp % 2]
        kg = "g2sE"
        if g2ag:
            def f_g2(sy, sp=sp, gs=gs):
                srcap = bass.AP(g2_d, 2048 * (sp // 2) * 1024 + 512 * (sp % 2),
                                [[1024, 128], [128 * 1024, 16], [1, 512]])
                dst = bass.AP(gs, 0, [[16 * 512, 128], [512, 16], [1, 512]])
                return sy.dma_start(dst, srcap)
        else:
            def f_g2(sy, sp=sp, gs=gs):
                srcap = bass.AP(g2_d, 512 * sp, [[S, 128], [128 * S, 16], [1, 512]])
                dst = bass.AP(gs, 0, [[16 * 512, 128], [512, 16], [1, 512]])
                return sy.dma_start(dst, srcap)
        p.dma("sp", f_g2, reads=(kg,), writes=(kg,), tag=f"ldg2_{sp}")
        for hc in range(16):
            p.op("pe", lambda t, hc=hc, gs=gs: t.matmul(
                pb[5][0:8, :], wbs[:, 8 * hc:8 * (hc + 1)],
                gs[:, 512 * hc:512 * (hc + 1)], start=(hc == 0), stop=(hc == 15)),
                reads=("wbs", kg), writes=("pb5",))
        p.op("dve", lambda v: v.tensor_scalar(
            betast[0:8, :], pb[5][0:8, :], bbc[0:8, :], None, op0=OP.add),
            reads=("pb5", "bbc"), writes=("betarow",))
        p.dma("act", lambda sy, sp=sp: sy.dma_start(
            beta_d[:, 512 * sp:512 * (sp + 1)], betast[0:8, :]),
            reads=("betarow",), writes=("betad",), tag="stbeta")
        for tc in range(4):
            for hc in range(16):
                p.op("pe", lambda t, tc=tc, hc=hc, gs=gs: t.matmul(
                    pb[6][:, 0:128],
                    gs[:, 512 * hc + 128 * tc:512 * hc + 128 * (tc + 1)],
                    wvs[:, 128 * hc:128 * (hc + 1)],
                    start=(hc == 0), stop=(hc == 15)),
                    reads=("wvs", kg), writes=("pb6",))
            p.op("dve", lambda v, sp=sp, tc=tc: v.tensor_copy(
                vsb[:, 512 * sp + 128 * tc:512 * sp + 128 * (tc + 1)],
                pb[6][:, 0:128]),
                reads=("pb6",), writes=("vsbE",))
        for pr in range(5):
            wsrc, wkey, col0 = (wqs, "wqs", 128 * pr) if pr < 4 else (wks, "wks", 0)
            nw2, nwk = (qnw2, "qnw2") if pr < 4 else (knw2, "knw2")
            dstb, dkey = (qsb[pr], f"qsbE{pr}") if pr < 4 else (ksb, "ksbE")
            wpitch = 512 if pr < 4 else 128
            psq = pb[pr % 4]
            kpsq = f"pb{pr % 4}"
            for hc in range(16):
                p.op("pe", lambda t, hc=hc, gs=gs, wsrc=wsrc, col0=col0,
                     wpitch=wpitch, psq=psq: t.matmul(
                    psq[:, :], wsrc[:, wpitch * hc + col0:wpitch * hc + col0 + 128],
                    gs[:, 512 * hc:512 * (hc + 1)],
                    start=(hc == 0), stop=(hc == 15)),
                    reads=(wkey, kg), writes=(kpsq,))
            p.op("act", lambda s, psq=psq: s.activation(sqs[:, :], psq[:, :],
                                                        AF.Square),
                 reads=(kpsq,), writes=("sqsE",))
            p.op("pe", lambda t: t.matmul(pb[4][0:2, :], bo2[:, :], sqs[:, :],
                                          start=True, stop=True),
                 reads=("sqsE", "bo2"), writes=("pb4",))
            eps6c, eps6k = p.const_col(1e-6)
            p.op("act", lambda s, eps6c=eps6c: s.activation(
                rstdr[:, :], pb[4][0:2, :], AF.Sqrt, bias=eps6c[0:2, :],
                scale=1.0 / 64.0),
                reads=("pb4", eps6k), writes=("rstdA",))
            p.op("dve", lambda v: v.reciprocal(rstdr[:, :], rstdr[:, :]),
                 reads=("rstdA",), writes=("rstdr",))
            p.op("pe", lambda t: t.matmul(pb[4][:, :], bo2t[:, :], rstdr[:, :],
                                          start=True, stop=True),
                 reads=("rstdr", "bo2t"), writes=("pb4",))
            p.op("dve", lambda v, psq=psq: v.tensor_copy(qe[:, :], psq[:, :]),
                 reads=(kpsq,), writes=("qeE",))
            p.op("dve", lambda v, nw2=nw2: v.scalar_tensor_tensor(
                qe[:, :], qe[:, :], nw2[:, :], pb[4][:, :],
                op0=OP.mult, op1=OP.mult),
                reads=("qeE", nwk, "pb4"), writes=("qeE",))
            p.op("pe", lambda t: t.matmul(pb[6][:, :], permt[:, :], qe[:, :],
                                          start=True, stop=True),
                 reads=("permt", "qeE"), writes=("pb6",))
            p.op("dve", lambda v, sp=sp: v.tensor_mul(
                rt1[:, :], qe[:, :], cosb[:, 512 * sp:512 * (sp + 1)]),
                reads=("qeE", "cosb"), writes=("rt1",))
            p.op("dve", lambda v, sp=sp: v.tensor_mul(
                rt2[:, :], pb[6][:, :], sinb[:, 512 * sp:512 * (sp + 1)]),
                reads=("pb6", "sinb"), writes=("rt2",))
            p.op("dve", lambda v, sp=sp, dstb=dstb: v.tensor_add(
                dstb[:, 512 * sp:512 * (sp + 1)], rt1[:, :], rt2[:, :]),
                reads=("rt1", "rt2"), writes=(dkey,))

    for sp2 in range(NSP):
        p.op("pe", lambda t, sp2=sp2: t.matmul(
            pb[0][:, :], perm64[:, :], ksb[:, 512 * sp2:512 * (sp2 + 1)],
            start=True, stop=True),
            reads=("perm64", "ksbE"), writes=("pb0",))
        p.op("dve", lambda v, sp2=sp2: v.tensor_copy(
            kswap[:, 512 * sp2:512 * (sp2 + 1)], pb[0][:, :]),
            reads=("pb0",), writes=("kswapE",))

    R_DS, R_DG, R_DC, R_RC, R_BRD, R_DRW = 0, 512, 1024, 1536, 2048, 2560
    for h in range(NQH):
        pr = h // 2
        qb = 64 * (h % 2)
        kvb = 64 * ((h // 4) % 2)
        kvo = kvb
        ksrc, kskey = (ksb, "ksbE") if qb == kvb else (kswap, "kswapE")
        qkey = f"qsbE{pr}"
        for sp in range(NSP):
            nch = 4 * (sp + 1)
            p.dma("sp", lambda sy, h=h, sp=sp: sy.dma_start(
                brow[:, :], beta_d[h:h + 1, 512 * sp:512 * (sp + 1)]),
                reads=("betad", "browE"), writes=("browE",), tag="ldbrow")
            pend = []
            for ch in range(nch):
                pss = pb[ch % 2]
                kpss = f"pb{ch % 2}"
                ket = f"eTE{ch % 2}"
                p.op("pe", lambda t, ch=ch, sp=sp, pss=pss, qb=qb,
                     pr=pr, ksrc=ksrc: t.matmul(
                    pss[:, :], ksrc[qb:qb + 64, 128 * ch:128 * (ch + 1)],
                    qsb[pr][qb:qb + 64, 512 * sp:512 * (sp + 1)],
                    start=True, stop=True),
                    reads=(kskey, qkey), writes=(kpss,))
                p.op("act", lambda s, ch=ch, pss=pss: s.activation(
                    eT[:, 512 * ch:512 * (ch + 1)], pss[:, :], AF.Exp,
                    bias=negM2b[:, :], scale=0.125),
                    reads=(kpss, "negM2b"), writes=(ket,))
                off = 128 * ch - 512 * sp
                newpend = []
                if off >= 0:
                    k = off // 128
                    ebuf = edg if k % 2 == 0 else edg2
                    kedg = f"edgE{k % 2}"
                    p.op("dve", lambda v, ch=ch, k=k, ebuf=ebuf: v.tensor_mul(
                        ebuf[:, :], eT[:, 512 * ch:512 * (ch + 1)],
                        masks[:, 512 * (4 + k):512 * (5 + k)]),
                        reads=(ket, "mdiag"), writes=(kedg,))
                    def f_diag(k=k, ebuf=ebuf, kedg=kedg):
                        p.op("pe", lambda t, k=k, ebuf=ebuf: t.matmul(
                            pb[3][0:1, :], c["ones_col"][:, :], ebuf[:, :],
                            start=(k == 0), stop=(k == 3)),
                            reads=(kedg, "c_ones"), writes=("pb3",))
                    newpend.append(f_diag)
                    p.op("dve", lambda v, ch=ch, k=k: v.tensor_mul(
                        eT[:, 512 * ch:512 * (ch + 1)],
                        eT[:, 512 * ch:512 * (ch + 1)],
                        masks[:, 512 * k:512 * (k + 1)]),
                        reads=(ket, "mstrict"), writes=(ket,))
                def f_rsum(ch=ch, nch=nch, ket=ket):
                    p.op("pe", lambda t, ch=ch, nch=nch: t.matmul(
                        pb[2][0:1, :], c["ones_col"][:, :],
                        eT[:, 512 * ch:512 * (ch + 1)],
                        start=(ch == 0), stop=(ch == nch - 1)),
                        reads=(ket, "c_ones"), writes=("pb2",))
                newpend.append(f_rsum)
                for f in pend:
                    f()
                pend = newpend
            for f in pend:
                f()
            p.op("dve", lambda v: v.tensor_copy(rows[:, R_DS:R_DS + 512],
                                                pb[2][0:1, :]),
                 reads=("pb2",), writes=("rDS",))
            p.op("dve", lambda v: v.tensor_copy(rows[:, R_DG:R_DG + 512],
                                                pb[3][0:1, :]),
                 reads=("pb3",), writes=("rDG",))
            p.op("dve", lambda v: v.tensor_add(
                rows[:, R_DC:R_DC + 512], rows[:, R_DS:R_DS + 512],
                rows[:, R_DG:R_DG + 512]),
                reads=("rDS", "rDG"), writes=("rDC",))
            p.op("dve", lambda v: v.reciprocal(rows[:, R_RC:R_RC + 512],
                                               rows[:, R_DC:R_DC + 512]),
                 reads=("rDC",), writes=("rRC",))
            p.op("dve", lambda v: v.tensor_scalar(
                rows[:, R_DS:R_DS + 512], rows[:, R_DS:R_DS + 512], 1e-20, None,
                op0=OP.max),
                reads=("rDS",), writes=("rDS",))
            p.op("dve", lambda v: v.reciprocal(rows[:, R_BRD:R_BRD + 512],
                                               rows[:, R_DS:R_DS + 512]),
                 reads=("rDS",), writes=("rBRD",))
            p.op("dve", lambda v: v.tensor_mul(
                rows[:, R_BRD:R_BRD + 512], rows[:, R_BRD:R_BRD + 512],
                brow[:, :]),
                reads=("rBRD", "browE"), writes=("rBRD",))
            p.op("dve", lambda v: v.tensor_scalar_mul(
                rows[:, R_BRD:R_BRD + 512], rows[:, R_BRD:R_BRD + 512], -1.0),
                reads=("rBRD",), writes=("rBRD",))
            p.op("dve", lambda v: v.tensor_mul(
                rows[:, R_DRW:R_DRW + 512], rows[:, R_DG:R_DG + 512],
                rows[:, R_RC:R_RC + 512]),
                reads=("rDG", "rRC"), writes=("rDRW",))
            for iq in range(4):
                I = 4 * sp + iq
                for (roff, dst, dk, bank) in ((R_BRD, brdcol, "brdcol", 5),
                                              (R_RC, rccol, "rccol", 6),
                                              (R_DRW, dcol, "dcol", 0)):
                    p.op("pe", lambda t, roff=roff, iq=iq, bank=bank: t.matmul(
                        pb[bank][:, 0:1],
                        rows[0:1, roff + 128 * iq:roff + 128 * (iq + 1)],
                        c["ones_row_f"][0:1, 0:1], start=True, stop=True),
                        reads=("rBRD", "rRC", "rDRW", "c_ones"),
                        writes=(f"pb{bank}",))
                    p.op("dve", lambda v, dst=dst, bank=bank: v.tensor_copy(
                        dst[:, :], pb[bank][:, 0:1]),
                        reads=(f"pb{bank}",), writes=(dk,))
                for J in range(I):
                    p.op("pe", lambda t, J=J, iq=iq, I=I: t.matmul(
                        pb[4][:, 0:64],
                        eT[:, 512 * J + 128 * iq:512 * J + 128 * (iq + 1)],
                        u_sb[:, 64 * J:64 * (J + 1)],
                        start=(J == 0), stop=False),
                        reads=("eTE0", "eTE1", "u_sbE"), writes=("pb4",))
                vcol0 = 128 * I + kvo
                if I > 0:
                    p.op("dve", lambda v, vcol0=vcol0: v.scalar_tensor_tensor(
                        ucur[:, :], pb[4][:, 0:64], brdcol[:, :],
                        vsb[:, vcol0:vcol0 + 64], op0=OP.mult, op1=OP.add),
                        reads=("pb4", "brdcol", "vsbE"), writes=("ucur",))
                else:
                    p.op("dve", lambda v, vcol0=vcol0: v.tensor_copy(
                        ucur[:, :], vsb[:, vcol0:vcol0 + 64]),
                        reads=("vsbE",), writes=("ucur",))
                p.op("pe", lambda t, iq=iq: t.matmul(
                    pb[5][:, 0:128], c["ones_row_f"][:, :],
                    rows[0:1, R_BRD + 128 * iq:R_BRD + 128 * (iq + 1)],
                    start=True, stop=True),
                    reads=("rBRD", "c_ones"), writes=("pb5",))
                p.op("dve", lambda v, I=I, iq=iq: v.tensor_mul(
                    NTs[:, :], eT[:, 512 * I + 128 * iq:512 * I + 128 * (iq + 1)],
                    pb[5][:, 0:128]),
                    reads=("eTE0", "eTE1", "pb5"), writes=("NTs",))
                p.op("pe", lambda t: t.transpose(pbT[:, 0:128], NTs[:, :],
                                                 ident_e[:, :]),
                     reads=("NTs", "identE"), writes=("pbT",))
                p.op("dve", lambda v: v.tensor_copy(Ns[:, :], pbT[:, 0:128]),
                     reads=("pbT",), writes=("Ns",))
                for lev in range(7):
                    PTl, kPT = (NTs, "NTs") if lev == 0 else (PTj[(lev - 1) % 2],
                                                              f"PTjE{(lev - 1) % 2}")
                    Pl, kP = (Ns, "Ns") if lev == 0 else (Pj[(lev - 1) % 2],
                                                          f"PjE{(lev - 1) % 2}")
                    p.op("pe", lambda t, PTl=PTl: t.matmul(
                        pb[5][:, 0:64], PTl[:, :], ucur[:, :],
                        start=True, stop=True),
                        reads=(kPT, "ucur"), writes=("pb5",))
                    p.op("dve", lambda v: v.tensor_add(ucur[:, :], ucur[:, :],
                                                       pb[5][:, 0:64]),
                         reads=("ucur", "pb5"), writes=("ucur",))
                    if lev < 6:
                        p.op("pe", lambda t, PTl=PTl, Pl=Pl: t.matmul(
                            pb[6][:, 0:128], PTl[:, :], Pl[:, :],
                            start=True, stop=True),
                            reads=(kPT, kP), writes=("pb6",))
                        p.op("dve", lambda v, lev=lev: v.tensor_copy(
                            Pj[lev % 2][:, :], pb[6][:, 0:128]),
                            reads=("pb6",), writes=(f"PjE{lev % 2}",))
                        p.op("pe", lambda t, PTl=PTl, Pl=Pl: t.matmul(
                            pb[0][:, 0:128], Pl[:, :], PTl[:, :],
                            start=True, stop=True),
                            reads=(kPT, kP), writes=("pb0",))
                        p.op("dve", lambda v, lev=lev: v.tensor_copy(
                            PTj[lev % 2][:, :], pb[0][:, 0:128]),
                            reads=("pb0",), writes=(f"PTjE{lev % 2}",))
                p.op("dve", lambda v, I=I: v.tensor_copy(
                    u_sb[:, 64 * I:64 * (I + 1)], ucur[:, :]),
                    reads=("ucur",), writes=("u_sbE",))
                p.op("pe", lambda t, I=I, iq=iq: t.matmul(
                    pb[4][:, 0:64],
                    eT[:, 512 * I + 128 * iq:512 * I + 128 * (iq + 1)],
                    ucur[:, :], start=(I == 0), stop=True),
                    reads=("eTE0", "eTE1", "ucur"), writes=("pb4",))
                p.op("dve", lambda v: v.tensor_scalar_mul(t1s[:, :],
                                                          pb[4][:, 0:64],
                                                          rccol[:, :]),
                     reads=("pb4", "rccol"), writes=("t1sE",))
                p.op("dve", lambda v: v.scalar_tensor_tensor(
                    oIb[:, :], ucur[:, :], dcol[:, :], t1s[:, :],
                    op0=OP.mult, op1=OP.add),
                    reads=("ucur", "dcol", "t1sE"), writes=("oIE",))
                p.op("pe", lambda t: t.transpose(pbT[0:64, 0:128], oIb[:, :],
                                                 ident_e[:, :]),
                     reads=("oIE", "identE"), writes=("pbT",))
                p.op("dve", lambda v, I=I, h=h: v.tensor_copy(
                    oth[64 * (h % 2):64 * (h % 2) + 64, 128 * I:128 * (I + 1)],
                    pbT[0:64, 0:128]),
                    reads=("pbT",), writes=("othE",))
        if h % 2 == 1:
            p.dma("sp", lambda sy, h=h: sy.dma_start(
                oT2_d[128 * (h // 2):128 * (h // 2) + 128, :], oth[:, :]),
                reads=("othE",), writes=("oT2o",), tag="stoT2")
        if debug and h == 0:
            p.dma("sp", lambda sy: sy.dma_start(dbg["q0"][:, :], qsb[0][:, :]),
                  reads=("qsbE0",), writes=("dbg",))
            p.dma("sp", lambda sy: sy.dma_start(dbg["k0"][:, :], ksb[:, :]),
                  reads=("ksbE",), writes=("dbg",))
            p.dma("sp", lambda sy: sy.dma_start(dbg["v0"][:, :], vsb[:, :]),
                  reads=("vsbE",), writes=("dbg",))
            p.dma("sp", lambda sy: sy.dma_start(dbg["beta"][:, :], beta_d[:, :]),
                  reads=("betad",), writes=("dbg",))
            p.dma("sp", lambda sy: sy.dma_start(dbg["eT0"][:, :], eT[:, 0:512]),
                  reads=("eTE0", "eTE1"), writes=("dbg",))
            p.dma("sp", lambda sy: sy.dma_start(dbg["rows"][:, :], rows[:, :]),
                  reads=("rDS", "rDG", "rDC", "rRC", "rBRD", "rDRW"), writes=("dbg",))
            p.dma("sp", lambda sy: sy.dma_start(dbg["u01"][:, :], u_sb[:, 0:128]),
                  reads=("u_sbE",), writes=("dbg",))
            p.dma("sp", lambda sy: sy.dma_start(dbg["cos"][:, :], cosb[:, :]),
                  reads=("cosb",), writes=("dbg",))
            p.dma("sp", lambda sy: sy.dma_start(dbg["sin"][:, :], sinb[:, :]),
                  reads=("sinb",), writes=("dbg",))
            p.dma("sp", lambda sy: sy.dma_start(dbg["perm"][:, :], permt[:, :]),
                  reads=("permt",), writes=("dbg",))
            p.dma("sp", lambda sy: sy.dma_start(dbg["negM"][:, :], negM2b[:, :]),
                  reads=("negM2b",), writes=("dbg",))
    if standalone:
        return p.finalize()


def run_launch_E(G2, ins):
    nc = _get_nc('E', build_launch_E)
    Wq, Wk, Wv, Wb = (ins["L1_attn_Wq"], ins["L1_attn_Wk"], ins["L1_attn_Wv"],
                      ins["L1_attn_Wb"])
    in_maps = []
    for cidx in range(NCORE):
        b, m = cidx // 4, cidx % 4
        in_maps.append({
            "g2": _np(G2[:, 4096 * b:4096 * (b + 1)]),
            "wq": _np(Wq[:, 512 * m:512 * (m + 1)]),
            "wk": _np(Wk[:, 128 * m:128 * (m + 1)]),
            "wv": _np(Wv[:, 128 * m:128 * (m + 1)]),
            "wb": _np(Wb[:, 8 * m:8 * (m + 1)]),
            "bb": _np(ins["L1_attn_bb"][8 * m:8 * (m + 1)].reshape(8, 1)),
            "qnw": _np(ins["L1_attn_q_norm_w"].reshape(64, 1)),
            "knw": _np(ins["L1_attn_k_norm_w"].reshape(64, 1)),
        })
    res = run_bass_kernel_spmd(nc, in_maps, core_ids=list(range(NCORE)))
    GO2 = np.zeros((B, DMODEL, S), dtype=ml_dtypes.bfloat16)
    for cidx in range(NCORE):
        b, m = cidx // 4, cidx % 4
        GO2[b, 512 * m:512 * (m + 1), :] = res.results[cidx]["oT2"]
    return GO2



def build_launch_F(p=None, io=None):
    standalone = p is None
    if standalone:
        p = Prog("F")
        x2sl_d = p.dram("x2sl", [TOK_PER_CORE, DMODEL], F32, kind="ExternalInput")
        go2sl_d = p.dram("go2sl", [DMODEL, TOK_PER_CORE], BF16, kind="ExternalInput")
        wo_d = p.dram("wo", [DMODEL, DMODEL], F32, kind="ExternalInput")
        rw_d = p.dram("rw", [DMODEL, 8], F32, kind="ExternalInput")
        shg_d = p.dram("shg", [DMODEL, 8192], F32, kind="ExternalInput")
        shu_d = p.dram("shu", [DMODEL, 8192], F32, kind="ExternalInput")
        shd_d = p.dram("shd", [8192, DMODEL], F32, kind="ExternalInput")
        rg_d = p.dram("rg", [DMODEL, 1024], F32, kind="ExternalInput")
        ru_d = p.dram("ru", [DMODEL, 1024], F32, kind="ExternalInput")
        rd_d = p.dram("rd", [1024, DMODEL], F32, kind="ExternalInput")
        out_d = p.dram("out", [TOK_PER_CORE, DMODEL], F32, kind="ExternalOutput")
        x3_d = p.dram("x3d", [TOK_PER_CORE, DMODEL], F32)
        ln3_d = p.dram("ln3d", [DMODEL, TOK_PER_CORE], BF16)
        c = emit_consts(p)
    else:
        x2sl_d, go2sl_d = io["x2"], io["go2"]
        wo_d, rw_d = io["wo"], io["rw"]
        shg_d, shu_d, shd_d = io["shg"], io["shu"], io["shd"]
        rg_d, ru_d, rd_d = io["rg"], io["ru"], io["rd"]
        out_d, x3_d, ln3_d = io["out"], io["x3d"], io["ln3d"]
        c = io["consts"]
    p.const_col(1e-5)
    NT = TOK_PER_CORE // 128
    _pidcF = {}

    xt = [p.sbuf(f"xtF{i}", [128, DMODEL], F32, scoped=False) for i in range(2)]
    rwF = p.sbuf("rwF", [128, 16 * 8], BF16, scoped=False)
    cbF = p.sbuf("cbF", [128, 8], F32, scoped=False)
    p.push_scope()
    go2 = [p.sbuf(f"go2F{i}", [128, 16 * 128], BF16) for i in range(2)]
    woF = [p.sbuf(f"woF{i}", [128, 16 * 512], BF16) for i in range(2)]
    psA = p.psum("psAF", [128, 512])
    psB2 = p.psum("psBF", [128, 512])

    def f_rw(g):
        srcap = bass.AP(rw_d, 0, [[8, 128], [128 * 8, 16], [1, 8]])
        dst = bass.AP(rwF, 0, [[16 * 8, 128], [8, 16], [1, 8]])
        return g.dma_start(dst, srcap)
    p.dma("pool", f_rw, writes=("rwF",), tag="ldrw")

    for tt in range(NT):
        xb = xt[tt % 2]
        kx = f"xtF{tt % 2}"
        p.dma("sp", lambda sy, tt=tt, xb=xb: sy.dma_start(
            xb[:, :], x2sl_d[128 * tt:128 * (tt + 1), :]),
            reads=(kx,), writes=(kx,), tag="ldx2")
        gb = go2[tt % 2]
        kgb = f"go2F{tt % 2}"
        if standalone:
            def f_go(sy, tt=tt, gb=gb):
                srcap = bass.AP(go2sl_d, 128 * tt,
                                [[TOK_PER_CORE, 128], [128 * TOK_PER_CORE, 16], [1, 128]])
                dst = bass.AP(gb, 0, [[16 * 128, 128], [128, 16], [1, 128]])
                return sy.dma_start(dst, srcap)
        else:
            def f_go(sy, tt=tt, gb=gb):
                if "c" not in _pidcF:
                    _pidcF["c"] = (sy.partition_id() % 4) * 1024
                srcap = bass.AP(go2sl_d, _pidcF["c"] + 128 * tt,
                                [[S, 128], [128 * S, 16], [1, 128]])
                dst = bass.AP(gb, 0, [[16 * 128, 128], [128, 16], [1, 128]])
                return sy.dma_start(dst, srcap)
        p.dma("sp", f_go, reads=(kgb,), writes=(kgb,), tag="ldgo2")
        for fb in range(4):
            wb = woF[fb % 2]
            kwb = f"woF{fb % 2}"
            if tt == 0 or True:
                def f_wo(g, fb=fb, wb=wb):
                    srcap = bass.AP(wo_d, 512 * fb,
                                    [[DMODEL, 128], [128 * DMODEL, 16], [1, 512]])
                    dst = bass.AP(wb, 0, [[16 * 512, 128], [512, 16], [1, 512]])
                    return g.dma_start(dst, srcap)
                p.dma("pool", f_wo, reads=(kwb,), writes=(kwb,), tag="ldwoF")
            for hc in range(16):
                p.op("pe", lambda t, hc=hc, gb=gb, wb=wb: t.matmul(
                    psA[:, :], gb[:, 128 * hc:128 * (hc + 1)],
                    wb[:, 512 * hc:512 * (hc + 1)],
                    start=(hc == 0), stop=(hc == 15)),
                    reads=(kgb, kwb), writes=("psAF",))
            p.op("dve", lambda v, fb=fb, xb=xb: v.tensor_add(
                xb[:, 512 * fb:512 * (fb + 1)],
                xb[:, 512 * fb:512 * (fb + 1)], psA[:, :]),
                reads=(kx, "psAF"), writes=(kx,))
        p.dma("sp", lambda sy, tt=tt, xb=xb: sy.dma_start(
            x3_d[128 * tt:128 * (tt + 1), :], xb[:, :]),
            reads=(kx,), writes=("x3d",), tag="stx3")

    lgF = p.sbuf("lgF", [128, 8], F32)
    m1 = p.sbuf("m1F", [128, 1], F32)
    nm1 = p.sbuf("nm1F", [128, 1], F32)
    mk1 = p.sbuf("mk1F", [128, 8], F32)
    w2 = p.sbuf("w2F", [128, 8], F32)
    m2 = p.sbuf("m2F", [128, 1], F32)
    e1 = p.sbuf("e1F", [128, 1], F32)
    dn1 = p.sbuf("dn1F", [128, 1], F32)
    rc1 = p.sbuf("rc1F", [128, 1], F32)

    def routerF(tt, st, kst):
        for hc in range(16):
            p.op("pe", lambda t, hc=hc, st=st: t.matmul(
                psB2[:, 0:8], st[:, 128 * hc:128 * (hc + 1)],
                rwF[:, 8 * hc:8 * (hc + 1)],
                start=(hc == 0), stop=(hc == 15)),
                reads=(kst, "rwF"), writes=("psBF",))
        p.op("dve", lambda v: v.tensor_copy(lgF[:, :], psB2[:, 0:8]),
             reads=("psBF",), writes=("lgF",))
        p.op("dve", lambda v: v.reduce_max(m1[:, :], lgF[:, :], axis=AX.X),
             reads=("lgF",), writes=("m1F",))
        p.op("dve", lambda v: v.tensor_scalar_mul(nm1[:, :], m1[:, :], -1.0),
             reads=("m1F",), writes=("nm1F",))
        p.op("dve", lambda v: v.tensor_single_scalar(mk1[:, :], lgF[:, :],
                                                     m1[:, :], OP.is_ge),
             reads=("lgF", "m1F"), writes=("mk1F",))
        p.op("dve", lambda v: v.scalar_tensor_tensor(
            w2[:, :], mk1[:, :], -1e30, lgF[:, :], op0=OP.mult, op1=OP.add),
            reads=("mk1F", "lgF"), writes=("w2F",))
        p.op("dve", lambda v: v.reduce_max(m2[:, :], w2[:, :], axis=AX.X),
             reads=("w2F",), writes=("m2F",))
        p.op("act", lambda s: s.activation(e1[:, :], m2[:, :], AF.Exp,
                                           bias=nm1[:, :]),
             reads=("m2F", "nm1F"), writes=("e1F",))
        p.op("dve", lambda v: v.tensor_scalar_add(dn1[:, :], e1[:, :], 1.0),
             reads=("e1F",), writes=("dn1F",))
        p.op("dve", lambda v: v.reciprocal(rc1[:, :], dn1[:, :]),
             reads=("dn1F",), writes=("rc1F",))
        p.op("dve", lambda v, tt=tt: v.tensor_mul(cbF[:, tt:tt + 1], dn1[:, :],
                                                  rc1[:, :]),
             reads=("dn1F", "rc1F"), writes=("cbF",))

    def get_x3_tile(tt):
        xb = xt[tt % 2]
        kx = f"xtF{tt % 2}"
        p.dma("sp", lambda sy, tt=tt, xb=xb: sy.dma_start(
            xb[:, :], x3_d[128 * tt:128 * (tt + 1), :]),
            reads=(kx, "x3d"), writes=(kx,), tag="ldx3")
        return xb, kx

    emit_ln_from_sbuf(p, c, get_x3_tile, NT, ln3_d, 0, "ln3",
                      post_tile=routerF)
    p.pop_scope()

    p.push_scope()
    lnF = p.sbuf("lnF", [128, 16 * 512], BF16)
    wbg = [p.sbuf(f"wbgF{i}", [128, 16 * 512], BF16) for i in range(2)]
    wbu = [p.sbuf(f"wbuF{i}", [128, 16 * 512], BF16) for i in range(2)]
    wbd = [p.sbuf(f"wbdF{i}", [128, 16 * 512], BF16) for i in range(2)]
    actF = p.sbuf("actF", [128, 16 * 512], BF16)
    sgFs = [p.sbuf(f"sgFF{i}", [128, 512], BF16) for i in range(2)]
    acc = [p.sbuf(f"accF{i}", [128, DMODEL], F32) for i in range(4)]
    psG1 = p.psum("psG1F", [128, 512])
    psG2 = p.psum("psG2F", [128, 512])
    psU1 = p.psum("psU1F", [128, 512])
    psU2 = p.psum("psU2F", [128, 512])
    psD = [p.psum(f"psDF{i}", [128, 512]) for i in range(4)]

    def gate_quarter(sp, row0, n_ic, wsrc_g, wsrc_u, wpitch, klb):
        ng = (n_ic + 3) // 4
        for g2_ in range(ng):
            bg, bu = wbg[g2_ % 2], wbu[g2_ % 2]
            kbg, kbu = f"wbgF{g2_ % 2}", f"wbuF{g2_ % 2}"
            nthis = min(4, n_ic - 4 * g2_)
            for (wsrc, dstb, kdst) in ((wsrc_g, bg, kbg), (wsrc_u, bu, kbu)):
                def f_w(g, wsrc=wsrc, dstb=dstb, g2_=g2_, nthis=nthis,
                        row0=row0, wpitch=wpitch):
                    srcap = bass.AP(wsrc, row0 + 512 * g2_,
                                    [[wpitch, 128], [128 * wpitch, 16],
                                     [1, 128 * nthis]])
                    dst = bass.AP(dstb, 0, [[16 * 512, 128], [512, 16],
                                            [1, 128 * nthis]])
                    return g.dma_start(dst, srcap)
                p.dma("pool", f_w, reads=(kdst,), writes=(kdst,), tag="ldgw4")
            for li in range(nthis):
                il = 4 * g2_ + li
                pg = psG1 if il % 2 == 0 else psG2
                pu = psU1 if il % 2 == 0 else psU2
                kpg = "psG1F" if il % 2 == 0 else "psG2F"
                kpu = "psU1F" if il % 2 == 0 else "psU2F"
                for hc in range(16):
                    p.op("pe", lambda t, hc=hc, li=li, bg=bg, pg=pg: t.matmul(
                        pg[:, :], bg[:, 512 * hc + 128 * li:512 * hc + 128 * (li + 1)],
                        lnF[:, 512 * hc:512 * (hc + 1)],
                        start=(hc == 0), stop=(hc == 15)),
                        reads=(kbg, klb), writes=(kpg,))
                sgF = sgFs[il % 2]
                ksgF = f"sgFF{il % 2}"
                p.op("act", lambda s, pg=pg, sgF=sgF: s.activation(
                    sgF[:, :], pg[:, :], AF.Silu),
                     reads=(kpg,), writes=(ksgF,))
                for hc in range(16):
                    p.op("pe", lambda t, hc=hc, li=li, bu=bu, pu=pu: t.matmul(
                        pu[:, :], bu[:, 512 * hc + 128 * li:512 * hc + 128 * (li + 1)],
                        lnF[:, 512 * hc:512 * (hc + 1)],
                        start=(hc == 0), stop=(hc == 15)),
                        reads=(kbu, klb), writes=(kpu,))
                p.op("dve", lambda v, il=il, pu=pu, sgF=sgF: v.tensor_mul(
                    actF[:, 512 * il:512 * (il + 1)], sgF[:, :], pu[:, :]),
                    reads=(ksgF, kpu), writes=("actF",))

    def down_quarter(sp, row0, n_ic, wsrc_d, first, scale_cb=False):
        for fb in range(4):
            db = wbd[fb % 2]
            kdb = f"wbdF{fb % 2}"
            def f_d(g, fb=fb, db=db, row0=row0, n_ic=n_ic, wsrc_d=wsrc_d):
                srcap = bass.AP(wsrc_d, row0 * DMODEL + 512 * fb,
                                [[DMODEL, 128], [128 * DMODEL, n_ic], [1, 512]])
                dst = bass.AP(db, 0, [[16 * 512, 128], [512, n_ic], [1, 512]])
                return g.dma_start(dst, srcap)
            p.dma("pool", f_d, reads=(kdb,), writes=(kdb,), tag="lddw4")
            for t128 in range(4):
                ps = psD[t128]
                kps = f"psDF{t128}"
                for il in range(n_ic):
                    p.op("pe", lambda t, il=il, t128=t128, db=db, ps=ps: t.matmul(
                        ps[:, :], actF[:, 512 * il + 128 * t128:
                                       512 * il + 128 * (t128 + 1)],
                        db[:, 512 * il:512 * (il + 1)],
                        start=(il == 0), stop=(il == n_ic - 1)),
                        reads=("actF", kdb), writes=(kps,))
                tt = 4 * sp + t128
                if scale_cb:
                    p.op("dve", lambda v, fb=fb, t128=t128, tt=tt, ps=ps, kps=kps:
                         v.scalar_tensor_tensor(
                        acc[t128][:, 512 * fb:512 * (fb + 1)], ps[:, :],
                        cbF[:, tt:tt + 1], acc[t128][:, 512 * fb:512 * (fb + 1)],
                        op0=OP.mult, op1=OP.add),
                        reads=(kps, "cbF", f"accF{t128}"), writes=(f"accF{t128}",))
                elif first:
                    p.op("dve", lambda v, fb=fb, t128=t128, ps=ps, kps=kps:
                         v.tensor_copy(
                        acc[t128][:, 512 * fb:512 * (fb + 1)], ps[:, :]),
                        reads=(kps,), writes=(f"accF{t128}",))
                else:
                    p.op("dve", lambda v, fb=fb, t128=t128, ps=ps, kps=kps:
                         v.tensor_add(
                        acc[t128][:, 512 * fb:512 * (fb + 1)],
                        acc[t128][:, 512 * fb:512 * (fb + 1)], ps[:, :]),
                        reads=(f"accF{t128}", kps), writes=(f"accF{t128}",))

    for sp in range(2):
        def f_ln(sy, sp=sp):
            srcap = bass.AP(ln3_d, 512 * sp,
                            [[TOK_PER_CORE, 128], [128 * TOK_PER_CORE, 16], [1, 512]])
            dst = bass.AP(lnF, 0, [[16 * 512, 128], [512, 16], [1, 512]])
            return sy.dma_start(dst, srcap)
        p.dma("sp", f_ln, reads=("lnF",), writes=("lnF",), tag="ldln3")
        for qq in range(4):
            gate_quarter(sp, 2048 * qq, 16, shg_d, shu_d, 8192, "lnF")
            down_quarter(sp, 2048 * qq, 16, shd_d, first=(qq == 0))
        gate_quarter(sp, 0, 8, rg_d, ru_d, 1024, "lnF")
        down_quarter(sp, 0, 8, rd_d, first=False, scale_cb=True)
        for t128 in range(4):
            tt = 4 * sp + t128
            xb, kx = get_x3_tile(tt)
            p.op("dve", lambda v, t128=t128, xb=xb: v.tensor_add(
                xb[:, :], xb[:, :], acc[t128][:, :]),
                reads=(kx, f"accF{t128}"), writes=(kx,))
            p.dma("sp", lambda sy, tt=tt, xb=xb: sy.dma_start(
                out_d[128 * tt:128 * (tt + 1), :], xb[:, :]),
                reads=(kx,), writes=("outF",), tag="stout")
    p.pop_scope()
    if standalone:
        return p.finalize()


def run_launch_F(x2, GO2, ins):
    nc = _get_nc('F', build_launch_F)
    woN = _np(ins["L1_attn_Wo"])
    rwT = _np(ins["L1_ffn_router_weight"].T)
    shgT = _np(ins["L1_ffn_sh_gate"].T)
    shuT = _np(ins["L1_ffn_sh_up"].T)
    shdT = _np(ins["L1_ffn_sh_down"].T)
    rgT = _np(ins["L1_ffn_r_gate"].T)
    ruT = _np(ins["L1_ffn_r_up"].T)
    rdT = _np(ins["L1_ffn_r_down"].T)
    in_maps = []
    for cidx in range(NCORE):
        b = cidx // 4
        t0 = 1024 * cidx
        tl0 = t0 - 4096 * b
        in_maps.append({
            "x2sl": _np(x2[t0:t0 + 1024]),
            "go2sl": _np(GO2[b][:, tl0:tl0 + 1024]),
            "wo": woN, "rw": rwT, "shg": shgT, "shu": shuT, "shd": shdT,
            "rg": rgT, "ru": ruT, "rd": rdT,
        })
    res = run_bass_kernel_spmd(nc, in_maps, core_ids=list(range(NCORE)))
    return np.concatenate([res.results[i]["out"] for i in range(NCORE)], axis=0)



def _np(a):
    return np.ascontiguousarray(a)


def run_launch_A(x, w_qkv):
    wT = _np(w_qkv.T)
    nc = _get_nc('A', build_launch_A)
    in_maps = []
    for cidx in range(NCORE):
        b, m = cidx // 4, cidx % 4
        cols = np.concatenate([
            wT[:, 512 * m:512 * (m + 1)],
            wT[:, 2048 + 512 * m:2048 + 512 * (m + 1)],
            wT[:, 4096 + 512 * m:4096 + 512 * (m + 1)]], axis=1)
        in_maps.append({"xb": _np(x[b]), "wsl": _np(cols)})
    res = run_bass_kernel_spmd(nc, in_maps, core_ids=list(range(NCORE)))
    GO = np.zeros((B, DMODEL, S), dtype=ml_dtypes.bfloat16)
    for cidx in range(NCORE):
        b, m = cidx // 4, cidx % 4
        GO[b, 512 * m:512 * (m + 1), :] = res.results[cidx]["oTn"]
    return GO


def run_launch_B(x_flat, GO, w_o, gate_w):
    woT = _np(w_o.T)
    gwT = _np(gate_w.T)
    nc = _get_nc('B', build_launch_B)
    in_maps = []
    for cidx in range(NCORE):
        b = cidx // 4
        t0 = 1024 * cidx
        tl0 = t0 - 4096 * b
        in_maps.append({
            "xsl": _np(x_flat[t0:t0 + 1024]),
            "gosl": _np(GO[b][:, tl0:tl0 + 1024]),
            "woT": woT, "gwT": gwT,
        })
    res = run_bass_kernel_spmd(nc, in_maps, core_ids=list(range(NCORE)))
    x1 = np.concatenate([res.results[i]["x1"] for i in range(NCORE)], axis=0)
    G1 = np.concatenate([res.results[i]["ln1T"] for i in range(NCORE)], axis=1)
    comb = np.concatenate([res.results[i]["comb"] for i in range(NCORE)], axis=0)
    return x1, G1, comb


def run_launch_C(G1, ins, comb):
    nc = _get_nc('C', build_launch_C)
    in_maps = []
    for cidx in range(NCORE):
        e0 = 2 * cidx
        in_maps.append({
            "g1": G1,
            "pkgu": _np(ins["L0_ffn_gate_up_packed"][e0:e0 + 2]),
            "scgu": _np(ins["L0_ffn_gate_up_scales"][e0:e0 + 2]),
            "pkdn": _np(ins["L0_ffn_down_packed"][e0:e0 + 2]),
            "scdn": _np(ins["L0_ffn_down_scales"][e0:e0 + 2]),
            "combsl": _np(comb[:, e0:e0 + 2]),
            "pksguc": _np(ins["L0_ffn_shared_gate_up_packed"][:, 384 * cidx:384 * (cidx + 1)]),
            "scsguc": _np(ins["L0_ffn_shared_gate_up_scales"][:, 384 * cidx:384 * (cidx + 1)]),
            "pksdnc": _np(ins["L0_ffn_shared_down_packed"][:, 256 * cidx:256 * (cidx + 1)]),
            "scsdnc": _np(ins["L0_ffn_shared_down_scales"][:, 256 * cidx:256 * (cidx + 1)]),
        })
    res = run_bass_kernel_spmd(nc, in_maps, core_ids=list(range(NCORE)))
    parts = [res.results[i]["partial"] for i in range(NCORE)]
    SGU = np.concatenate([res.results[i]["sgslo"] for i in range(NCORE)], axis=1)
    SDN = np.concatenate([res.results[i]["sdno"] for i in range(NCORE)], axis=1)
    return parts, SGU, SDN



def build_fused():
    p = Prog("FUSED")
    xb = p.dram("i_xb", [S, DMODEL], F32, kind="ExternalInput")
    xsl = p.dram("i_xsl", [TOK_PER_CORE, DMODEL], F32, kind="ExternalInput")
    wsl = p.dram("i_wsl", [DMODEL, 1536], F32, kind="ExternalInput")
    woT = p.dram("i_woT", [DMODEL, DMODEL], F32, kind="ExternalInput")
    gwTp = p.dram("i_gwTp", [DMODEL, 16], F32, kind="ExternalInput")
    pkgu = p.dram("i_pkgu", [2, 256, 3072], I32, kind="ExternalInput")
    scgu = p.dram("i_scgu", [2, 16, 3072], F32, kind="ExternalInput")
    pkdn = p.dram("i_pkdn", [2, 192, 2048], I32, kind="ExternalInput")
    scdn = p.dram("i_scdn", [2, 12, 2048], F32, kind="ExternalInput")
    pksgu = p.dram("i_pksgu", [256, 3072], I32, kind="ExternalInput")
    scsgu = p.dram("i_scsgu", [16, 3072], F32, kind="ExternalInput")
    pksdn = p.dram("i_pksdn", [192, 2048], I32, kind="ExternalInput")
    scsdn = p.dram("i_scsdn", [12, 2048], F32, kind="ExternalInput")
    wq = p.dram("i_wq", [DMODEL, 512], F32, kind="ExternalInput")
    wk = p.dram("i_wk", [DMODEL, 128], F32, kind="ExternalInput")
    wv = p.dram("i_wv", [DMODEL, 128], F32, kind="ExternalInput")
    wb = p.dram("i_wb", [DMODEL, 8], F32, kind="ExternalInput")
    bb = p.dram("i_bb", [8, 1], F32, kind="ExternalInput")
    qnw = p.dram("i_qnw", [64, 1], F32, kind="ExternalInput")
    knw = p.dram("i_knw", [64, 1], F32, kind="ExternalInput")
    wo = p.dram("i_wo", [DMODEL, DMODEL], F32, kind="ExternalInput")
    rw = p.dram("i_rw", [DMODEL, 8], F32, kind="ExternalInput")
    shg = p.dram("i_shg", [DMODEL, 8192], F32, kind="ExternalInput")
    shu = p.dram("i_shu", [DMODEL, 8192], F32, kind="ExternalInput")
    shd = p.dram("i_shd", [8192, DMODEL], F32, kind="ExternalInput")
    rg = p.dram("i_rg", [DMODEL, 1024], F32, kind="ExternalInput")
    ru = p.dram("i_ru", [DMODEL, 1024], F32, kind="ExternalInput")
    rd = p.dram("i_rd", [1024, DMODEL], F32, kind="ExternalInput")
    out = p.dram("i_out", [TOK_PER_CORE, DMODEL], F32, kind="ExternalOutput")
    lnT = p.dram("lnT", [DMODEL, S], BF16)
    oTn_c = p.dram("oTn_c", [512, S], BF16)
    go_gb = p.dram("go_gb", [2048, S], BF16)
    x1 = p.dram("x1i", [TOK_PER_CORE, DMODEL], F32)
    ln1T = p.dram("ln1Ti", [DMODEL, TOK_PER_CORE], BF16)
    g1_g = p.dram("g1_g", [NCORE * DMODEL, TOK_PER_CORE], BF16)
    prt_c = p.dram("prt_c", [NTOK, DMODEL], BF16)
    rs_out = p.dram("rs_out", [TOK_PER_CORE, DMODEL], BF16)
    wgu2 = p.dram("wgu2i", [2048, 1536], BF16)
    sgu_loc = p.dram("sgu_loc", [DMODEL, 3072], BF16)
    sdn_loc = p.dram("sdn_loc", [1536, DMODEL], BF16)
    x2 = p.dram("x2i", [TOK_PER_CORE, DMODEL], F32)
    ln2T = p.dram("ln2Ti", [DMODEL, TOK_PER_CORE], BF16)
    g2_gb = p.dram("g2_gb", [4 * DMODEL, TOK_PER_CORE], BF16)
    oT2_c = p.dram("oT2_c", [512, S], BF16)
    go2_gb = p.dram("go2_gb", [2048, S], BF16)
    betad = p.dram("betadi", [8, S], F32)
    x3d = p.dram("x3di", [TOK_PER_CORE, DMODEL], F32)
    ln3d = p.dram("ln3di", [DMODEL, TOK_PER_CORE], BF16)

    c = emit_consts(p)
    for v in (1e-5, -2.0, -6.0, 1e-6, float(np.pi / 2.0)):
        p.const_col(v)

    def coll(kind, op_, groups, in_h, out_h, reads, writes, tag):
        def f(g):
            return g.collective_compute(kind, op_, replica_groups=groups,
                                        ins=[in_h.ap().opt()],
                                        outs=[out_h.ap().opt()])
        p.op("pool", f, reads=reads, writes=writes, tag=tag)
        p.fence()

    GRP4 = [[0, 1, 2, 3], [4, 5, 6, 7]]
    GRP8 = [list(range(8))]

    p.push_scope()
    build_launch_A(p=p, io=dict(xb=xb, wsl=wsl, oTn=oTn_c, lnT=lnT, consts=c))
    p.pop_scope()
    coll("AllGather", OP.bypass, GRP4, oTn_c, go_gb,
         ("oTn_out0", "oTn_out1", "oTn_out2", "oTn_out3"), ("go_gb",), "cGO")

    p.push_scope()
    build_launch_B(p=p, io=dict(xsl=xsl, gosl=go_gb, woT=woT, x1=x1,
                                ln1T=ln1T, consts=c))
    p.pop_scope()
    coll("AllGather", OP.bypass, GRP8, ln1T, g1_g, ("ln1_all",), ("g1_g",), "cG1")

    p.push_scope()
    build_launch_C(p=p, io=dict(g1=g1_g, pkgu=pkgu, scgu=scgu, pkdn=pkdn,
                                scdn=scdn, pksgu=pksgu, scsgu=scsgu,
                                pksdn=pksdn, scsdn=scsdn, gwTp=gwTp,
                                partial=prt_c, sgu_loc=sgu_loc,
                                sdn_loc=sdn_loc, wgu2=wgu2, consts=c))
    p.pop_scope()
    coll("ReduceScatter", OP.add, GRP8, prt_c, rs_out, ("partial",),
         ("rs_out",), "cRS")

    p.push_scope()
    build_launch_D(p=p, io=dict(x1=x1, ln1T=ln1T, rs_out=rs_out,
                                sgu_loc=sgu_loc, sdn_loc=sdn_loc, x2=x2,
                                ln2T=ln2T, consts=c))
    p.pop_scope()
    coll("AllGather", OP.bypass, GRP4, ln2T, g2_gb, ("ln2_all",), ("g2_gb",), "cG2")

    p.push_scope()
    build_launch_E(p=p, io=dict(g2=g2_gb, wq=wq, wk=wk, wv=wv, wb=wb, bb=bb,
                                qnw=qnw, knw=knw, oT2=oT2_c, betad=betad))
    p.pop_scope()
    coll("AllGather", OP.bypass, GRP4, oT2_c, go2_gb, ("oT2o",), ("go2_gb",), "cGO2")

    p.push_scope()
    build_launch_F(p=p, io=dict(x2=x2, go2=go2_gb, wo=wo, rw=rw, shg=shg,
                                shu=shu, shd=shd, rg=rg, ru=ru, rd=rd, out=out,
                                x3d=x3d, ln3d=ln3d, consts=c))
    p.pop_scope()
    return p.finalize()


def run_fused(ins):
    nc = _get_nc("FUSED", build_fused)
    x = ins["L0_attn_hidden_states"].reshape(NTOK, DMODEL)
    wT = _np(ins["L0_attn_w_qkv"].T)
    woT = _np(ins["L0_attn_w_o"].T)
    gwT = ins["L0_ffn_gate_w"].T
    woN = _np(ins["L1_attn_Wo"])
    rwT = _np(ins["L1_ffn_router_weight"].T)
    shgT = _np(ins["L1_ffn_sh_gate"].T)
    shuT = _np(ins["L1_ffn_sh_up"].T)
    shdT = _np(ins["L1_ffn_sh_down"].T)
    rgT = _np(ins["L1_ffn_r_gate"].T)
    ruT = _np(ins["L1_ffn_r_up"].T)
    rdT = _np(ins["L1_ffn_r_down"].T)
    Wq, Wk, Wv, Wb = (ins["L1_attn_Wq"], ins["L1_attn_Wk"], ins["L1_attn_Wv"],
                      ins["L1_attn_Wb"])
    in_maps = []
    for cidx in range(NCORE):
        b, m = cidx // 4, cidx % 4
        t0 = 1024 * cidx
        e0 = 2 * cidx
        cols = np.concatenate([
            wT[:, 512 * m:512 * (m + 1)],
            wT[:, 2048 + 512 * m:2048 + 512 * (m + 1)],
            wT[:, 4096 + 512 * m:4096 + 512 * (m + 1)]], axis=1)
        eperm = [e0, e0 + 1] + [e for e in range(16) if e not in (e0, e0 + 1)]
        in_maps.append({
            "i_xb": _np(x[4096 * b:4096 * (b + 1)]),
            "i_xsl": _np(x[t0:t0 + 1024]),
            "i_wsl": _np(cols),
            "i_woT": woT,
            "i_gwTp": _np(gwT[:, eperm]),
            "i_pkgu": _np(ins["L0_ffn_gate_up_packed"][e0:e0 + 2]),
            "i_scgu": _np(ins["L0_ffn_gate_up_scales"][e0:e0 + 2]),
            "i_pkdn": _np(ins["L0_ffn_down_packed"][e0:e0 + 2]),
            "i_scdn": _np(ins["L0_ffn_down_scales"][e0:e0 + 2]),
            "i_pksgu": _np(ins["L0_ffn_shared_gate_up_packed"]),
            "i_scsgu": _np(ins["L0_ffn_shared_gate_up_scales"]),
            "i_pksdn": _np(ins["L0_ffn_shared_down_packed"]),
            "i_scsdn": _np(ins["L0_ffn_shared_down_scales"]),
            "i_wq": _np(Wq[:, 512 * m:512 * (m + 1)]),
            "i_wk": _np(Wk[:, 128 * m:128 * (m + 1)]),
            "i_wv": _np(Wv[:, 128 * m:128 * (m + 1)]),
            "i_wb": _np(Wb[:, 8 * m:8 * (m + 1)]),
            "i_bb": _np(ins["L1_attn_bb"][8 * m:8 * (m + 1)].reshape(8, 1)),
            "i_qnw": _np(ins["L1_attn_q_norm_w"].reshape(64, 1)),
            "i_knw": _np(ins["L1_attn_k_norm_w"].reshape(64, 1)),
            "i_wo": woN, "i_rw": rwT, "i_shg": shgT, "i_shu": shuT, "i_shd": shdT,
            "i_rg": rgT, "i_ru": ruT, "i_rd": rdT,
        })
    res = run_bass_kernel_spmd(nc, in_maps, core_ids=list(range(NCORE)))
    return np.concatenate([res.results[i]["i_out"] for i in range(NCORE)], axis=0)


_NC_CACHE = {}


def _get_nc(name, builder):
    if name not in _NC_CACHE:
        _NC_CACHE[name] = builder()
    return _NC_CACHE[name]


def kernel(**ins):
    out = run_fused(ins)
    return out.reshape(B, S, DMODEL).astype(np.float32)
